# version 16
# baseline (speedup 1.0000x reference)
import numpy as np
import ml_dtypes
import concourse.bass as bass
import concourse.mybir as mybir
from concourse.bass_utils import run_bass_kernel_spmd

F32 = mybir.dt.float32
BF16 = mybir.dt.bfloat16
ALU = mybir.AluOpType
ACTF = mybir.ActivationFunctionType
AX = mybir.AxisListType

NCORE = 8
L, D, ED, N, R, KC = 16384, 1024, 2048, 16, 64, 4
DEPTH = 6
LC = L // NCORE
ET, DT = ED // 128, D // 128
SR = ET + NCORE - 1
EPS = 1e-5
IF = 2816
ITF = IF // 128
LT = LC // 128
HW = 64
BQ = 16
RG8 = [[0, 1, 2, 3, 4, 5, 6, 7]]
STOP_AFTER = None
DUMP_U = False
DUMP_SRC = "u"


class Views:
    pass


def mkviews(T):
    V = Views()
    af = T["t_af"]
    ab = T["t_ab"]

    def f(a, b):
        return af[:, a:b]

    V.xs = f(0, 8320)
    V.u = f(8320, 12416)
    V.xc1 = f(12416, 12928)
    V.xc2 = f(12928, 13440)
    V.gst = f(13440, 13952)
    V.tail = f(13952, 14976)
    V.un = f(14976, 17024).bitcast(BF16)
    V.unlo = f(17024, 19072).bitcast(BF16)
    V.unf = f(19072, 21120)
    V.xct = f(17024, 19072)
    V.gt = f(19072, 21120)
    V.dl = f(0, 2048)
    V.w = f(2048, 4096)
    V.m = f(4096, 6144)
    V.tmp2 = f(6144, 8192)
    V.da = f(8192, 10240)
    V.dbx = f(10240, 12288)
    V.h = f(12288, 14336)
    V.yacc = f(14336, 16384)
    V.tmp1 = f(21120, 23168)
    V.row = f(23168, 24192)
    V.v1 = f(16384, 16896)
    V.v2 = f(0, 512)
    V.v3 = f(512, 1024)
    V.halg = f(0, 4096)
    V.halt = f(4096, 8192)
    V.halxs = f(21120, 22144)
    V.wo = f(0, 8192).bitcast(BF16)
    V.wo2 = f(8192, 16384).bitcast(BF16)
    V.uw = f(17408, 17920)
    V.uo = f(17920, 18432)
    V.ylos = f(16384, 17408).bitcast(BF16)
    V.ygf = f(21120, 23168)
    V.ylo_st = f(5120, 6144).bitcast(BF16)
    V.q0 = f(0, 2816)
    V.q1 = f(2816, 5632)
    V.q2 = f(5632, 8448)
    V.q3 = f(8448, 11264)
    V.h2 = f(11264, 14080)
    V.qs = f(14080, 14464)
    V.qb = f(14464, 15872).bitcast(BF16)
    V.qt = f(15872, 15936).bitcast(BF16)
    V.hqt = f(15936, 17344).bitcast(BF16)
    V.xq = f(17344, 17856).bitcast(BF16)
    V.o1 = f(17856, 18880)
    V.wfs = f(18880, 24512).bitcast(BF16)
    V.fv1 = f(24448, 24576)
    V.big1 = f(0, 4096)
    V.bigb = f(4096, 6144).bitcast(BF16)
    V.win = ab[:, :]
    V.ygall = ab[:, :]
    return V


def emit_program(role, E, T, S, DR):
    V = mkviews(T)
    cnt = {}
    stop = {"hit": False}

    def n_(sem):
        return cnt.get(sem, 0)

    def w(r, sem, val=None):
        if role == r:
            E.wait_ge(S[sem], n_(sem) if val is None else val)

    def op(r, sem, fn, amt=1):
        inst = fn(E) if role == r else None
        cnt[sem] = cnt.get(sem, 0) + amt
        if inst is not None:
            inst.then_inc(S[sem], amt)

    def dma(r, out_ap, in_ap):
        sem = {"sync": "ld", "gpsimd": "gd"}[r]
        op(r, sem, lambda e: e.dma_start(out=out_ap, in_=in_ap), 16)

    def dma_st(out_ap, in_ap):
        op("sync", "st", lambda e: e.dma_start(out=out_ap, in_=in_ap), 16)

    class StopEmit(Exception):
        pass

    def checkpoint(name):
        if STOP_AFTER is not None and name == STOP_AFTER:
            raise StopEmit()

    def finish_dummy():
        if DUMP_U:
            w("sync", "st")
            w("sync", "dve")
            w("sync", "act")
            w("sync", "pe")
            w("sync", "gd")
            if DUMP_SRC in ("u", "x2"):
                srcb = DR["u"] if DUMP_SRC == "u" else DR["x2cm"]
                for kt in range(DT):
                    for j in range(2):
                        dma_st(DR["out"][(kt + j * DT) * 128:(kt + j * DT + 1) * 128, :],
                               srcb[kt][:, j * 1024:(j + 1) * 1024])
            else:
                srcb = DR["xc"] if DUMP_SRC == "xc" else DR["g"]
                for et in range(8):
                    for j in range(2):
                        dma_st(DR["out"][(et * 2 + j) * 128:(et * 2 + j + 1) * 128, :],
                               srcb[et][:, j * 1024:(j + 1) * 1024])
        else:
            op("gpsimd", "gps", lambda e: e.memset(T["t_sc1"][:, :], 0.0))
            w("sync", "gps")
            for lt in range(LT):
                for j in range(2):
                    dma_st(DR["out"][lt * 128:(lt + 1) * 128, j * 512:(j + 1) * 512],
                           T["t_sc1"][:, 0:512])
        w("sync", "st")

    p_a, p_b = T["p_a"], T["p_b"]

    try:
        return _emit_body(role, E, T, S, DR, V, cnt, n_, w, op, dma, dma_st,
                          checkpoint, StopEmit, finish_dummy, p_a, p_b)
    except StopEmit:
        finish_dummy()
        return cnt


def _noop():
    pass


def _emit_body(role, E, T, S, DR, V, cnt, n_, w, op, dma, dma_st, checkpoint,
               StopEmit, finish_dummy, p_a, p_b):
    dma("sync", T["t_rmsw"][:, :], DR["rmsw"][:])
    dma("sync", T["t_lnw"][:, :], DR["lnw"][:])
    dma("sync", T["t_lnb"][:, :], DR["lnb"][:])
    dma("sync", T["t_cw"][:, :].rearrange("p (e k) -> p e k", k=KC),
        DR["convw"].rearrange("(e p) k -> p e k", p=128))
    dma("sync", T["t_cb"][:, :], DR["convb"][:])
    dma("sync", T["t_bdt"][:, :], DR["bdt"][:])
    dma("sync", T["t_dp"][:, :], DR["dp"][:])
    dma("sync", T["t_alog"][:, :].rearrange("p (e n) -> p e n", n=N),
        DR["alog"].rearrange("(e p) n -> p e n", p=128))
    dma("sync", T["t_wx"][:, :].rearrange("p (e k) -> p e k", k=96),
        DR["wx"].rearrange("(e p) k -> p e k", p=128))
    dma("sync", T["t_wdt"][:, :].rearrange("p (e k) -> p e k", k=128),
        DR["wdt"].rearrange("r (e k) -> r e k", k=128))
    dma("sync", T["t_selbc"][:, :].rearrange("p (s k) -> p s k", k=128),
        DR["selbc"].transpose([1, 0, 2]))
    dma("sync", T["t_ident"][:, :], DR["ident"][:])
    dma("sync", T["t_identf"][:, :], DR["identf"][:])
    dma("sync", T["t_selmask"][:, :], DR["selmask"][:])
    dma("sync", T["t_halom"][:, :], DR["halom"][:])
    LDC = n_("ld")
    op("gpsimd", "gps", lambda e: e.memset(T["t_ones"][:, :], 1.0))
    op("gpsimd", "gps", lambda e: e.memset(T["t_ones1"][0:1, :], 1.0))

    w("scalar", "ld", LDC)
    for e_ in range(ET):
        op("scalar", "act", lambda eng, e_=e_: eng.activation(
            T["t_sc1"][:, e_ * N:(e_ + 1) * N], T["t_alog"][:, e_ * N:(e_ + 1) * N], ACTF.Exp))
    w("vector", "act")
    for e_ in range(ET):
        op("vector", "dve", lambda eng, e_=e_: eng.tensor_scalar_mul(
            T["t_alog"][:, e_ * N:(e_ + 1) * N], T["t_sc1"][:, e_ * N:(e_ + 1) * N], -1.0))

    w("sync", "ld", LDC)
    for kt in range(DT):
        w("sync", "dve")
        dma("sync", V.big1, DR["win"][kt * 128:(kt + 1) * 128, :])
        w("vector", "ld")
        w("vector", "st")
        op("vector", "dve", lambda e, kt=kt: e.tensor_scalar_mul(
            V.bigb, V.big1, T["t_rmsw"][:, kt:kt + 1]))
        w("sync", "dve")
        dma_st(DR["wib"][kt], V.bigb)
    for et in range(ET):
        w("sync", "dve")
        dma("sync", V.big1[:, 0:D], DR["wout"][et * 128:(et + 1) * 128, :])
        w("vector", "ld")
        w("vector", "st")
        op("vector", "dve", lambda e, et=et: e.tensor_copy(V.bigb[:, 0:D], V.big1[:, 0:D]))
        op("vector", "dve", lambda e, et=et: e.tensor_sub(
            V.bigb[:, D:2 * D], V.big1[:, 0:D], V.bigb[:, 0:D]))
        w("sync", "dve")
        dma_st(DR["wob"][et], V.bigb[:, 0:D])
        dma_st(DR["wobl"][et], V.bigb[:, D:2 * D])

    checkpoint("wconv")
    def quant(src_ap, width, outb_ap):
        nb = width // BQ
        op("vector", "dve", lambda e: e.tensor_reduce(
            V.qs[:, 0:nb], src_ap.rearrange("p (a b) -> p a b", b=BQ),
            axis=AX.X, op=ALU.max, apply_absolute_value=True))
        op("vector", "dve", lambda e: e.tensor_scalar(
            V.qs[:, 0:nb], V.qs[:, 0:nb], 1.0 / 6.0, 1e-12, op0=ALU.mult, op1=ALU.max))
        op("vector", "dve", lambda e: e.reciprocal(V.qs[:, nb:2 * nb], V.qs[:, 0:nb]))
        op("vector", "dve", lambda e: e.tensor_tensor(
            V.q2[:, 0:width].rearrange("p (a b) -> p a b", b=BQ),
            src_ap.rearrange("p (a b) -> p a b", b=BQ),
            V.qs[:, nb:2 * nb].unsqueeze(2).broadcast_to([128, nb, BQ]), op=ALU.mult))
        w("scalar", "dve")
        op("scalar", "act", lambda e: e.activation(V.q3[:, 0:width], V.q2[:, 0:width], ACTF.Abs))
        w("vector", "act")
        op("vector", "dve", lambda e: e.tensor_single_scalar(
            V.q1[:, 0:width], V.q3[:, 0:width], 0.25, op=ALU.is_gt))
        for m in (0.75, 1.25, 1.75):
            op("vector", "dve", lambda e, m=m: e.scalar_tensor_tensor(
                V.q1[:, 0:width], V.q3[:, 0:width], m, V.q1[:, 0:width],
                op0=ALU.is_gt, op1=ALU.add))
        op("vector", "dve", lambda e: e.tensor_scalar_mul(V.q1[:, 0:width], V.q1[:, 0:width], 0.5))
        for m in (2.5, 3.5):
            op("vector", "dve", lambda e, m=m: e.scalar_tensor_tensor(
                V.q1[:, 0:width], V.q3[:, 0:width], m, V.q1[:, 0:width],
                op0=ALU.is_gt, op1=ALU.add))
        op("vector", "dve", lambda e: e.tensor_single_scalar(
            V.q2[:, 0:width], V.q3[:, 0:width], 5.0, op=ALU.is_gt))
        op("vector", "dve", lambda e: e.scalar_tensor_tensor(
            V.q1[:, 0:width], V.q2[:, 0:width], 2.0, V.q1[:, 0:width],
            op0=ALU.mult, op1=ALU.add))
        op("vector", "dve", lambda e: e.tensor_single_scalar(
            V.q2[:, 0:width], src_ap, 0.0, op=ALU.is_lt))
        op("vector", "dve", lambda e: e.tensor_scalar(
            V.q2[:, 0:width], V.q2[:, 0:width], -2.0, 1.0, op0=ALU.mult, op1=ALU.add))
        op("vector", "dve", lambda e: e.tensor_tensor(
            V.q1[:, 0:width], V.q1[:, 0:width], V.q2[:, 0:width], op=ALU.mult))
        op("vector", "dve", lambda e: e.tensor_tensor(
            outb_ap.rearrange("p (a b) -> p a b", b=BQ),
            V.q1[:, 0:width].rearrange("p (a b) -> p a b", b=BQ),
            V.qs[:, 0:nb].unsqueeze(2).broadcast_to([128, nb, BQ]), op=ALU.mult))

    for (src_d, dst_d, ntile, wid) in ((DR["wg"], DR["wgq"], ITF, D),
                                       (DR["wu"], DR["wuq"], ITF, D),
                                       (DR["wd"], DR["wdq"], DT, IF)):
        for ti in range(ntile):
            w("sync", "dve")
            dma("sync", V.q0[:, 0:wid], src_d[ti * 128:(ti + 1) * 128, 0:wid])
            w("vector", "ld")
            w("scalar", "ld")
            quant(V.q0[:, 0:wid], wid, V.qb[:, 0:wid])
            for wt in range(wid // 128):
                w("tensor", "dve")
                op("tensor", "pe", lambda e, wt=wt: e.transpose(
                    p_a[0:128, 0:64].bitcast(BF16), V.qb[:, wt * 128:(wt + 1) * 128], T["t_ident"][:, :]))
                w("vector", "pe")
                w("vector", "st")
                op("vector", "dve", lambda e: e.tensor_copy(V.qt[:, 0:128], p_a[0:128, 0:64].bitcast(BF16)))
                w("sync", "dve")
                dma_st(dst_d[wt][:, ti * 128:(ti + 1) * 128], V.qt[:, 0:128])

    checkpoint("ffnq")
    def emit_ln(src_tiles, dst_tiles, scaled=False, add_tiles=None):
        w("sync", "st")
        for scn in range(LC // 512):
            sl = slice(scn * 512, (scn + 1) * 512)
            for kt in range(DT):
                w("sync", "dve")
                dma("sync", V.u[:, kt * 512:(kt + 1) * 512], src_tiles[kt][:, sl])
            LDU = n_("ld")
            w("scalar", "ld", LDU)
            w("tensor", "ld", LDU)
            w("tensor", "dve")
            for kt in range(DT):
                w("scalar", "pe")
                op("scalar", "act", lambda e, kt=kt: e.activation(
                    V.v1, V.u[:, kt * 512:(kt + 1) * 512], ACTF.Square))
                w("tensor", "act")
                op("tensor", "pe", lambda e, kt=kt: e.matmul(
                    p_b[0:1, 0:512], T["t_ones"][:, 0:1], V.u[:, kt * 512:(kt + 1) * 512],
                    start=(kt == 0), stop=(kt == DT - 1)))
                op("tensor", "pe", lambda e, kt=kt: e.matmul(
                    p_b[0:1, 512:1024], T["t_ones"][:, 0:1], V.v1,
                    start=(kt == 0), stop=(kt == DT - 1)))
            rw = V.row
            w("vector", "pe")
            op("vector", "dve", lambda e: e.tensor_scalar_mul(rw[0:1, 0:512], p_b[0:1, 0:512], 1.0 / D))
            op("vector", "dve", lambda e: e.tensor_scalar_mul(rw[0:1, 512:1024], p_b[0:1, 512:1024], 1.0 / D))
            op("vector", "dve", lambda e: e.tensor_tensor(
                V.v2[0:1, 0:512], rw[0:1, 0:512], rw[0:1, 0:512], op=ALU.mult))
            op("vector", "dve", lambda e: e.tensor_sub(
                rw[0:1, 512:1024], rw[0:1, 512:1024], V.v2[0:1, 0:512]))
            op("vector", "dve", lambda e: e.tensor_scalar_add(rw[0:1, 512:1024], rw[0:1, 512:1024], EPS))
            op("vector", "dve", lambda e: e.reciprocal(V.v2[0:1, 0:512], rw[0:1, 512:1024]))
            w("scalar", "dve")
            op("scalar", "act", lambda e: e.activation(V.v1[0:1, 0:512], V.v2[0:1, 0:512], ACTF.Sqrt))
            w("vector", "act")
            op("vector", "dve", lambda e: e.tensor_tensor(
                V.v3[0:1, 0:512], V.v1[0:1, 0:512], V.v1[0:1, 0:512], op=ALU.mult))
            op("vector", "dve", lambda e: e.tensor_tensor(
                V.v3[0:1, 0:512], V.v3[0:1, 0:512], rw[0:1, 512:1024], op=ALU.mult))
            op("vector", "dve", lambda e: e.tensor_scalar(
                V.v3[0:1, 0:512], V.v3[0:1, 0:512], -0.5, 1.5, op0=ALU.mult, op1=ALU.add))
            op("vector", "dve", lambda e: e.tensor_tensor(
                rw[0:1, 512:1024], V.v1[0:1, 0:512], V.v3[0:1, 0:512], op=ALU.mult))
            w("tensor", "dve")
            w("tensor", "act")
            op("tensor", "pe", lambda e: e.matmul(
                p_b[0:128, 1024:1536], T["t_ones1"][0:1, 0:128], rw[0:1, 0:512], start=True, stop=True))
            op("tensor", "pe", lambda e: e.matmul(
                p_b[0:128, 1536:2048], T["t_ones1"][0:1, 0:128], rw[0:1, 512:1024], start=True, stop=True))
            w("vector", "pe")
            for kt in range(DT):
                w("vector", "st")
                op("vector", "dve", lambda e, kt=kt: e.tensor_sub(
                    V.v2, V.u[:, kt * 512:(kt + 1) * 512], p_b[0:128, 1024:1536]))
                op("vector", "dve", lambda e, kt=kt: e.tensor_tensor(
                    V.v3, V.v2, p_b[0:128, 1536:2048], op=ALU.mult))
                if scaled:
                    op("vector", "dve", lambda e, kt=kt: e.tensor_scalar(
                        V.v3, V.v3, T["t_lnw"][:, kt:kt + 1], T["t_lnb"][:, kt:kt + 1],
                        op0=ALU.mult, op1=ALU.add))
                if add_tiles is not None:
                    w("sync", "dve")
                    dma("sync", V.xc1, add_tiles[kt][:, sl])
                    w("vector", "ld")
                    op("vector", "dve", lambda e: e.tensor_add(V.v3, V.v3, V.xc1))
                w("sync", "dve")
                dma_st(dst_tiles[kt][:, sl], V.v3)

    u_t = [DR["u"][kt] for kt in range(DT)]
    x_t = [DR["x_cm"][kt * 128:(kt + 1) * 128, :] for kt in range(DT)]
    x2_t = [DR["x2cm"][kt] for kt in range(DT)]
    emit_ln(x_t, u_t)

    checkpoint("l1")
    for k in range(DEPTH):
        w("gpsimd", "st")
        for kt in range(DT):
            dma("gpsimd", DR["aghs"][k][kt * 128:(kt + 1) * 128, :], DR["u"][kt][:, LC - HW:LC])
        w("gpsimd", "gd")
        op("gpsimd", "ag", lambda e, k=k: e.collective_compute(
            "AllGather", ALU.bypass, ins=[DR["aghs"][k][:]], outs=[DR["aghd"][k][:]],
            replica_groups=RG8))
        w("gpsimd", "ag")
        w("gpsimd", "dve")
        for r_ in range(8):
            dma("gpsimd",
                V.halg.rearrange("p (a r h) -> p a r h", r=8, h=HW)[:, :, r_, :],
                DR["aghd"][k][r_].rearrange("(a p) h -> p a h", p=128))
        w("vector", "gd")
        checkpoint("haloA%d" % k)
        op("vector", "dve", lambda e: e.tensor_tensor(
            V.halt.rearrange("p (a h r) -> p a h r", h=HW, r=8),
            V.halg.rearrange("p (a r h) -> p a r h", r=8, h=HW).transpose([0, 1, 3, 2]),
            T["t_halom"][:, :].rearrange("p (h r) -> p h r", r=8).unsqueeze(1).broadcast_to([128, DT, HW, 8]),
            op=ALU.mult))
        op("vector", "dve", lambda e: e.tensor_reduce(
            T["t_hal"][:, :].rearrange("p (a h) -> p a h", h=HW),
            V.halt.rearrange("p (a h r) -> p a h r", h=HW, r=8),
            axis=AX.X, op=ALU.add))
        HAL_DVE = n_("dve")
        checkpoint("haloB%d" % k)
        w("scalar", "dve", HAL_DVE)
        w("tensor", "dve", HAL_DVE)
        for kt in range(DT):
            op("scalar", "act", lambda e, kt=kt: e.activation(
                T["t_sc1"][:, 0:HW], T["t_hal"][:, kt * HW:(kt + 1) * HW], ACTF.Square))
            w("tensor", "act")
            op("tensor", "pe", lambda e, kt=kt: e.matmul(
                p_b[0:1, 0:HW], T["t_ones"][:, 0:1], T["t_sc1"][:, 0:HW],
                start=(kt == 0), stop=(kt == DT - 1)))
        w("vector", "pe")
        op("vector", "dve", lambda e: e.tensor_scalar(
            V.row[0:1, 0:HW], p_b[0:1, 0:HW], 1.0 / D, EPS, op0=ALU.mult, op1=ALU.add))
        op("vector", "dve", lambda e: e.reciprocal(V.row[0:1, HW:2 * HW], V.row[0:1, 0:HW]))
        w("scalar", "dve")
        op("scalar", "act", lambda e: e.activation(V.xc1[0:1, 0:HW], V.row[0:1, HW:2 * HW], ACTF.Sqrt))
        w("vector", "act")
        op("vector", "dve", lambda e: e.tensor_tensor(
            V.xc2[0:1, 0:HW], V.xc1[0:1, 0:HW], V.xc1[0:1, 0:HW], op=ALU.mult))
        op("vector", "dve", lambda e: e.tensor_tensor(
            V.xc2[0:1, 0:HW], V.xc2[0:1, 0:HW], V.row[0:1, 0:HW], op=ALU.mult))
        op("vector", "dve", lambda e: e.tensor_scalar(
            V.xc2[0:1, 0:HW], V.xc2[0:1, 0:HW], -0.5, 1.5, op0=ALU.mult, op1=ALU.add))
        op("vector", "dve", lambda e: e.tensor_tensor(
            V.row[0:1, 0:HW], V.xc1[0:1, 0:HW], V.xc2[0:1, 0:HW], op=ALU.mult))
        w("tensor", "dve")
        w("tensor", "act")
        op("tensor", "pe", lambda e: e.matmul(
            p_b[0:128, 512:512 + HW], T["t_ones1"][0:1, 0:128], V.row[0:1, 0:HW], start=True, stop=True))
        w("vector", "pe")
        for kt in range(DT):
            op("vector", "dve", lambda e, kt=kt: e.tensor_tensor(
                T["t_halnb"][:, kt * HW:(kt + 1) * HW], T["t_hal"][:, kt * HW:(kt + 1) * HW],
                p_b[0:128, 512:512 + HW], op=ALU.mult))
        HALN_DVE = n_("dve")
        checkpoint("haloC%d" % k)
        w("sync", "pe")
        for kt in range(DT):
            dma("sync", V.win[:, kt * 4096:(kt + 1) * 4096], DR["wib"][kt])
        LD_WIN = n_("ld")
        w("tensor", "dve", HALN_DVE)
        w("tensor", "ld", LD_WIN)
        for et in range(ET):
            pst = (et % 2) * 512
            if et >= 2:
                w("tensor", "dve")
            for kt in range(DT):
                op("tensor", "pe", lambda e, et=et, kt=kt, pst=pst: e.matmul(
                    p_a[0:128, pst:pst + HW],
                    V.win[:, kt * 4096 + et * 128: kt * 4096 + (et + 1) * 128],
                    T["t_halnb"][:, kt * HW:(kt + 1) * HW],
                    start=(kt == 0), stop=(kt == DT - 1)))
            w("vector", "pe")
            op("vector", "dve", lambda e, et=et, pst=pst: e.tensor_copy(
                V.halxs[:, et * HW:(et + 1) * HW], p_a[0:128, pst:pst + HW]))
        HALXS_DVE = n_("dve")
        checkpoint("halo%d" % k)

        w("sync", "st")
        for scn in range(LC // 512):
            sl = slice(scn * 512, (scn + 1) * 512)
            for kt in range(DT):
                w("sync", "dve")
                dma("sync", V.u[:, kt * 512:(kt + 1) * 512], DR["u"][kt][:, sl])
            LDU = n_("ld")
            w("scalar", "ld", LDU)
            w("tensor", "ld", LDU)
            w("tensor", "dve")
            for kt in range(DT):
                op("scalar", "act", lambda e, kt=kt: e.activation(
                    V.v1, V.u[:, kt * 512:(kt + 1) * 512], ACTF.Square))
                w("tensor", "act")
                op("tensor", "pe", lambda e, kt=kt: e.matmul(
                    p_b[0:1, 0:512], T["t_ones"][:, 0:1], V.v1,
                    start=(kt == 0), stop=(kt == DT - 1)))
            w("vector", "pe")
            op("vector", "dve", lambda e: e.tensor_scalar(
                V.row[0:1, 0:512], p_b[0:1, 0:512], 1.0 / D, EPS, op0=ALU.mult, op1=ALU.add))
            op("vector", "dve", lambda e: e.reciprocal(V.row[0:1, 512:1024], V.row[0:1, 0:512]))
            w("scalar", "dve")
            op("scalar", "act", lambda e: e.activation(V.xc1[0:1, 0:512], V.row[0:1, 512:1024], ACTF.Sqrt))
            w("vector", "act")
            op("vector", "dve", lambda e: e.tensor_tensor(
                V.xc2[0:1, 0:512], V.xc1[0:1, 0:512], V.xc1[0:1, 0:512], op=ALU.mult))
            op("vector", "dve", lambda e: e.tensor_tensor(
                V.xc2[0:1, 0:512], V.xc2[0:1, 0:512], V.row[0:1, 0:512], op=ALU.mult))
            op("vector", "dve", lambda e: e.tensor_scalar(
                V.xc2[0:1, 0:512], V.xc2[0:1, 0:512], -0.5, 1.5, op0=ALU.mult, op1=ALU.add))
            op("vector", "dve", lambda e: e.tensor_tensor(
                V.row[0:1, 0:512], V.xc1[0:1, 0:512], V.xc2[0:1, 0:512], op=ALU.mult))
            w("tensor", "dve")
            op("tensor", "pe", lambda e: e.matmul(
                p_b[0:128, 512:1024], T["t_ones1"][0:1, 0:128], V.row[0:1, 0:512], start=True, stop=True))
            w("vector", "pe")
            for kt in range(DT):
                half = kt % 4
                op("vector", "dve", lambda e, kt=kt, half=half: e.tensor_tensor(
                    V.unf[:, half * 512:(half + 1) * 512], V.u[:, kt * 512:(kt + 1) * 512],
                    p_b[0:128, 512:1024], op=ALU.mult))
                op("vector", "dve", lambda e, kt=kt, half=half: e.tensor_copy(
                    V.un[:, kt * 512:(kt + 1) * 512], V.unf[:, half * 512:(half + 1) * 512]))
                op("vector", "dve", lambda e, kt=kt, half=half: e.tensor_sub(
                    V.unlo[:, kt * 512:(kt + 1) * 512], V.unf[:, half * 512:(half + 1) * 512],
                    V.un[:, kt * 512:(kt + 1) * 512]))
            UN_DVE = n_("dve")
            w("tensor", "ld", LD_WIN)
            w("tensor", "dve", UN_DVE)
            for jt in range(2 * ET):
                pslot = jt % 2
                pap = p_a[0:128, pslot * 512:(pslot + 1) * 512]
                if jt >= 2:
                    if pslot == 0:
                        w("tensor", "dve")
                    else:
                        w("tensor", "act")
                for kt in range(DT):
                    op("tensor", "pe", lambda e, jt=jt, kt=kt, pap=pap: e.matmul(
                        pap, V.win[:, kt * 4096 + jt * 128: kt * 4096 + (jt + 1) * 128],
                        V.un[:, kt * 512:(kt + 1) * 512],
                        start=(kt == 0), stop=False))
                    op("tensor", "pe", lambda e, jt=jt, kt=kt, pap=pap: e.matmul(
                        pap, V.win[:, kt * 4096 + jt * 128: kt * 4096 + (jt + 1) * 128],
                        V.unlo[:, kt * 512:(kt + 1) * 512],
                        start=False, stop=(kt == DT - 1)))
                if jt < ET:
                    et = jt
                    w("vector", "pe")
                    op("vector", "dve", lambda e, et=et, pap=pap: e.tensor_copy(
                        V.xs[:, et * 520 + KC - 1: et * 520 + KC - 1 + 512], pap))
                else:
                    et = jt - ET
                    w("scalar", "pe")
                    w("scalar", "st")
                    op("scalar", "act", lambda e, pap=pap: e.activation(V.gst, pap, ACTF.Silu))
                    w("sync", "act")
                    dma_st(DR["g"][et][:, sl], V.gst)
            for et in range(ET):
                base = et * 520
                if scn == 0:
                    op("vector", "dve", lambda e, et=et, base=base: e.tensor_copy(
                        V.xs[:, base:base + KC - 1],
                        V.halxs[:, et * HW + HW - (KC - 1): (et + 1) * HW]))
                op("vector", "dve", lambda e, et=et, base=base: e.tensor_scalar_mul(
                    V.xc1, V.xs[:, base:base + 512], T["t_cw"][:, et * KC:et * KC + 1]))
                for kk in range(1, KC):
                    op("vector", "dve", lambda e, et=et, base=base, kk=kk: e.scalar_tensor_tensor(
                        V.xc1, V.xs[:, base + kk:base + kk + 512],
                        T["t_cw"][:, et * KC + kk:et * KC + kk + 1], V.xc1,
                        op0=ALU.mult, op1=ALU.add))
                w("scalar", "dve")
                w("scalar", "st")
                w("scalar", "pe")
                op("scalar", "act", lambda e, et=et: e.activation(
                    V.xc2, V.xc1, ACTF.Silu, bias=T["t_cb"][:, et:et + 1]))
                w("sync", "act")
                dma_st(DR["xc"][et][:, sl], V.xc2)
                w("tensor", "act")
                op("tensor", "pe", lambda e, et=et: e.matmul(
                    p_b[0:96, 1024:1536], T["t_wx"][:, et * 96:(et + 1) * 96], V.xc2,
                    start=(et == 0), stop=(et == ET - 1)))
                op("vector", "dve", lambda e, et=et, base=base: e.tensor_copy(
                    T["t_tail"][:, et * 4:et * 4 + KC - 1],
                    V.xs[:, base + 512: base + 512 + KC - 1]))
            w("vector", "pe")
            op("vector", "dve", lambda e, sl=sl: e.tensor_copy(T["t_dbc"][0:96, sl], p_b[0:96, 1024:1536]))
            if scn < LC // 512 - 1:
                for et in range(ET):
                    op("vector", "dve", lambda e, et=et: e.tensor_copy(
                        V.xs[:, et * 520:et * 520 + KC - 1],
                        T["t_tail"][:, et * 4:et * 4 + KC - 1]))
        DBC_DVE = n_("dve")
        XCG_ST = n_("st")
        checkpoint("pass1_%d" % k)

        for s in range(SR):
            t = s % ET
            w("sync", "st", XCG_ST)
            w("sync", "dve")
            dma("sync", V.xct, DR["xc"][t])
            dma("sync", V.gt, DR["g"][t])
            LD_SLOT = n_("ld")
            w("tensor", "dve")
            w("tensor", "act")
            for lb in range(4):
                op("tensor", "pe", lambda e, t=t, lb=lb: e.matmul(
                    p_a[0:128, lb * 512:(lb + 1) * 512],
                    T["t_wdt"][:, t * 128:(t + 1) * 128],
                    T["t_dbc"][0:64, lb * 512:(lb + 1) * 512], start=True, stop=True))
            PE_DEL = n_("pe")
            w("scalar", "pe", PE_DEL)
            for lb in range(4):
                sl2 = slice(lb * 512, (lb + 1) * 512)
                op("scalar", "act", lambda e, t=t, sl2=sl2: e.activation(
                    V.m[:, sl2], p_a[0:128, sl2], ACTF.Relu, bias=T["t_bdt"][:, t:t + 1]))
                op("scalar", "act", lambda e, t=t, sl2=sl2: e.activation(
                    V.tmp2[:, sl2], p_a[0:128, sl2], ACTF.Abs, bias=T["t_bdt"][:, t:t + 1]))
                op("scalar", "act", lambda e, sl2=sl2: e.activation(
                    V.tmp2[:, sl2], V.tmp2[:, sl2], ACTF.Exp, scale=-1.0))
                op("scalar", "act", lambda e, sl2=sl2: e.activation(
                    V.tmp2[:, sl2], V.tmp2[:, sl2], ACTF.Ln, bias=1.0))
            ACT_SP = n_("act")
            w("vector", "act", ACT_SP)
            op("vector", "dve", lambda e: e.tensor_add(V.dl, V.m, V.tmp2))
            w("vector", "ld", LD_SLOT)
            op("vector", "dve", lambda e: e.tensor_tensor(
                V.w, V.dl, V.xct, op=ALU.mult))
            DVE_W = n_("dve")
            SCAN_DVE = {}
            for n2 in range(N):
                dab = V.da if n2 % 2 == 0 else V.tmp2
                if n2 == 0:
                    w("tensor", "act", ACT_SP)
                    w("tensor", "dve", DVE_W)
                else:
                    w("tensor", "dve")
                for lb in range(4):
                    op("tensor", "pe", lambda e, n2=n2, lb=lb: e.matmul(
                        p_a[0:128, lb * 512:(lb + 1) * 512],
                        T["t_selbc"][0:96, n2 * 128:(n2 + 1) * 128],
                        T["t_dbc"][0:96, lb * 512:(lb + 1) * 512], start=True, stop=True))
                PE_B = n_("pe")
                for lb in range(4):
                    op("tensor", "pe", lambda e, n2=n2, lb=lb: e.matmul(
                        p_b[0:128, lb * 512:(lb + 1) * 512],
                        T["t_selbc"][0:96, (N + n2) * 128:(N + n2 + 1) * 128],
                        T["t_dbc"][0:96, lb * 512:(lb + 1) * 512], start=True, stop=True))
                PE_C = n_("pe")
                w("scalar", "dve", SCAN_DVE[n2 - 2] if n2 >= 2 else DVE_W)
                op("scalar", "act", lambda e, t=t, n2=n2, dab=dab: e.activation(
                    dab, V.dl, ACTF.Exp,
                    scale=T["t_alog"][:, t * N + n2:t * N + n2 + 1]))
                ACT_DA = n_("act")
                w("vector", "pe", PE_B)
                op("vector", "dve", lambda e: e.tensor_tensor(
                    V.dbx, V.w, p_a[0:128, 0:2048], op=ALU.mult))
                w("vector", "act", ACT_DA)
                op("vector", "dve", lambda e, n2=n2, dab=dab: e.tensor_tensor_scan(
                    V.h, dab, V.dbx, T["t_carry"][:, n2:n2 + 1], op0=ALU.mult, op1=ALU.add))
                SCAN_DVE[n2] = n_("dve")
                op("vector", "dve", lambda e, n2=n2: e.tensor_copy(
                    T["t_hend"][:, n2:n2 + 1], V.h[:, LC - 1:LC]))
                w("vector", "pe", PE_C)
                op("vector", "dve", lambda e: e.tensor_tensor(
                    V.tmp1, V.h, p_b[0:128, 0:2048], op=ALU.mult))
                if n2 == 0:
                    op("vector", "dve", lambda e: e.tensor_copy(V.yacc, V.tmp1))
                else:
                    op("vector", "dve", lambda e: e.tensor_add(V.yacc, V.yacc, V.tmp1))
            op("vector", "dve", lambda e, t=t: e.scalar_tensor_tensor(
                V.yacc, V.xct, T["t_dp"][:, t:t + 1],
                V.yacc, op0=ALU.mult, op1=ALU.add))
            op("vector", "dve", lambda e, t=t: e.tensor_tensor(
                V.ygf, V.yacc, V.gt, op=ALU.mult))
            op("vector", "dve", lambda e, t=t: e.tensor_copy(
                V.ygall[:, t * LC:(t + 1) * LC], V.ygf))
            w("vector", "st")
            op("vector", "dve", lambda e, t=t: e.tensor_sub(
                V.ylo_st[:, 0:LC], V.ygf, V.ygall[:, t * LC:(t + 1) * LC]))
            w("sync", "dve")
            dma_st(DR["ylo"][t], V.ylo_st[:, 0:LC])
            YG_DVE = n_("dve")
            w("gpsimd", "dve", YG_DVE)
            dma("gpsimd", DR["agrs"][k][s][:], T["t_hend"][:, :])
            w("gpsimd", "gd")
            op("gpsimd", "ag", lambda e, k=k, s=s: e.collective_compute(
                "AllGather", ALU.bypass, ins=[DR["agrs"][k][s][:]], outs=[DR["agrd"][k][s][:]],
                replica_groups=RG8))
            w("gpsimd", "ag")
            for r_ in range(8):
                dma("gpsimd", V.m[:, 128 + r_ * N:128 + (r_ + 1) * N], DR["agrd"][k][s][r_])
            w("vector", "gd")
            op("vector", "dve", lambda e: e.tensor_tensor(
                V.m[:, 0:128].rearrange("p (n r) -> p n r", r=8),
                V.m[:, 128:256].rearrange("p (r n) -> p n r", n=N),
                T["t_selmask"][:, :].rearrange("p (n r) -> p n r", r=8), op=ALU.mult))
            op("vector", "dve", lambda e: e.tensor_reduce(
                T["t_carry"][:, :], V.m[:, 0:128].rearrange("p (n r) -> p n r", r=8),
                axis=AX.X, op=ALU.add))
            checkpoint("round%d_%d" % (k, s))

        W_YG = n_("dve")
        w("sync", "dve", W_YG)
        w("sync", "st")
        for t2 in range(ET):
            dma("sync", V.wo[:, t2 * D:(t2 + 1) * D], DR["wob"][t2])
            dma("sync", V.wo2[:, t2 * D:(t2 + 1) * D], DR["wobl"][t2])
        WO_LD = n_("ld")
        for lb in range(4):
            sl2 = slice(lb * 512, (lb + 1) * 512)
            w("tensor", "dve")
            for t2 in range(ET):
                w("sync", "pe")
                dma("sync", V.ylos[:, 0:512], DR["ylo"][t2][:, sl2])
                w("tensor", "ld")
                for dtile in range(DT):
                    pb = p_a if dtile < 4 else p_b
                    po = (dtile % 4) * 512
                    yhi = V.ygall[:, t2 * LC + lb * 512: t2 * LC + (lb + 1) * 512]
                    whi = V.wo[:, t2 * D + dtile * 128: t2 * D + (dtile + 1) * 128]
                    wlo = V.wo2[:, t2 * D + dtile * 128: t2 * D + (dtile + 1) * 128]
                    op("tensor", "pe", lambda e, pb=pb, po=po, whi=whi, yhi=yhi, t2=t2: e.matmul(
                        pb[0:128, po:po + 512], whi, yhi, start=(t2 == 0), stop=False))
                    op("tensor", "pe", lambda e, pb=pb, po=po, wlo=wlo, yhi=yhi: e.matmul(
                        pb[0:128, po:po + 512], wlo, yhi, start=False, stop=False))
                    op("tensor", "pe", lambda e, pb=pb, po=po, whi=whi, t2=t2: e.matmul(
                        pb[0:128, po:po + 512], whi, V.ylos[:, 0:512],
                        start=False, stop=(t2 == ET - 1)))
            for dtile in range(DT):
                pb = p_a if dtile < 4 else p_b
                po = (dtile % 4) * 512
                w("sync", "dve")
                dma("sync", V.uw, DR["u"][dtile][:, sl2])
                w("vector", "ld")
                w("vector", "pe")
                w("vector", "st")
                op("vector", "dve", lambda e, pb=pb, po=po: e.scalar_tensor_tensor(
                    V.uo, V.uw, 2.0, pb[0:128, po:po + 512], op0=ALU.mult, op1=ALU.add))
                w("sync", "dve")
                dma_st(DR["u"][dtile][:, sl2], V.uo)
        checkpoint("iter%d" % k)

    emit_ln(u_t, x2_t, scaled=True, add_tiles=x_t)
    emit_ln(x2_t, u_t)
    checkpoint("epi")

    w("sync", "st")
    for lt in range(LT):
        for kt in range(DT):
            w("sync", "dve")
            dma("sync", V.v1[:, 0:128], DR["u"][kt][:, lt * 128:(lt + 1) * 128])
            w("tensor", "ld")
            w("tensor", "dve")
            op("tensor", "pe", lambda e: e.transpose(
                p_a[0:128, 0:128], V.v1[:, 0:128], T["t_identf"][:, :]))
            w("vector", "pe")
            op("vector", "dve", lambda e, kt=kt: e.tensor_copy(
                V.q0[:, kt * 128:(kt + 1) * 128], p_a[0:128, 0:128]))
        quant(V.q0[:, 0:D], D, V.qb[:, 0:D])
        for kt in range(DT):
            w("tensor", "dve")
            op("tensor", "pe", lambda e, kt=kt: e.transpose(
                p_a[0:128, 128:192].bitcast(BF16), V.qb[:, kt * 128:(kt + 1) * 128], T["t_ident"][:, :]))
            w("vector", "pe")
            w("vector", "st")
            op("vector", "dve", lambda e: e.tensor_copy(V.qt[:, 0:128], p_a[0:128, 128:192].bitcast(BF16)))
            w("sync", "dve")
            dma_st(DR["xqt"][kt][:, lt * 128:(lt + 1) * 128], V.qt[:, 0:128])

    checkpoint("qx")
    w("sync", "st")
    nib = (IF + 511) // 512
    for lt in range(LT):
        w("sync", "dve")
        for kt in range(DT):
            dma("sync", V.xq[:, kt * 128:(kt + 1) * 128], DR["xqt"][kt][:, lt * 128:(lt + 1) * 128])
        for which, wsrc in ((0, DR["wgq"]), (1, DR["wuq"])):
            for ib in range(nib):
                iw = min(512, IF - ib * 512)
                w("sync", "pe")
                for kt in range(DT):
                    dma("sync", V.wfs[:, kt * 512:kt * 512 + iw], wsrc[kt][:, ib * 512:ib * 512 + iw])
                w("tensor", "ld")
                if which == 0:
                    w("tensor", "act")
                else:
                    w("tensor", "dve")
                for kt in range(DT):
                    op("tensor", "pe", lambda e, kt=kt, iw=iw: e.matmul(
                        p_a[0:128, 0:iw],
                        V.xq[:, kt * 128:(kt + 1) * 128],
                        V.wfs[:, kt * 512:kt * 512 + iw],
                        start=(kt == 0), stop=(kt == DT - 1)))
                if which == 0:
                    w("scalar", "pe")
                    op("scalar", "act", lambda e, ib=ib, iw=iw: e.activation(
                        V.q2[:, ib * 512:ib * 512 + iw], p_a[0:128, 0:iw], ACTF.Silu))
                else:
                    w("vector", "pe")
                    w("vector", "act")
                    op("vector", "dve", lambda e, ib=ib, iw=iw: e.tensor_tensor(
                        V.h2[:, ib * 512:ib * 512 + iw], V.q2[:, ib * 512:ib * 512 + iw],
                        p_a[0:128, 0:iw], op=ALU.mult))
        quant(V.h2[:, 0:IF], IF, V.qb[:, 0:IF])
        for it in range(ITF):
            w("tensor", "dve")
            op("tensor", "pe", lambda e, it=it: e.transpose(
                p_a[0:128, 0:64].bitcast(BF16), V.qb[:, it * 128:(it + 1) * 128], T["t_ident"][:, :]))
            w("vector", "pe")
            op("vector", "dve", lambda e, it=it: e.tensor_copy(
                V.hqt[:, it * 128:(it + 1) * 128], p_a[0:128, 0:64].bitcast(BF16)))
        HQ_DVE = n_("dve")
        for db in range(2):
            w("sync", "pe")
            w("vector", "st")
            for it in range(ITF):
                dma("sync", V.wfs[:, it * 512:(it + 1) * 512], DR["wdq"][it][:, db * 512:(db + 1) * 512])
            w("tensor", "ld")
            w("tensor", "dve")
            for it in range(ITF):
                op("tensor", "pe", lambda e, it=it: e.matmul(
                    p_b[0:128, 0:512], V.hqt[:, it * 128:(it + 1) * 128],
                    V.wfs[:, it * 512:(it + 1) * 512],
                    start=(it == 0), stop=(it == ITF - 1)))
            for kt2 in range(4):
                kt = db * 4 + kt2
                w("sync", "pe")
                dma("sync", V.fv1[:, 0:128], DR["x2cm"][kt][:, lt * 128:(lt + 1) * 128])
                w("tensor", "ld")
                op("tensor", "pe", lambda e, kt2=kt2: e.transpose(
                    p_b[0:128, 1024 + kt2 * 128:1024 + (kt2 + 1) * 128],
                    V.fv1[:, 0:128], T["t_identf"][:, :]))
            w("vector", "pe")
            op("vector", "dve", lambda e: e.tensor_copy(V.q0[:, 0:512], p_b[0:128, 1024:1536]))
            op("vector", "dve", lambda e, db=db: e.tensor_add(
                V.o1[:, db * 512:(db + 1) * 512], p_b[0:128, 0:512], V.q0[:, 0:512]))
        w("sync", "dve")
        dma_st(DR["out"][lt * 128:(lt + 1) * 128, :], V.o1[:, 0:D])
    w("sync", "st")
    return cnt


def build_nc():
    nc = bass.Bass("TRN2", target_bir_lowering=False)
    DR = {}
    DR["x_cm"] = nc.dram_tensor("x_cm", [D, LC], F32, kind="ExternalInput").ap()
    DR["win"] = nc.dram_tensor("Win", [D, 2 * ED], F32, kind="ExternalInput").ap()
    DR["rmsw"] = nc.dram_tensor("rmsw", [128, DT], F32, kind="ExternalInput").ap()
    DR["convw"] = nc.dram_tensor("convw", [ED, KC], F32, kind="ExternalInput").ap()
    DR["convb"] = nc.dram_tensor("convb", [128, ET], F32, kind="ExternalInput").ap()
    DR["wx"] = nc.dram_tensor("Wx", [ED, 96], F32, kind="ExternalInput").ap()
    DR["wdt"] = nc.dram_tensor("Wdt", [R, ED], F32, kind="ExternalInput").ap()
    DR["bdt"] = nc.dram_tensor("bdt", [128, ET], F32, kind="ExternalInput").ap()
    DR["alog"] = nc.dram_tensor("Alog", [ED, N], F32, kind="ExternalInput").ap()
    DR["dp"] = nc.dram_tensor("Dp", [128, ET], F32, kind="ExternalInput").ap()
    DR["wout"] = nc.dram_tensor("Wout", [ED, D], F32, kind="ExternalInput").ap()
    DR["lnw"] = nc.dram_tensor("lnw", [128, DT], F32, kind="ExternalInput").ap()
    DR["lnb"] = nc.dram_tensor("lnb", [128, DT], F32, kind="ExternalInput").ap()
    DR["wg"] = nc.dram_tensor("wg", [IF, D], F32, kind="ExternalInput").ap()
    DR["wu"] = nc.dram_tensor("wu", [IF, D], F32, kind="ExternalInput").ap()
    DR["wd"] = nc.dram_tensor("wd", [D, IF], F32, kind="ExternalInput").ap()
    DR["selmask"] = nc.dram_tensor("selmask", [128, N * 8], F32, kind="ExternalInput").ap()
    DR["halom"] = nc.dram_tensor("halom", [128, HW * 8], F32, kind="ExternalInput").ap()
    DR["selbc"] = nc.dram_tensor("selbc", [32, 96, 128], F32, kind="ExternalInput").ap()
    DR["ident"] = nc.dram_tensor("ident", [128, 128], BF16, kind="ExternalInput").ap()
    DR["identf"] = nc.dram_tensor("identf", [128, 128], F32, kind="ExternalInput").ap()
    DR["out"] = nc.dram_tensor("out", [LC, D], F32, kind="ExternalOutput").ap()

    DR["u"] = [nc.dram_tensor(f"u_scr{i}", [128, LC], F32, kind="Internal").ap() for i in range(DT)]
    DR["x2cm"] = [nc.dram_tensor(f"x2_scr{i}", [128, LC], F32, kind="Internal").ap() for i in range(DT)]
    DR["xc"] = [nc.dram_tensor(f"xc_scr{i}", [128, LC], F32, kind="Internal").ap() for i in range(ET)]
    DR["g"] = [nc.dram_tensor(f"g_scr{i}", [128, LC], F32, kind="Internal").ap() for i in range(ET)]
    DR["wib"] = [nc.dram_tensor(f"wib{i}", [128, 2 * ED], BF16, kind="Internal").ap() for i in range(DT)]
    DR["wob"] = [nc.dram_tensor(f"wob{i}", [128, D], BF16, kind="Internal").ap() for i in range(ET)]
    DR["wobl"] = [nc.dram_tensor(f"wobl{i}", [128, D], BF16, kind="Internal").ap() for i in range(ET)]
    DR["ylo"] = [nc.dram_tensor(f"ylo{i}", [128, LC], BF16, kind="Internal").ap() for i in range(ET)]
    DR["wgq"] = [nc.dram_tensor(f"wgq{i}", [128, IF], BF16, kind="Internal").ap() for i in range(DT)]
    DR["wuq"] = [nc.dram_tensor(f"wuq{i}", [128, IF], BF16, kind="Internal").ap() for i in range(DT)]
    DR["wdq"] = [nc.dram_tensor(f"wdq{i}", [128, D], BF16, kind="Internal").ap() for i in range(ITF)]
    DR["xqt"] = [nc.dram_tensor(f"xqt{i}", [128, LC], BF16, kind="Internal").ap() for i in range(DT)]
    DR["aghs"] = [nc.dram_tensor(f"aghs{k}", [D, HW], F32, kind="Internal").ap() for k in range(DEPTH)]
    DR["aghd"] = [nc.dram_tensor(f"aghd{k}", [8, D, HW], F32, kind="Internal").ap() for k in range(DEPTH)]
    DR["agrs"] = [[nc.dram_tensor(f"agrs{k}_{s}", [128, N], F32, kind="Internal").ap()
                   for s in range(SR)] for k in range(DEPTH)]
    DR["agrd"] = [[nc.dram_tensor(f"agrd{k}_{s}", [8, 128, N], F32, kind="Internal").ap()
                   for s in range(SR)] for k in range(DEPTH)]

    specs = [
        ("t_af", [128, 24576], F32), ("t_ab", [128, 32768], BF16),
        ("t_rmsw", [128, DT], F32), ("t_lnw", [128, DT], F32), ("t_lnb", [128, DT], F32),
        ("t_cw", [128, ET * KC], F32), ("t_cb", [128, ET], F32), ("t_bdt", [128, ET], F32),
        ("t_dp", [128, ET], F32), ("t_alog", [128, ET * N], F32),
        ("t_wx", [128, ET * 96], F32), ("t_wdt", [64, ET * 128], F32),
        ("t_selbc", [96, 32 * 128], F32), ("t_ident", [128, 128], BF16),
        ("t_identf", [128, 128], F32),
        ("t_selmask", [128, N * 8], F32), ("t_halom", [128, HW * 8], F32),
        ("t_ones", [128, 1], F32), ("t_ones1", [1, 128], F32),
        ("t_sc1", [128, 256], F32),
        ("t_dbc", [96, LC], F32),
                ("t_hal", [128, DT * HW], F32), ("t_halnb", [128, DT * HW], BF16),
        ("t_tail", [128, ET * 4], F32),
        ("t_hend", [128, N], F32), ("t_carry", [128, N], F32),
    ]
    ctxs = []
    T = {}
    for (nm, sh, dt_) in specs:
        c = nc.sbuf_tensor(nm, sh, dt_)
        ctxs.append((nm, c))
    for (nm, sh) in [("p_a", [128, 2048]), ("p_b", [128, 2048])]:
        c = nc.psum_tensor(nm, sh, F32)
        ctxs.append((nm, c))
    semnames = ["ld", "st", "gd", "pe", "act", "dve", "gps", "ag"]
    for s_ in semnames:
        ctxs.append(("sem_" + s_, nc.semaphore("sem_" + s_)))
    entered = {}
    for nm, c in ctxs:
        entered[nm] = c.__enter__()
    T = {nm: entered[nm] for nm in entered if not nm.startswith("sem_")}
    S = {s_: entered["sem_" + s_] for s_ in semnames}
    T = {nm: (h.ap() if hasattr(h, "ap") else h) for nm, h in T.items()}

    counts = {}
    with nc.Block() as block:
        @block.sync
        def _(e):
            counts["sync"] = emit_program("sync", e, T, S, DR)

        @block.tensor
        def _(e):
            counts["tensor"] = emit_program("tensor", e, T, S, DR)

        @block.scalar
        def _(e):
            counts["scalar"] = emit_program("scalar", e, T, S, DR)

        @block.vector
        def _(e):
            counts["vector"] = emit_program("vector", e, T, S, DR)

        @block.gpsimd
        def _(e):
            counts["gpsimd"] = emit_program("gpsimd", e, T, S, DR)

    base = counts["sync"]
    for r_, c_ in counts.items():
        assert c_ == base, f"counter divergence in {r_}: {c_} vs {base}"
    for nm, c in reversed(ctxs):
        c.__exit__(None, None, None)
    return nc


def kernel(**inputs):
    bf = ml_dtypes.bfloat16
    x = np.asarray(inputs["L0_attn_x"], dtype=np.float32)[0]
    rmsw = np.asarray(inputs["L0_attn_rms_w"], np.float32)
    Win = np.asarray(inputs["L0_attn_W_in"], np.float32)
    convw = np.asarray(inputs["L0_attn_conv_w"], np.float32)
    convb = np.asarray(inputs["L0_attn_conv_b"], np.float32)
    Wx = np.asarray(inputs["L0_attn_W_x"], np.float32)
    Wdt = np.asarray(inputs["L0_attn_W_dt"], np.float32)
    bdt = np.asarray(inputs["L0_attn_b_dt"], np.float32)
    Alog = np.asarray(inputs["L0_attn_A_log"], np.float32)
    Dp = np.asarray(inputs["L0_attn_Dp"], np.float32)
    Wout = np.asarray(inputs["L0_attn_W_out"], np.float32)
    lnw = np.asarray(inputs["L0_attn_ln_w"], np.float32)
    lnb = np.asarray(inputs["L0_attn_ln_b"], np.float32)
    wg = np.asarray(inputs["L0_ffn_w_gate"], np.float32)
    wu = np.asarray(inputs["L0_ffn_w_up"], np.float32)
    wd = np.asarray(inputs["L0_ffn_w_down"], np.float32)

    ident = np.eye(128, dtype=bf)
    identf = np.eye(128, dtype=np.float32)
    selbc = np.zeros((32, 96, 128), dtype=np.float32)
    for n2 in range(N):
        selbc[n2, 64 + n2, :] = 1.0
        selbc[N + n2, 80 + n2, :] = 1.0

    in_maps = []
    for c in range(NCORE):
        perm = np.concatenate([np.arange(((t - c) % ET) * 128, ((t - c) % ET) * 128 + 128)
                               for t in range(ET)])
        selmask = np.zeros((128, N, 8), np.float32)
        halom = np.zeros((128, HW, 8), np.float32)
        if c > 0:
            selmask[:, :, c - 1] = 1.0
            halom[:, :, c - 1] = 1.0
        xs_rows = x[c * LC:(c + 1) * LC]
        m = {
            "x_cm": np.ascontiguousarray(xs_rows.T),
            "Win": np.ascontiguousarray(
                np.concatenate([Win[:, :ED][:, perm], Win[:, ED:][:, perm]], axis=1)),
            "rmsw": np.ascontiguousarray(rmsw.reshape(DT, 128).T),
            "convw": np.ascontiguousarray(convw[perm]),
            "convb": np.ascontiguousarray(convb[perm].reshape(ET, 128).T),
            "Wx": np.ascontiguousarray(Wx[perm]),
            "Wdt": np.ascontiguousarray(Wdt[:, perm]),
            "bdt": np.ascontiguousarray(bdt[perm].reshape(ET, 128).T),
            "Alog": np.ascontiguousarray(Alog[perm]),
            "Dp": np.ascontiguousarray(Dp[perm].reshape(ET, 128).T),
            "Wout": np.ascontiguousarray(Wout[perm]),
            "lnw": np.ascontiguousarray(lnw.reshape(DT, 128).T),
            "lnb": np.ascontiguousarray(lnb.reshape(DT, 128).T),
            "wg": wg, "wu": wu, "wd": wd,
            "selmask": selmask.reshape(128, N * 8),
            "halom": halom.reshape(128, HW * 8),
            "selbc": selbc, "ident": ident, "identf": identf,
        }
        in_maps.append(m)

    if "nc" not in _NC_CACHE:
        _NC_CACHE["nc"] = build_nc()
    nc = _NC_CACHE["nc"]
    res = run_bass_kernel_spmd(nc, in_maps, core_ids=list(range(NCORE)))
    out = np.concatenate([res.results[c]["out"] for c in range(NCORE)], axis=0)
    return out.reshape(1, L, D).astype(np.float32)


_NC_CACHE = {}


# version 19
# speedup vs baseline: 1.2711x; 1.2711x over previous
import numpy as np
import ml_dtypes
import concourse.bass as bass
import concourse.mybir as mybir
from concourse.bass_utils import run_bass_kernel_spmd

F32 = mybir.dt.float32
BF16 = mybir.dt.bfloat16
ALU = mybir.AluOpType
ACTF = mybir.ActivationFunctionType
AX = mybir.AxisListType

NCORE = 8
L, D, ED, N, R, KC = 16384, 1024, 2048, 16, 64, 4
DEPTH = 6
LC = L // NCORE
ET, DT = ED // 128, D // 128
SR = ET + NCORE - 1
EPS = 1e-5
IF = 2816
ITF = IF // 128
LT = LC // 128
HW = 64
BQ = 16
RG8 = [[0, 1, 2, 3, 4, 5, 6, 7]]
STOP_AFTER = None
DUMP_U = False
DUMP_SRC = "u"


class Views:
    pass


def mkviews(T):
    V = Views()
    af = T["t_af"]
    ab = T["t_ab"]

    def f(a, b):
        return af[:, a:b]

    V.xs = f(0, 8320)
    V.u = f(8320, 12416)
    V.xc1 = f(12416, 12928)
    V.xc2 = f(12928, 13440)
    V.gst = f(13440, 13952)
    V.tail = f(13952, 14976)
    V.un = f(14976, 17024).bitcast(BF16)
    V.unlo = f(17024, 19072).bitcast(BF16)
    V.unf = f(19072, 21120)
    V.dbcb = f(23168, 24192).bitcast(BF16)
    V.xct = f(17024, 19072)
    V.gt = f(19072, 21120)
    V.dl = f(0, 2048)
    V.w = f(2048, 4096)
    V.m = f(4096, 6144)
    V.tmp2 = f(6144, 8192)
    V.da = f(8192, 10240)
    V.dbx = f(10240, 12288)
    V.h = f(12288, 14336)
    V.yacc = f(14336, 16384)
    V.tmp1 = f(21120, 23168)
    V.row = f(23168, 24192)
    V.v1 = f(16384, 16896)
    V.v2 = f(0, 512)
    V.v3 = f(512, 1024)
    V.halg = f(0, 4096)
    V.halt = f(4096, 8192)
    V.halxs = f(21120, 22144)
    V.wo = f(0, 8192).bitcast(BF16)
    V.wo2 = f(8192, 16384).bitcast(BF16)
    V.uw = f(17408, 17920)
    V.uo = f(17920, 18432)
    V.ylos = f(16384, 17408).bitcast(BF16)
    V.ygf = f(21120, 23168)
    V.ylo_st = f(5120, 6144).bitcast(BF16)
    V.q0 = f(0, 2816)
    V.q1 = f(2816, 5632)
    V.q2 = f(5632, 8448)
    V.q3 = f(8448, 11264)
    V.h2 = f(11264, 14080)
    V.qs = f(14080, 14464)
    V.qb = f(14464, 15872).bitcast(BF16)
    V.qt = f(15872, 15936).bitcast(BF16)
    V.hqt = f(15936, 17344).bitcast(BF16)
    V.xq = f(17344, 17856).bitcast(BF16)
    V.o1 = f(17856, 18880)
    V.wfs = f(18880, 24512).bitcast(BF16)
    V.fv1 = f(24448, 24576)
    V.big1 = f(0, 4096)
    V.bigb = f(4096, 6144).bitcast(BF16)
    V.win = ab[:, :]
    V.ygall = ab[:, :]
    return V


def emit_program(role, E, T, S, DR):
    V = mkviews(T)
    cnt = {}
    stop = {"hit": False}

    def n_(sem):
        return cnt.get(sem, 0)

    def w(r, sem, val=None):
        if role == r:
            E.wait_ge(S[sem], n_(sem) if val is None else val)

    def op(r, sem, fn, amt=1):
        inst = fn(E) if role == r else None
        cnt[sem] = cnt.get(sem, 0) + amt
        if inst is not None:
            inst.then_inc(S[sem], amt)

    def dma(r, out_ap, in_ap):
        sem = {"sync": "ld", "gpsimd": "gd"}[r]
        op(r, sem, lambda e: e.dma_start(out=out_ap, in_=in_ap), 16)

    def dma_st(out_ap, in_ap):
        op("sync", "st", lambda e: e.dma_start(out=out_ap, in_=in_ap), 16)

    class StopEmit(Exception):
        pass

    def checkpoint(name):
        if STOP_AFTER is not None and name == STOP_AFTER:
            raise StopEmit()

    def finish_dummy():
        if DUMP_U:
            w("sync", "st")
            w("sync", "dve")
            w("sync", "act")
            w("sync", "pe")
            w("sync", "gd")
            if DUMP_SRC in ("u", "x2"):
                srcb = DR["u"] if DUMP_SRC == "u" else DR["x2cm"]
                for kt in range(DT):
                    for j in range(2):
                        dma_st(DR["out"][(kt + j * DT) * 128:(kt + j * DT + 1) * 128, :],
                               srcb[kt][:, j * 1024:(j + 1) * 1024])
            else:
                srcb = DR["xc"] if DUMP_SRC == "xc" else DR["g"]
                for et in range(8):
                    for j in range(2):
                        dma_st(DR["out"][(et * 2 + j) * 128:(et * 2 + j + 1) * 128, :],
                               srcb[et][:, j * 1024:(j + 1) * 1024])
        else:
            op("gpsimd", "gps", lambda e: e.memset(T["t_sc1"][:, :], 0.0))
            w("sync", "gps")
            for lt in range(LT):
                for j in range(2):
                    dma_st(DR["out"][lt * 128:(lt + 1) * 128, j * 512:(j + 1) * 512],
                           T["t_sc1"][:, 0:512])
        w("sync", "st")

    p_a, p_b = T["p_a"], T["p_b"]

    try:
        return _emit_body(role, E, T, S, DR, V, cnt, n_, w, op, dma, dma_st,
                          checkpoint, StopEmit, finish_dummy, p_a, p_b)
    except StopEmit:
        finish_dummy()
        return cnt


def _noop():
    pass


def _emit_body(role, E, T, S, DR, V, cnt, n_, w, op, dma, dma_st, checkpoint,
               StopEmit, finish_dummy, p_a, p_b):
    dma("sync", T["t_rmsw"][:, :], DR["rmsw"][:])
    dma("sync", T["t_lnw"][:, :], DR["lnw"][:])
    dma("sync", T["t_lnb"][:, :], DR["lnb"][:])
    dma("sync", T["t_cw"][:, :].rearrange("p (e k) -> p e k", k=KC),
        DR["convw"].rearrange("(e p) k -> p e k", p=128))
    dma("sync", T["t_cb"][:, :], DR["convb"][:])
    dma("sync", T["t_bdt"][:, :], DR["bdt"][:])
    dma("sync", T["t_dp"][:, :], DR["dp"][:])
    dma("sync", T["t_alog"][:, :].rearrange("p (e n) -> p e n", n=N),
        DR["alog"].rearrange("(e p) n -> p e n", p=128))
    dma("sync", T["t_wx"][:, :].rearrange("p (e k) -> p e k", k=96),
        DR["wx"].rearrange("(e p) k -> p e k", p=128))
    dma("sync", T["t_wdt"][:, :].rearrange("p (e k) -> p e k", k=128),
        DR["wdt"].rearrange("r (e k) -> r e k", k=128))
    dma("sync", T["t_selbc"][:, :].rearrange("p (s k) -> p s k", k=128),
        DR["selbc"].transpose([1, 0, 2]))
    dma("sync", T["t_ident"][:, :], DR["ident"][:])
    dma("sync", T["t_identf"][:, :], DR["identf"][:])
    dma("sync", T["t_selmask"][:, :], DR["selmask"][:])
    dma("sync", T["t_halom"][:, :], DR["halom"][:])
    LDC = n_("ld")
    op("gpsimd", "gps", lambda e: e.memset(T["t_ones"][:, :], 1.0))
    op("gpsimd", "gps", lambda e: e.memset(T["t_ones1"][0:1, :], 1.0))

    w("scalar", "ld", LDC)
    for e_ in range(ET):
        op("scalar", "act", lambda eng, e_=e_: eng.activation(
            T["t_sc1"][:, e_ * N:(e_ + 1) * N], T["t_alog"][:, e_ * N:(e_ + 1) * N], ACTF.Exp))
    w("vector", "act")
    for e_ in range(ET):
        op("vector", "dve", lambda eng, e_=e_: eng.tensor_scalar_mul(
            T["t_alog"][:, e_ * N:(e_ + 1) * N], T["t_sc1"][:, e_ * N:(e_ + 1) * N], -1.0))

    w("sync", "ld", LDC)
    for kt in range(DT):
        w("sync", "dve")
        dma("sync", V.big1, DR["win"][kt * 128:(kt + 1) * 128, :])
        w("vector", "ld")
        w("vector", "st")
        op("vector", "dve", lambda e, kt=kt: e.tensor_scalar_mul(
            V.bigb, V.big1, T["t_rmsw"][:, kt:kt + 1]))
        w("sync", "dve")
        dma_st(DR["wib"][kt], V.bigb)
    for et in range(ET):
        w("sync", "dve")
        dma("sync", V.big1[:, 0:D], DR["wout"][et * 128:(et + 1) * 128, :])
        w("vector", "ld")
        w("vector", "st")
        op("vector", "dve", lambda e, et=et: e.tensor_copy(V.bigb[:, 0:D], V.big1[:, 0:D]))
        op("vector", "dve", lambda e, et=et: e.tensor_sub(
            V.bigb[:, D:2 * D], V.big1[:, 0:D], V.bigb[:, 0:D]))
        w("sync", "dve")
        dma_st(DR["wob"][et], V.bigb[:, 0:D])
        dma_st(DR["wobl"][et], V.bigb[:, D:2 * D])

    checkpoint("wconv")
    def quant(src_ap, width, outb_ap):
        nb = width // BQ
        op("vector", "dve", lambda e: e.tensor_reduce(
            V.qs[:, 0:nb], src_ap.rearrange("p (a b) -> p a b", b=BQ),
            axis=AX.X, op=ALU.max, apply_absolute_value=True))
        op("vector", "dve", lambda e: e.tensor_scalar(
            V.qs[:, 0:nb], V.qs[:, 0:nb], 1.0 / 6.0, 1e-12, op0=ALU.mult, op1=ALU.max))
        op("vector", "dve", lambda e: e.reciprocal(V.qs[:, nb:2 * nb], V.qs[:, 0:nb]))
        op("vector", "dve", lambda e: e.tensor_tensor(
            V.q2[:, 0:width].rearrange("p (a b) -> p a b", b=BQ),
            src_ap.rearrange("p (a b) -> p a b", b=BQ),
            V.qs[:, nb:2 * nb].unsqueeze(2).broadcast_to([128, nb, BQ]), op=ALU.mult))
        w("scalar", "dve")
        op("scalar", "act", lambda e: e.activation(V.q3[:, 0:width], V.q2[:, 0:width], ACTF.Abs))
        w("vector", "act")
        op("vector", "dve", lambda e: e.tensor_single_scalar(
            V.q1[:, 0:width], V.q3[:, 0:width], 0.25, op=ALU.is_gt))
        for m in (0.75, 1.25, 1.75):
            op("vector", "dve", lambda e, m=m: e.scalar_tensor_tensor(
                V.q1[:, 0:width], V.q3[:, 0:width], m, V.q1[:, 0:width],
                op0=ALU.is_gt, op1=ALU.add))
        op("vector", "dve", lambda e: e.tensor_scalar_mul(V.q1[:, 0:width], V.q1[:, 0:width], 0.5))
        for m in (2.5, 3.5):
            op("vector", "dve", lambda e, m=m: e.scalar_tensor_tensor(
                V.q1[:, 0:width], V.q3[:, 0:width], m, V.q1[:, 0:width],
                op0=ALU.is_gt, op1=ALU.add))
        op("vector", "dve", lambda e: e.tensor_single_scalar(
            V.q2[:, 0:width], V.q3[:, 0:width], 5.0, op=ALU.is_gt))
        op("vector", "dve", lambda e: e.scalar_tensor_tensor(
            V.q1[:, 0:width], V.q2[:, 0:width], 2.0, V.q1[:, 0:width],
            op0=ALU.mult, op1=ALU.add))
        op("vector", "dve", lambda e: e.tensor_single_scalar(
            V.q2[:, 0:width], src_ap, 0.0, op=ALU.is_lt))
        op("vector", "dve", lambda e: e.tensor_scalar(
            V.q2[:, 0:width], V.q2[:, 0:width], -2.0, 1.0, op0=ALU.mult, op1=ALU.add))
        op("vector", "dve", lambda e: e.tensor_tensor(
            V.q1[:, 0:width], V.q1[:, 0:width], V.q2[:, 0:width], op=ALU.mult))
        op("vector", "dve", lambda e: e.tensor_tensor(
            outb_ap.rearrange("p (a b) -> p a b", b=BQ),
            V.q1[:, 0:width].rearrange("p (a b) -> p a b", b=BQ),
            V.qs[:, 0:nb].unsqueeze(2).broadcast_to([128, nb, BQ]), op=ALU.mult))

    for (src_d, dst_d, ntile, wid) in ((DR["wg"], DR["wgq"], ITF, D),
                                       (DR["wu"], DR["wuq"], ITF, D),
                                       (DR["wd"], DR["wdq"], DT, IF)):
        for ti in range(ntile):
            w("sync", "dve")
            dma("sync", V.q0[:, 0:wid], src_d[ti * 128:(ti + 1) * 128, 0:wid])
            w("vector", "ld")
            w("scalar", "ld")
            quant(V.q0[:, 0:wid], wid, V.qb[:, 0:wid])
            for wt in range(wid // 128):
                w("tensor", "dve")
                op("tensor", "pe", lambda e, wt=wt: e.transpose(
                    p_a[0:128, 0:64].bitcast(BF16), V.qb[:, wt * 128:(wt + 1) * 128], T["t_ident"][:, :]))
                w("vector", "pe")
                w("vector", "st")
                op("vector", "dve", lambda e: e.tensor_copy(V.qt[:, 0:128], p_a[0:128, 0:64].bitcast(BF16)))
                w("sync", "dve")
                dma_st(dst_d[wt][:, ti * 128:(ti + 1) * 128], V.qt[:, 0:128])

    checkpoint("ffnq")
    def emit_ln(src_tiles, dst_tiles, scaled=False, add_tiles=None):
        w("sync", "st")
        for scn in range(LC // 512):
            sl = slice(scn * 512, (scn + 1) * 512)
            for kt in range(DT):
                w("sync", "dve")
                dma("sync", V.u[:, kt * 512:(kt + 1) * 512], src_tiles[kt][:, sl])
            LDU = n_("ld")
            w("scalar", "ld", LDU)
            w("tensor", "ld", LDU)
            w("tensor", "dve")
            for kt in range(DT):
                w("scalar", "pe")
                op("scalar", "act", lambda e, kt=kt: e.activation(
                    V.v1, V.u[:, kt * 512:(kt + 1) * 512], ACTF.Square))
                w("tensor", "act")
                op("tensor", "pe", lambda e, kt=kt: e.matmul(
                    p_b[0:1, 0:512], T["t_ones"][:, 0:1], V.u[:, kt * 512:(kt + 1) * 512],
                    start=(kt == 0), stop=(kt == DT - 1)))
                op("tensor", "pe", lambda e, kt=kt: e.matmul(
                    p_b[0:1, 512:1024], T["t_ones"][:, 0:1], V.v1,
                    start=(kt == 0), stop=(kt == DT - 1)))
            rw = V.row
            w("vector", "pe")
            op("vector", "dve", lambda e: e.tensor_scalar_mul(rw[0:1, 0:512], p_b[0:1, 0:512], 1.0 / D))
            op("vector", "dve", lambda e: e.tensor_scalar_mul(rw[0:1, 512:1024], p_b[0:1, 512:1024], 1.0 / D))
            op("vector", "dve", lambda e: e.tensor_tensor(
                V.v2[0:1, 0:512], rw[0:1, 0:512], rw[0:1, 0:512], op=ALU.mult))
            op("vector", "dve", lambda e: e.tensor_sub(
                rw[0:1, 512:1024], rw[0:1, 512:1024], V.v2[0:1, 0:512]))
            op("vector", "dve", lambda e: e.tensor_scalar_add(rw[0:1, 512:1024], rw[0:1, 512:1024], EPS))
            op("vector", "dve", lambda e: e.reciprocal(V.v2[0:1, 0:512], rw[0:1, 512:1024]))
            w("scalar", "dve")
            op("scalar", "act", lambda e: e.activation(V.v1[0:1, 0:512], V.v2[0:1, 0:512], ACTF.Sqrt))
            w("vector", "act")
            op("vector", "dve", lambda e: e.tensor_tensor(
                V.v3[0:1, 0:512], V.v1[0:1, 0:512], V.v1[0:1, 0:512], op=ALU.mult))
            op("vector", "dve", lambda e: e.tensor_tensor(
                V.v3[0:1, 0:512], V.v3[0:1, 0:512], rw[0:1, 512:1024], op=ALU.mult))
            op("vector", "dve", lambda e: e.tensor_scalar(
                V.v3[0:1, 0:512], V.v3[0:1, 0:512], -0.5, 1.5, op0=ALU.mult, op1=ALU.add))
            op("vector", "dve", lambda e: e.tensor_tensor(
                rw[0:1, 512:1024], V.v1[0:1, 0:512], V.v3[0:1, 0:512], op=ALU.mult))
            w("tensor", "dve")
            w("tensor", "act")
            op("tensor", "pe", lambda e: e.matmul(
                p_b[0:128, 1024:1536], T["t_ones1"][0:1, 0:128], rw[0:1, 0:512], start=True, stop=True))
            op("tensor", "pe", lambda e: e.matmul(
                p_b[0:128, 1536:2048], T["t_ones1"][0:1, 0:128], rw[0:1, 512:1024], start=True, stop=True))
            w("vector", "pe")
            for kt in range(DT):
                w("vector", "st")
                op("vector", "dve", lambda e, kt=kt: e.tensor_sub(
                    V.v2, V.u[:, kt * 512:(kt + 1) * 512], p_b[0:128, 1024:1536]))
                op("vector", "dve", lambda e, kt=kt: e.tensor_tensor(
                    V.v3, V.v2, p_b[0:128, 1536:2048], op=ALU.mult))
                if scaled:
                    op("vector", "dve", lambda e, kt=kt: e.tensor_scalar(
                        V.v3, V.v3, T["t_lnw"][:, kt:kt + 1], T["t_lnb"][:, kt:kt + 1],
                        op0=ALU.mult, op1=ALU.add))
                if add_tiles is not None:
                    w("sync", "dve")
                    dma("sync", V.xc1, add_tiles[kt][:, sl])
                    w("vector", "ld")
                    op("vector", "dve", lambda e: e.tensor_add(V.v3, V.v3, V.xc1))
                w("sync", "dve")
                dma_st(dst_tiles[kt][:, sl], V.v3)

    u_t = [DR["u"][kt] for kt in range(DT)]
    x_t = [DR["x_cm"][kt * 128:(kt + 1) * 128, :] for kt in range(DT)]
    x2_t = [DR["x2cm"][kt] for kt in range(DT)]
    emit_ln(x_t, u_t)

    checkpoint("l1")
    for k in range(DEPTH):
        w("gpsimd", "st")
        for kt in range(DT):
            dma("gpsimd", DR["aghs"][k][kt * 128:(kt + 1) * 128, :], DR["u"][kt][:, LC - HW:LC])
        w("gpsimd", "gd")
        op("gpsimd", "ag", lambda e, k=k: e.collective_compute(
            "AllGather", ALU.bypass, ins=[DR["aghs"][k][:]], outs=[DR["aghd"][k][:]],
            replica_groups=RG8))
        w("gpsimd", "ag")
        w("gpsimd", "dve")
        for r_ in range(8):
            dma("gpsimd",
                V.halg.rearrange("p (a r h) -> p a r h", r=8, h=HW)[:, :, r_, :],
                DR["aghd"][k][r_].rearrange("(a p) h -> p a h", p=128))
        w("vector", "gd")
        checkpoint("haloA%d" % k)
        op("vector", "dve", lambda e: e.tensor_tensor(
            V.halt.rearrange("p (a h r) -> p a h r", h=HW, r=8),
            V.halg.rearrange("p (a r h) -> p a r h", r=8, h=HW).transpose([0, 1, 3, 2]),
            T["t_halom"][:, :].rearrange("p (h r) -> p h r", r=8).unsqueeze(1).broadcast_to([128, DT, HW, 8]),
            op=ALU.mult))
        op("vector", "dve", lambda e: e.tensor_reduce(
            T["t_hal"][:, :].rearrange("p (a h) -> p a h", h=HW),
            V.halt.rearrange("p (a h r) -> p a h r", h=HW, r=8),
            axis=AX.X, op=ALU.add))
        HAL_DVE = n_("dve")
        checkpoint("haloB%d" % k)
        w("scalar", "dve", HAL_DVE)
        w("tensor", "dve", HAL_DVE)
        for kt in range(DT):
            op("scalar", "act", lambda e, kt=kt: e.activation(
                T["t_sc1"][:, 0:HW], T["t_hal"][:, kt * HW:(kt + 1) * HW], ACTF.Square))
            w("tensor", "act")
            op("tensor", "pe", lambda e, kt=kt: e.matmul(
                p_b[0:1, 0:HW], T["t_ones"][:, 0:1], T["t_sc1"][:, 0:HW],
                start=(kt == 0), stop=(kt == DT - 1)))
        w("vector", "pe")
        op("vector", "dve", lambda e: e.tensor_scalar(
            V.row[0:1, 0:HW], p_b[0:1, 0:HW], 1.0 / D, EPS, op0=ALU.mult, op1=ALU.add))
        op("vector", "dve", lambda e: e.reciprocal(V.row[0:1, HW:2 * HW], V.row[0:1, 0:HW]))
        w("scalar", "dve")
        op("scalar", "act", lambda e: e.activation(V.xc1[0:1, 0:HW], V.row[0:1, HW:2 * HW], ACTF.Sqrt))
        w("vector", "act")
        op("vector", "dve", lambda e: e.tensor_tensor(
            V.xc2[0:1, 0:HW], V.xc1[0:1, 0:HW], V.xc1[0:1, 0:HW], op=ALU.mult))
        op("vector", "dve", lambda e: e.tensor_tensor(
            V.xc2[0:1, 0:HW], V.xc2[0:1, 0:HW], V.row[0:1, 0:HW], op=ALU.mult))
        op("vector", "dve", lambda e: e.tensor_scalar(
            V.xc2[0:1, 0:HW], V.xc2[0:1, 0:HW], -0.5, 1.5, op0=ALU.mult, op1=ALU.add))
        op("vector", "dve", lambda e: e.tensor_tensor(
            V.row[0:1, 0:HW], V.xc1[0:1, 0:HW], V.xc2[0:1, 0:HW], op=ALU.mult))
        w("tensor", "dve")
        w("tensor", "act")
        op("tensor", "pe", lambda e: e.matmul(
            p_b[0:128, 512:512 + HW], T["t_ones1"][0:1, 0:128], V.row[0:1, 0:HW], start=True, stop=True))
        w("vector", "pe")
        for kt in range(DT):
            op("vector", "dve", lambda e, kt=kt: e.tensor_tensor(
                T["t_halnb"][:, kt * HW:(kt + 1) * HW], T["t_hal"][:, kt * HW:(kt + 1) * HW],
                p_b[0:128, 512:512 + HW], op=ALU.mult))
        HALN_DVE = n_("dve")
        checkpoint("haloC%d" % k)
        w("sync", "pe")
        for kt in range(DT):
            dma("sync", V.win[:, kt * 4096:(kt + 1) * 4096], DR["wib"][kt])
        LD_WIN = n_("ld")
        w("tensor", "dve", HALN_DVE)
        w("tensor", "ld", LD_WIN)
        for et in range(ET):
            pst = (et % 2) * 512
            if et >= 2:
                w("tensor", "dve")
            for kt in range(DT):
                op("tensor", "pe", lambda e, et=et, kt=kt, pst=pst: e.matmul(
                    p_a[0:128, pst:pst + HW],
                    V.win[:, kt * 4096 + et * 128: kt * 4096 + (et + 1) * 128],
                    T["t_halnb"][:, kt * HW:(kt + 1) * HW],
                    start=(kt == 0), stop=(kt == DT - 1)))
            w("vector", "pe")
            op("vector", "dve", lambda e, et=et, pst=pst: e.tensor_copy(
                V.halxs[:, et * HW:(et + 1) * HW], p_a[0:128, pst:pst + HW]))
        HALXS_DVE = n_("dve")
        checkpoint("halo%d" % k)

        w("sync", "st")
        for scn in range(LC // 512):
            sl = slice(scn * 512, (scn + 1) * 512)
            for kt in range(DT):
                w("sync", "dve")
                dma("sync", V.u[:, kt * 512:(kt + 1) * 512], DR["u"][kt][:, sl])
            LDU = n_("ld")
            w("scalar", "ld", LDU)
            w("tensor", "ld", LDU)
            w("tensor", "dve")
            for kt in range(DT):
                op("scalar", "act", lambda e, kt=kt: e.activation(
                    V.v1, V.u[:, kt * 512:(kt + 1) * 512], ACTF.Square))
                w("tensor", "act")
                op("tensor", "pe", lambda e, kt=kt: e.matmul(
                    p_b[0:1, 0:512], T["t_ones"][:, 0:1], V.v1,
                    start=(kt == 0), stop=(kt == DT - 1)))
            w("vector", "pe")
            op("vector", "dve", lambda e: e.tensor_scalar(
                V.row[0:1, 0:512], p_b[0:1, 0:512], 1.0 / D, EPS, op0=ALU.mult, op1=ALU.add))
            op("vector", "dve", lambda e: e.reciprocal(V.row[0:1, 512:1024], V.row[0:1, 0:512]))
            w("scalar", "dve")
            op("scalar", "act", lambda e: e.activation(V.xc1[0:1, 0:512], V.row[0:1, 512:1024], ACTF.Sqrt))
            w("vector", "act")
            op("vector", "dve", lambda e: e.tensor_tensor(
                V.xc2[0:1, 0:512], V.xc1[0:1, 0:512], V.xc1[0:1, 0:512], op=ALU.mult))
            op("vector", "dve", lambda e: e.tensor_tensor(
                V.xc2[0:1, 0:512], V.xc2[0:1, 0:512], V.row[0:1, 0:512], op=ALU.mult))
            op("vector", "dve", lambda e: e.tensor_scalar(
                V.xc2[0:1, 0:512], V.xc2[0:1, 0:512], -0.5, 1.5, op0=ALU.mult, op1=ALU.add))
            op("vector", "dve", lambda e: e.tensor_tensor(
                V.row[0:1, 0:512], V.xc1[0:1, 0:512], V.xc2[0:1, 0:512], op=ALU.mult))
            w("tensor", "dve")
            op("tensor", "pe", lambda e: e.matmul(
                p_b[0:128, 512:1024], T["t_ones1"][0:1, 0:128], V.row[0:1, 0:512], start=True, stop=True))
            w("vector", "pe")
            for kt in range(DT):
                op("vector", "dve", lambda e, kt=kt: e.tensor_tensor(
                    V.un[:, kt * 512:(kt + 1) * 512], V.u[:, kt * 512:(kt + 1) * 512],
                    p_b[0:128, 512:1024], op=ALU.mult))
            UN_DVE = n_("dve")
            w("tensor", "ld", LD_WIN)
            w("tensor", "dve", UN_DVE)
            for jt in range(2 * ET):
                pslot = jt % 2
                pap = p_a[0:128, pslot * 512:(pslot + 1) * 512]
                if jt >= 2:
                    if pslot == 0:
                        w("tensor", "dve")
                    else:
                        w("tensor", "act")
                for kt in range(DT):
                    op("tensor", "pe", lambda e, jt=jt, kt=kt, pap=pap: e.matmul(
                        pap, V.win[:, kt * 4096 + jt * 128: kt * 4096 + (jt + 1) * 128],
                        V.un[:, kt * 512:(kt + 1) * 512],
                        start=(kt == 0), stop=(kt == DT - 1)))
                if jt < ET:
                    et = jt
                    w("vector", "pe")
                    op("vector", "dve", lambda e, et=et, pap=pap: e.tensor_copy(
                        V.xs[:, et * 520 + KC - 1: et * 520 + KC - 1 + 512], pap))
                else:
                    et = jt - ET
                    w("scalar", "pe")
                    w("scalar", "st")
                    op("scalar", "act", lambda e, pap=pap: e.activation(V.gst, pap, ACTF.Silu))
                    w("sync", "act")
                    dma_st(DR["g"][et][:, sl], V.gst)
            for et in range(ET):
                base = et * 520
                if scn == 0:
                    op("vector", "dve", lambda e, et=et, base=base: e.tensor_copy(
                        V.xs[:, base:base + KC - 1],
                        V.halxs[:, et * HW + HW - (KC - 1): (et + 1) * HW]))
                op("vector", "dve", lambda e, et=et, base=base: e.tensor_scalar_mul(
                    V.xc1, V.xs[:, base:base + 512], T["t_cw"][:, et * KC:et * KC + 1]))
                for kk in range(1, KC):
                    op("vector", "dve", lambda e, et=et, base=base, kk=kk: e.scalar_tensor_tensor(
                        V.xc1, V.xs[:, base + kk:base + kk + 512],
                        T["t_cw"][:, et * KC + kk:et * KC + kk + 1], V.xc1,
                        op0=ALU.mult, op1=ALU.add))
                w("scalar", "dve")
                w("scalar", "st")
                w("scalar", "pe")
                op("scalar", "act", lambda e, et=et: e.activation(
                    V.xc2, V.xc1, ACTF.Silu, bias=T["t_cb"][:, et:et + 1]))
                w("sync", "act")
                dma_st(DR["xc"][et][:, sl], V.xc2)
                w("tensor", "act")
                op("tensor", "pe", lambda e, et=et: e.matmul(
                    p_b[0:96, 1024:1536], T["t_wx"][:, et * 96:(et + 1) * 96], V.xc2,
                    start=(et == 0), stop=(et == ET - 1)))
                op("vector", "dve", lambda e, et=et, base=base: e.tensor_copy(
                    T["t_tail"][:, et * 4:et * 4 + KC - 1],
                    V.xs[:, base + 512: base + 512 + KC - 1]))
            w("vector", "pe")
            op("vector", "dve", lambda e, sl=sl: e.tensor_copy(T["t_dbc"][0:96, sl], p_b[0:96, 1024:1536]))
            if scn < LC // 512 - 1:
                for et in range(ET):
                    op("vector", "dve", lambda e, et=et: e.tensor_copy(
                        V.xs[:, et * 520:et * 520 + KC - 1],
                        T["t_tail"][:, et * 4:et * 4 + KC - 1]))
        op("vector", "dve", lambda e: e.tensor_copy(V.dbcb[0:96, 0:2048], T["t_dbc"][0:96, :]))
        DBC_DVE = n_("dve")
        XCG_ST = n_("st")
        checkpoint("pass1_%d" % k)

        for s in range(SR):
            t = s % ET
            w("sync", "st", XCG_ST)
            w("sync", "dve")
            dma("sync", V.xct, DR["xc"][t])
            dma("sync", V.gt, DR["g"][t])
            LD_SLOT = n_("ld")
            w("tensor", "dve")
            w("tensor", "act")
            for lb in range(4):
                op("tensor", "pe", lambda e, t=t, lb=lb: e.matmul(
                    p_a[0:128, lb * 512:(lb + 1) * 512],
                    T["t_wdt"][:, t * 128:(t + 1) * 128],
                    T["t_dbc"][0:64, lb * 512:(lb + 1) * 512], start=True, stop=True))
            PE_DEL = n_("pe")
            w("scalar", "pe", PE_DEL)
            for lb in range(4):
                sl2 = slice(lb * 512, (lb + 1) * 512)
                op("scalar", "act", lambda e, t=t, sl2=sl2: e.activation(
                    V.m[:, sl2], p_a[0:128, sl2], ACTF.Relu, bias=T["t_bdt"][:, t:t + 1]))
                op("scalar", "act", lambda e, t=t, sl2=sl2: e.activation(
                    V.tmp2[:, sl2], p_a[0:128, sl2], ACTF.Abs, bias=T["t_bdt"][:, t:t + 1]))
                op("scalar", "act", lambda e, sl2=sl2: e.activation(
                    V.tmp2[:, sl2], V.tmp2[:, sl2], ACTF.Exp, scale=-1.0))
                op("scalar", "act", lambda e, sl2=sl2: e.activation(
                    V.tmp2[:, sl2], V.tmp2[:, sl2], ACTF.Ln, bias=1.0))
            ACT_SP = n_("act")
            w("vector", "act", ACT_SP)
            op("vector", "dve", lambda e: e.tensor_add(V.dl, V.m, V.tmp2))
            w("vector", "ld", LD_SLOT)
            op("vector", "dve", lambda e: e.tensor_tensor(
                V.w, V.dl, V.xct, op=ALU.mult))
            DVE_W = n_("dve")
            SCAN_DVE = {}
            for n2 in range(N):
                dab = V.da if n2 % 2 == 0 else V.tmp2
                if n2 == 0:
                    w("tensor", "act", ACT_SP)
                    w("tensor", "dve", DVE_W)
                else:
                    w("tensor", "dve")
                for lb in range(4):
                    op("tensor", "pe", lambda e, n2=n2, lb=lb: e.matmul(
                        p_a[0:128, lb * 512:(lb + 1) * 512],
                        T["t_selbc"][0:96, n2 * 128:(n2 + 1) * 128],
                        V.dbcb[0:96, lb * 512:(lb + 1) * 512], start=True, stop=True))
                PE_B = n_("pe")
                for lb in range(4):
                    op("tensor", "pe", lambda e, n2=n2, lb=lb: e.matmul(
                        p_b[0:128, lb * 512:(lb + 1) * 512],
                        T["t_selbc"][0:96, (N + n2) * 128:(N + n2 + 1) * 128],
                        V.dbcb[0:96, lb * 512:(lb + 1) * 512], start=True, stop=True))
                PE_C = n_("pe")
                w("scalar", "dve", SCAN_DVE[n2 - 2] if n2 >= 2 else DVE_W)
                op("scalar", "act", lambda e, t=t, n2=n2, dab=dab: e.activation(
                    dab, V.dl, ACTF.Exp,
                    scale=T["t_alog"][:, t * N + n2:t * N + n2 + 1]))
                ACT_DA = n_("act")
                w("vector", "pe", PE_B)
                op("vector", "dve", lambda e: e.tensor_tensor(
                    V.dbx, V.w, p_a[0:128, 0:2048], op=ALU.mult))
                w("vector", "act", ACT_DA)
                op("vector", "dve", lambda e, n2=n2, dab=dab: e.tensor_tensor_scan(
                    V.h, dab, V.dbx, T["t_carry"][:, n2:n2 + 1], op0=ALU.mult, op1=ALU.add))
                SCAN_DVE[n2] = n_("dve")
                op("vector", "dve", lambda e, n2=n2: e.tensor_copy(
                    T["t_hend"][:, n2:n2 + 1], V.h[:, LC - 1:LC]))
                w("vector", "pe", PE_C)
                op("vector", "dve", lambda e: e.tensor_tensor(
                    V.tmp1, V.h, p_b[0:128, 0:2048], op=ALU.mult))
                if n2 == 0:
                    op("vector", "dve", lambda e: e.tensor_copy(V.yacc, V.tmp1))
                else:
                    op("vector", "dve", lambda e: e.tensor_add(V.yacc, V.yacc, V.tmp1))
            op("vector", "dve", lambda e, t=t: e.scalar_tensor_tensor(
                V.yacc, V.xct, T["t_dp"][:, t:t + 1],
                V.yacc, op0=ALU.mult, op1=ALU.add))
            op("vector", "dve", lambda e, t=t: e.tensor_tensor(
                V.ygall[:, t * LC:(t + 1) * LC], V.yacc, V.gt, op=ALU.mult))
            YG_DVE = n_("dve")
            w("gpsimd", "dve", YG_DVE)
            dma("gpsimd", DR["agrs"][k][s][:], T["t_hend"][:, :])
            w("gpsimd", "gd")
            op("gpsimd", "ag", lambda e, k=k, s=s: e.collective_compute(
                "AllGather", ALU.bypass, ins=[DR["agrs"][k][s][:]], outs=[DR["agrd"][k][s][:]],
                replica_groups=RG8))
            w("gpsimd", "ag")
            for r_ in range(8):
                dma("gpsimd", V.m[:, 128 + r_ * N:128 + (r_ + 1) * N], DR["agrd"][k][s][r_])
            w("vector", "gd")
            op("vector", "dve", lambda e: e.tensor_tensor(
                V.m[:, 0:128].rearrange("p (n r) -> p n r", r=8),
                V.m[:, 128:256].rearrange("p (r n) -> p n r", n=N),
                T["t_selmask"][:, :].rearrange("p (n r) -> p n r", r=8), op=ALU.mult))
            op("vector", "dve", lambda e: e.tensor_reduce(
                T["t_carry"][:, :], V.m[:, 0:128].rearrange("p (n r) -> p n r", r=8),
                axis=AX.X, op=ALU.add))
            checkpoint("round%d_%d" % (k, s))

        W_YG = n_("dve")
        w("sync", "dve", W_YG)
        w("sync", "st")
        for t2 in range(ET):
            dma("sync", V.wo[:, t2 * D:(t2 + 1) * D], DR["wob"][t2])
        WO_LD = n_("ld")
        for lb in range(4):
            sl2 = slice(lb * 512, (lb + 1) * 512)
            for dtile in range(DT):
                pb = p_a if dtile < 4 else p_b
                po = (dtile % 4) * 512
                w("tensor", "ld", WO_LD)
                if lb > 0 and dtile == 0:
                    w("tensor", "dve")
                for t2 in range(ET):
                    op("tensor", "pe", lambda e, t2=t2, dtile=dtile, lb=lb, pb=pb, po=po: e.matmul(
                        pb[0:128, po:po + 512],
                        V.wo[:, t2 * D + dtile * 128: t2 * D + (dtile + 1) * 128],
                        V.ygall[:, t2 * LC + lb * 512: t2 * LC + (lb + 1) * 512],
                        start=(t2 == 0), stop=(t2 == ET - 1)))
            for dtile in range(DT):
                pb = p_a if dtile < 4 else p_b
                po = (dtile % 4) * 512
                w("sync", "dve")
                dma("sync", V.uw, DR["u"][dtile][:, sl2])
                w("vector", "ld")
                w("vector", "pe")
                w("vector", "st")
                op("vector", "dve", lambda e, pb=pb, po=po: e.scalar_tensor_tensor(
                    V.uo, V.uw, 2.0, pb[0:128, po:po + 512], op0=ALU.mult, op1=ALU.add))
                w("sync", "dve")
                dma_st(DR["u"][dtile][:, sl2], V.uo)
        checkpoint("iter%d" % k)

    emit_ln(u_t, x2_t, scaled=True, add_tiles=x_t)
    emit_ln(x2_t, u_t)
    checkpoint("epi")

    w("sync", "st")
    for lt in range(LT):
        for kt in range(DT):
            w("sync", "dve")
            dma("sync", V.v1[:, 0:128], DR["u"][kt][:, lt * 128:(lt + 1) * 128])
            w("tensor", "ld")
            w("tensor", "dve")
            op("tensor", "pe", lambda e: e.transpose(
                p_a[0:128, 0:128], V.v1[:, 0:128], T["t_identf"][:, :]))
            w("vector", "pe")
            op("vector", "dve", lambda e, kt=kt: e.tensor_copy(
                V.q0[:, kt * 128:(kt + 1) * 128], p_a[0:128, 0:128]))
        quant(V.q0[:, 0:D], D, V.qb[:, 0:D])
        for kt in range(DT):
            w("tensor", "dve")
            op("tensor", "pe", lambda e, kt=kt: e.transpose(
                p_a[0:128, 128:192].bitcast(BF16), V.qb[:, kt * 128:(kt + 1) * 128], T["t_ident"][:, :]))
            w("vector", "pe")
            w("vector", "st")
            op("vector", "dve", lambda e: e.tensor_copy(V.qt[:, 0:128], p_a[0:128, 128:192].bitcast(BF16)))
            w("sync", "dve")
            dma_st(DR["xqt"][kt][:, lt * 128:(lt + 1) * 128], V.qt[:, 0:128])

    checkpoint("qx")
    w("sync", "st")
    nib = (IF + 511) // 512
    for lt in range(LT):
        w("sync", "dve")
        for kt in range(DT):
            dma("sync", V.xq[:, kt * 128:(kt + 1) * 128], DR["xqt"][kt][:, lt * 128:(lt + 1) * 128])
        for which, wsrc in ((0, DR["wgq"]), (1, DR["wuq"])):
            for ib in range(nib):
                iw = min(512, IF - ib * 512)
                w("sync", "pe")
                for kt in range(DT):
                    dma("sync", V.wfs[:, kt * 512:kt * 512 + iw], wsrc[kt][:, ib * 512:ib * 512 + iw])
                w("tensor", "ld")
                if which == 0:
                    w("tensor", "act")
                else:
                    w("tensor", "dve")
                for kt in range(DT):
                    op("tensor", "pe", lambda e, kt=kt, iw=iw: e.matmul(
                        p_a[0:128, 0:iw],
                        V.xq[:, kt * 128:(kt + 1) * 128],
                        V.wfs[:, kt * 512:kt * 512 + iw],
                        start=(kt == 0), stop=(kt == DT - 1)))
                if which == 0:
                    w("scalar", "pe")
                    op("scalar", "act", lambda e, ib=ib, iw=iw: e.activation(
                        V.q2[:, ib * 512:ib * 512 + iw], p_a[0:128, 0:iw], ACTF.Silu))
                else:
                    w("vector", "pe")
                    w("vector", "act")
                    op("vector", "dve", lambda e, ib=ib, iw=iw: e.tensor_tensor(
                        V.h2[:, ib * 512:ib * 512 + iw], V.q2[:, ib * 512:ib * 512 + iw],
                        p_a[0:128, 0:iw], op=ALU.mult))
        quant(V.h2[:, 0:IF], IF, V.qb[:, 0:IF])
        for it in range(ITF):
            w("tensor", "dve")
            op("tensor", "pe", lambda e, it=it: e.transpose(
                p_a[0:128, 0:64].bitcast(BF16), V.qb[:, it * 128:(it + 1) * 128], T["t_ident"][:, :]))
            w("vector", "pe")
            op("vector", "dve", lambda e, it=it: e.tensor_copy(
                V.hqt[:, it * 128:(it + 1) * 128], p_a[0:128, 0:64].bitcast(BF16)))
        HQ_DVE = n_("dve")
        for db in range(2):
            w("sync", "pe")
            w("vector", "st")
            for it in range(ITF):
                dma("sync", V.wfs[:, it * 512:(it + 1) * 512], DR["wdq"][it][:, db * 512:(db + 1) * 512])
            w("tensor", "ld")
            w("tensor", "dve")
            for it in range(ITF):
                op("tensor", "pe", lambda e, it=it: e.matmul(
                    p_b[0:128, 0:512], V.hqt[:, it * 128:(it + 1) * 128],
                    V.wfs[:, it * 512:(it + 1) * 512],
                    start=(it == 0), stop=(it == ITF - 1)))
            for kt2 in range(4):
                kt = db * 4 + kt2
                w("sync", "pe")
                dma("sync", V.fv1[:, 0:128], DR["x2cm"][kt][:, lt * 128:(lt + 1) * 128])
                w("tensor", "ld")
                op("tensor", "pe", lambda e, kt2=kt2: e.transpose(
                    p_b[0:128, 1024 + kt2 * 128:1024 + (kt2 + 1) * 128],
                    V.fv1[:, 0:128], T["t_identf"][:, :]))
            w("vector", "pe")
            op("vector", "dve", lambda e: e.tensor_copy(V.q0[:, 0:512], p_b[0:128, 1024:1536]))
            op("vector", "dve", lambda e, db=db: e.tensor_add(
                V.o1[:, db * 512:(db + 1) * 512], p_b[0:128, 0:512], V.q0[:, 0:512]))
        w("sync", "dve")
        dma_st(DR["out"][lt * 128:(lt + 1) * 128, :], V.o1[:, 0:D])
    w("sync", "st")
    return cnt


def build_nc():
    nc = bass.Bass("TRN2", target_bir_lowering=False)
    DR = {}
    DR["x_cm"] = nc.dram_tensor("x_cm", [D, LC], F32, kind="ExternalInput").ap()
    DR["win"] = nc.dram_tensor("Win", [D, 2 * ED], F32, kind="ExternalInput").ap()
    DR["rmsw"] = nc.dram_tensor("rmsw", [128, DT], F32, kind="ExternalInput").ap()
    DR["convw"] = nc.dram_tensor("convw", [ED, KC], F32, kind="ExternalInput").ap()
    DR["convb"] = nc.dram_tensor("convb", [128, ET], F32, kind="ExternalInput").ap()
    DR["wx"] = nc.dram_tensor("Wx", [ED, 96], F32, kind="ExternalInput").ap()
    DR["wdt"] = nc.dram_tensor("Wdt", [R, ED], F32, kind="ExternalInput").ap()
    DR["bdt"] = nc.dram_tensor("bdt", [128, ET], F32, kind="ExternalInput").ap()
    DR["alog"] = nc.dram_tensor("Alog", [ED, N], F32, kind="ExternalInput").ap()
    DR["dp"] = nc.dram_tensor("Dp", [128, ET], F32, kind="ExternalInput").ap()
    DR["wout"] = nc.dram_tensor("Wout", [ED, D], F32, kind="ExternalInput").ap()
    DR["lnw"] = nc.dram_tensor("lnw", [128, DT], F32, kind="ExternalInput").ap()
    DR["lnb"] = nc.dram_tensor("lnb", [128, DT], F32, kind="ExternalInput").ap()
    DR["wg"] = nc.dram_tensor("wg", [IF, D], F32, kind="ExternalInput").ap()
    DR["wu"] = nc.dram_tensor("wu", [IF, D], F32, kind="ExternalInput").ap()
    DR["wd"] = nc.dram_tensor("wd", [D, IF], F32, kind="ExternalInput").ap()
    DR["selmask"] = nc.dram_tensor("selmask", [128, N * 8], F32, kind="ExternalInput").ap()
    DR["halom"] = nc.dram_tensor("halom", [128, HW * 8], F32, kind="ExternalInput").ap()
    DR["selbc"] = nc.dram_tensor("selbc", [32, 96, 128], BF16, kind="ExternalInput").ap()
    DR["ident"] = nc.dram_tensor("ident", [128, 128], BF16, kind="ExternalInput").ap()
    DR["identf"] = nc.dram_tensor("identf", [128, 128], F32, kind="ExternalInput").ap()
    DR["out"] = nc.dram_tensor("out", [LC, D], F32, kind="ExternalOutput").ap()

    DR["u"] = [nc.dram_tensor(f"u_scr{i}", [128, LC], F32, kind="Internal").ap() for i in range(DT)]
    DR["x2cm"] = [nc.dram_tensor(f"x2_scr{i}", [128, LC], F32, kind="Internal").ap() for i in range(DT)]
    DR["xc"] = [nc.dram_tensor(f"xc_scr{i}", [128, LC], F32, kind="Internal").ap() for i in range(ET)]
    DR["g"] = [nc.dram_tensor(f"g_scr{i}", [128, LC], F32, kind="Internal").ap() for i in range(ET)]
    DR["wib"] = [nc.dram_tensor(f"wib{i}", [128, 2 * ED], BF16, kind="Internal").ap() for i in range(DT)]
    DR["wob"] = [nc.dram_tensor(f"wob{i}", [128, D], BF16, kind="Internal").ap() for i in range(ET)]
    DR["wobl"] = [nc.dram_tensor(f"wobl{i}", [128, D], BF16, kind="Internal").ap() for i in range(ET)]
    DR["ylo"] = [nc.dram_tensor(f"ylo{i}", [128, LC], BF16, kind="Internal").ap() for i in range(ET)]
    DR["wgq"] = [nc.dram_tensor(f"wgq{i}", [128, IF], BF16, kind="Internal").ap() for i in range(DT)]
    DR["wuq"] = [nc.dram_tensor(f"wuq{i}", [128, IF], BF16, kind="Internal").ap() for i in range(DT)]
    DR["wdq"] = [nc.dram_tensor(f"wdq{i}", [128, D], BF16, kind="Internal").ap() for i in range(ITF)]
    DR["xqt"] = [nc.dram_tensor(f"xqt{i}", [128, LC], BF16, kind="Internal").ap() for i in range(DT)]
    DR["aghs"] = [nc.dram_tensor(f"aghs{k}", [D, HW], F32, kind="Internal").ap() for k in range(DEPTH)]
    DR["aghd"] = [nc.dram_tensor(f"aghd{k}", [8, D, HW], F32, kind="Internal").ap() for k in range(DEPTH)]
    DR["agrs"] = [[nc.dram_tensor(f"agrs{k}_{s}", [128, N], F32, kind="Internal").ap()
                   for s in range(SR)] for k in range(DEPTH)]
    DR["agrd"] = [[nc.dram_tensor(f"agrd{k}_{s}", [8, 128, N], F32, kind="Internal").ap()
                   for s in range(SR)] for k in range(DEPTH)]

    specs = [
        ("t_af", [128, 24576], F32), ("t_ab", [128, 32768], BF16),
        ("t_rmsw", [128, DT], F32), ("t_lnw", [128, DT], F32), ("t_lnb", [128, DT], F32),
        ("t_cw", [128, ET * KC], F32), ("t_cb", [128, ET], F32), ("t_bdt", [128, ET], F32),
        ("t_dp", [128, ET], F32), ("t_alog", [128, ET * N], F32),
        ("t_wx", [128, ET * 96], F32), ("t_wdt", [64, ET * 128], F32),
        ("t_selbc", [96, 32 * 128], BF16), ("t_ident", [128, 128], BF16),
        ("t_identf", [128, 128], F32),
        ("t_selmask", [128, N * 8], F32), ("t_halom", [128, HW * 8], F32),
        ("t_ones", [128, 1], F32), ("t_ones1", [1, 128], F32),
        ("t_sc1", [128, 256], F32),
        ("t_dbc", [96, LC], F32),
                ("t_hal", [128, DT * HW], F32), ("t_halnb", [128, DT * HW], BF16),
        ("t_tail", [128, ET * 4], F32),
        ("t_hend", [128, N], F32), ("t_carry", [128, N], F32),
    ]
    ctxs = []
    T = {}
    for (nm, sh, dt_) in specs:
        c = nc.sbuf_tensor(nm, sh, dt_)
        ctxs.append((nm, c))
    for (nm, sh) in [("p_a", [128, 2048]), ("p_b", [128, 2048])]:
        c = nc.psum_tensor(nm, sh, F32)
        ctxs.append((nm, c))
    semnames = ["ld", "st", "gd", "pe", "act", "dve", "gps", "ag"]
    for s_ in semnames:
        ctxs.append(("sem_" + s_, nc.semaphore("sem_" + s_)))
    entered = {}
    for nm, c in ctxs:
        entered[nm] = c.__enter__()
    T = {nm: entered[nm] for nm in entered if not nm.startswith("sem_")}
    S = {s_: entered["sem_" + s_] for s_ in semnames}
    T = {nm: (h.ap() if hasattr(h, "ap") else h) for nm, h in T.items()}

    counts = {}
    with nc.Block() as block:
        @block.sync
        def _(e):
            counts["sync"] = emit_program("sync", e, T, S, DR)

        @block.tensor
        def _(e):
            counts["tensor"] = emit_program("tensor", e, T, S, DR)

        @block.scalar
        def _(e):
            counts["scalar"] = emit_program("scalar", e, T, S, DR)

        @block.vector
        def _(e):
            counts["vector"] = emit_program("vector", e, T, S, DR)

        @block.gpsimd
        def _(e):
            counts["gpsimd"] = emit_program("gpsimd", e, T, S, DR)

    base = counts["sync"]
    for r_, c_ in counts.items():
        assert c_ == base, f"counter divergence in {r_}: {c_} vs {base}"
    for nm, c in reversed(ctxs):
        c.__exit__(None, None, None)
    return nc


def kernel(**inputs):
    bf = ml_dtypes.bfloat16
    x = np.asarray(inputs["L0_attn_x"], dtype=np.float32)[0]
    rmsw = np.asarray(inputs["L0_attn_rms_w"], np.float32)
    Win = np.asarray(inputs["L0_attn_W_in"], np.float32)
    convw = np.asarray(inputs["L0_attn_conv_w"], np.float32)
    convb = np.asarray(inputs["L0_attn_conv_b"], np.float32)
    Wx = np.asarray(inputs["L0_attn_W_x"], np.float32)
    Wdt = np.asarray(inputs["L0_attn_W_dt"], np.float32)
    bdt = np.asarray(inputs["L0_attn_b_dt"], np.float32)
    Alog = np.asarray(inputs["L0_attn_A_log"], np.float32)
    Dp = np.asarray(inputs["L0_attn_Dp"], np.float32)
    Wout = np.asarray(inputs["L0_attn_W_out"], np.float32)
    lnw = np.asarray(inputs["L0_attn_ln_w"], np.float32)
    lnb = np.asarray(inputs["L0_attn_ln_b"], np.float32)
    wg = np.asarray(inputs["L0_ffn_w_gate"], np.float32)
    wu = np.asarray(inputs["L0_ffn_w_up"], np.float32)
    wd = np.asarray(inputs["L0_ffn_w_down"], np.float32)

    ident = np.eye(128, dtype=bf)
    identf = np.eye(128, dtype=np.float32)
    selbc = np.zeros((32, 96, 128), dtype=bf)
    for n2 in range(N):
        selbc[n2, 64 + n2, :] = 1.0
        selbc[N + n2, 80 + n2, :] = 1.0

    in_maps = []
    for c in range(NCORE):
        perm = np.concatenate([np.arange(((t - c) % ET) * 128, ((t - c) % ET) * 128 + 128)
                               for t in range(ET)])
        selmask = np.zeros((128, N, 8), np.float32)
        halom = np.zeros((128, HW, 8), np.float32)
        if c > 0:
            selmask[:, :, c - 1] = 1.0
            halom[:, :, c - 1] = 1.0
        xs_rows = x[c * LC:(c + 1) * LC]
        m = {
            "x_cm": np.ascontiguousarray(xs_rows.T),
            "Win": np.ascontiguousarray(
                np.concatenate([Win[:, :ED][:, perm], Win[:, ED:][:, perm]], axis=1)),
            "rmsw": np.ascontiguousarray(rmsw.reshape(DT, 128).T),
            "convw": np.ascontiguousarray(convw[perm]),
            "convb": np.ascontiguousarray(convb[perm].reshape(ET, 128).T),
            "Wx": np.ascontiguousarray(Wx[perm]),
            "Wdt": np.ascontiguousarray(Wdt[:, perm]),
            "bdt": np.ascontiguousarray(bdt[perm].reshape(ET, 128).T),
            "Alog": np.ascontiguousarray(Alog[perm]),
            "Dp": np.ascontiguousarray(Dp[perm].reshape(ET, 128).T),
            "Wout": np.ascontiguousarray(Wout[perm]),
            "lnw": np.ascontiguousarray(lnw.reshape(DT, 128).T),
            "lnb": np.ascontiguousarray(lnb.reshape(DT, 128).T),
            "wg": wg, "wu": wu, "wd": wd,
            "selmask": selmask.reshape(128, N * 8),
            "halom": halom.reshape(128, HW * 8),
            "selbc": selbc, "ident": ident, "identf": identf,
        }
        in_maps.append(m)

    if "nc" not in _NC_CACHE:
        _NC_CACHE["nc"] = build_nc()
    nc = _NC_CACHE["nc"]
    res = run_bass_kernel_spmd(nc, in_maps, core_ids=list(range(NCORE)))
    out = np.concatenate([res.results[c]["out"] for c in range(NCORE)], axis=0)
    return out.reshape(1, L, D).astype(np.float32)


_NC_CACHE = {}


# version 20
# speedup vs baseline: 1.3342x; 1.0497x over previous
import numpy as np
import ml_dtypes
import concourse.bass as bass
import concourse.mybir as mybir
from concourse.bass_utils import run_bass_kernel_spmd

F32 = mybir.dt.float32
BF16 = mybir.dt.bfloat16
ALU = mybir.AluOpType
ACTF = mybir.ActivationFunctionType
AX = mybir.AxisListType

NCORE = 8
L, D, ED, N, R, KC = 16384, 1024, 2048, 16, 64, 4
DEPTH = 6
LC = L // NCORE
ET, DT = ED // 128, D // 128
SR = ET + NCORE - 1
EPS = 1e-5
IF = 2816
ITF = IF // 128
LT = LC // 128
HW = 64
BQ = 16
RG8 = [[0, 1, 2, 3, 4, 5, 6, 7]]
STOP_AFTER = None
DUMP_U = False
DUMP_SRC = "u"


class Views:
    pass


def mkviews(T):
    V = Views()
    af = T["t_af"]
    ab = T["t_ab"]

    def f(a, b):
        return af[:, a:b]

    V.xs = f(0, 8320)
    V.u = f(8320, 12416)
    V.xc1 = f(12416, 12928)
    V.xc2 = f(12928, 13440)
    V.gst = f(13440, 13952)
    V.tail = f(13952, 14976)
    V.un = f(14976, 17024).bitcast(BF16)
    V.unlo = f(17024, 19072).bitcast(BF16)
    V.unf = f(19072, 21120)
    V.dbcb = f(23168, 24192).bitcast(BF16)
    V.xct = f(17024, 19072)
    V.gt = f(19072, 21120)
    V.dl = f(0, 2048)
    V.w = f(2048, 4096)
    V.m = f(4096, 6144)
    V.tmp2 = f(6144, 8192)
    V.da = f(8192, 10240)
    V.dbx = f(10240, 12288)
    V.h = f(12288, 14336)
    V.yacc = f(14336, 16384)
    V.tmp1 = f(21120, 23168)
    V.row = f(23168, 24192)
    V.v1 = f(16384, 16896)
    V.v2 = f(0, 512)
    V.v3 = f(512, 1024)
    V.halg = f(0, 4096)
    V.halt = f(4096, 8192)
    V.halxs = f(21120, 22144)
    V.wo = f(0, 8192).bitcast(BF16)
    V.wo2 = f(8192, 16384).bitcast(BF16)
    V.uw = f(17408, 17920)
    V.uo = f(17920, 18432)
    V.ylos = f(16384, 17408).bitcast(BF16)
    V.ygf = f(21120, 23168)
    V.ylo_st = f(5120, 6144).bitcast(BF16)
    V.q0 = f(0, 2816)
    V.q1 = f(2816, 5632)
    V.q2 = f(5632, 8448)
    V.q3 = f(8448, 11264)
    V.h2 = f(11264, 14080)
    V.qs = f(14080, 14464)
    V.qb = f(14464, 15872).bitcast(BF16)
    V.qt = f(15872, 15936).bitcast(BF16)
    V.hqt = f(15936, 17344).bitcast(BF16)
    V.xq = f(17344, 17856).bitcast(BF16)
    V.o1 = f(17856, 18880)
    V.wfs = f(18880, 24512).bitcast(BF16)
    V.fv1 = f(24448, 24576)
    V.big1 = f(0, 4096)
    V.bigb = f(4096, 6144).bitcast(BF16)
    V.win = ab[:, :]
    V.ygall = ab[:, :]
    return V


def emit_program(role, E, T, S, DR):
    V = mkviews(T)
    cnt = {}
    stop = {"hit": False}

    def n_(sem):
        return cnt.get(sem, 0)

    def w(r, sem, val=None):
        if role == r:
            E.wait_ge(S[sem], n_(sem) if val is None else val)

    def op(r, sem, fn, amt=1):
        inst = fn(E) if role == r else None
        cnt[sem] = cnt.get(sem, 0) + amt
        if inst is not None:
            inst.then_inc(S[sem], amt)

    def dma(r, out_ap, in_ap):
        sem = {"sync": "ld", "gpsimd": "gd"}[r]
        op(r, sem, lambda e: e.dma_start(out=out_ap, in_=in_ap), 16)

    def dma_st(out_ap, in_ap):
        op("sync", "st", lambda e: e.dma_start(out=out_ap, in_=in_ap), 16)

    class StopEmit(Exception):
        pass

    def checkpoint(name):
        if STOP_AFTER is not None and name == STOP_AFTER:
            raise StopEmit()

    def finish_dummy():
        if DUMP_U:
            w("sync", "st")
            w("sync", "dve")
            w("sync", "act")
            w("sync", "pe")
            w("sync", "gd")
            if DUMP_SRC in ("u", "x2"):
                srcb = DR["u"] if DUMP_SRC == "u" else DR["x2cm"]
                for kt in range(DT):
                    for j in range(2):
                        dma_st(DR["out"][(kt + j * DT) * 128:(kt + j * DT + 1) * 128, :],
                               srcb[kt][:, j * 1024:(j + 1) * 1024])
            else:
                srcb = DR["xc"] if DUMP_SRC == "xc" else DR["g"]
                for et in range(8):
                    for j in range(2):
                        dma_st(DR["out"][(et * 2 + j) * 128:(et * 2 + j + 1) * 128, :],
                               srcb[et][:, j * 1024:(j + 1) * 1024])
        else:
            op("gpsimd", "gps", lambda e: e.memset(T["t_sc1"][:, :], 0.0))
            w("sync", "gps")
            for lt in range(LT):
                for j in range(2):
                    dma_st(DR["out"][lt * 128:(lt + 1) * 128, j * 512:(j + 1) * 512],
                           T["t_sc1"][:, 0:512])
        w("sync", "st")

    p_a, p_b = T["p_a"], T["p_b"]

    try:
        return _emit_body(role, E, T, S, DR, V, cnt, n_, w, op, dma, dma_st,
                          checkpoint, StopEmit, finish_dummy, p_a, p_b)
    except StopEmit:
        finish_dummy()
        return cnt


def _noop():
    pass


def _emit_body(role, E, T, S, DR, V, cnt, n_, w, op, dma, dma_st, checkpoint,
               StopEmit, finish_dummy, p_a, p_b):
    dma("sync", T["t_rmsw"][:, :], DR["rmsw"][:])
    dma("sync", T["t_lnw"][:, :], DR["lnw"][:])
    dma("sync", T["t_lnb"][:, :], DR["lnb"][:])
    dma("sync", T["t_cw"][:, :].rearrange("p (e k) -> p e k", k=KC),
        DR["convw"].rearrange("(e p) k -> p e k", p=128))
    dma("sync", T["t_cb"][:, :], DR["convb"][:])
    dma("sync", T["t_bdt"][:, :], DR["bdt"][:])
    dma("sync", T["t_dp"][:, :], DR["dp"][:])
    dma("sync", T["t_alog"][:, :].rearrange("p (e n) -> p e n", n=N),
        DR["alog"].rearrange("(e p) n -> p e n", p=128))
    dma("sync", T["t_wx"][:, :].rearrange("p (e k) -> p e k", k=96),
        DR["wx"].rearrange("(e p) k -> p e k", p=128))
    dma("sync", T["t_wdt"][:, :].rearrange("p (e k) -> p e k", k=128),
        DR["wdt"].rearrange("r (e k) -> r e k", k=128))
    dma("sync", T["t_selbc"][:, :].rearrange("p (s k) -> p s k", k=128),
        DR["selbc"].transpose([1, 0, 2]))
    dma("sync", T["t_ident"][:, :], DR["ident"][:])
    dma("sync", T["t_identf"][:, :], DR["identf"][:])
    dma("sync", T["t_selmask"][:, :], DR["selmask"][:])
    dma("sync", T["t_halom"][:, :], DR["halom"][:])
    LDC = n_("ld")
    op("gpsimd", "gps", lambda e: e.memset(T["t_ones"][:, :], 1.0))
    op("gpsimd", "gps", lambda e: e.memset(T["t_ones1"][0:1, :], 1.0))

    w("scalar", "ld", LDC)
    for e_ in range(ET):
        op("scalar", "act", lambda eng, e_=e_: eng.activation(
            T["t_sc1"][:, e_ * N:(e_ + 1) * N], T["t_alog"][:, e_ * N:(e_ + 1) * N], ACTF.Exp))
    w("vector", "act")
    for e_ in range(ET):
        op("vector", "dve", lambda eng, e_=e_: eng.tensor_scalar_mul(
            T["t_alog"][:, e_ * N:(e_ + 1) * N], T["t_sc1"][:, e_ * N:(e_ + 1) * N], -1.0))

    w("sync", "ld", LDC)
    for kt in range(DT):
        w("sync", "dve")
        dma("sync", V.big1, DR["win"][kt * 128:(kt + 1) * 128, :])
        w("vector", "ld")
        w("vector", "st")
        op("vector", "dve", lambda e, kt=kt: e.tensor_scalar_mul(
            V.bigb, V.big1, T["t_rmsw"][:, kt:kt + 1]))
        w("sync", "dve")
        dma_st(DR["wib"][kt], V.bigb)
    for et in range(ET):
        w("sync", "dve")
        dma("sync", V.big1[:, 0:D], DR["wout"][et * 128:(et + 1) * 128, :])
        w("vector", "ld")
        w("vector", "st")
        op("vector", "dve", lambda e, et=et: e.tensor_copy(V.bigb[:, 0:D], V.big1[:, 0:D]))
        op("vector", "dve", lambda e, et=et: e.tensor_sub(
            V.bigb[:, D:2 * D], V.big1[:, 0:D], V.bigb[:, 0:D]))
        w("sync", "dve")
        dma_st(DR["wob"][et], V.bigb[:, 0:D])
        dma_st(DR["wobl"][et], V.bigb[:, D:2 * D])

    checkpoint("wconv")
    def quant(src_ap, width, outb_ap):
        nb = width // BQ
        op("vector", "dve", lambda e: e.tensor_reduce(
            V.qs[:, 0:nb], src_ap.rearrange("p (a b) -> p a b", b=BQ),
            axis=AX.X, op=ALU.max, apply_absolute_value=True))
        op("vector", "dve", lambda e: e.tensor_scalar(
            V.qs[:, 0:nb], V.qs[:, 0:nb], 1.0 / 6.0, 1e-12, op0=ALU.mult, op1=ALU.max))
        op("vector", "dve", lambda e: e.reciprocal(V.qs[:, nb:2 * nb], V.qs[:, 0:nb]))
        op("vector", "dve", lambda e: e.tensor_tensor(
            V.q2[:, 0:width].rearrange("p (a b) -> p a b", b=BQ),
            src_ap.rearrange("p (a b) -> p a b", b=BQ),
            V.qs[:, nb:2 * nb].unsqueeze(2).broadcast_to([128, nb, BQ]), op=ALU.mult))
        w("scalar", "dve")
        op("scalar", "act", lambda e: e.activation(V.q3[:, 0:width], V.q2[:, 0:width], ACTF.Abs))
        w("vector", "act")
        op("vector", "dve", lambda e: e.tensor_single_scalar(
            V.q1[:, 0:width], V.q3[:, 0:width], 0.25, op=ALU.is_gt))
        for m in (0.75, 1.25, 1.75):
            op("vector", "dve", lambda e, m=m: e.scalar_tensor_tensor(
                V.q1[:, 0:width], V.q3[:, 0:width], m, V.q1[:, 0:width],
                op0=ALU.is_gt, op1=ALU.add))
        op("vector", "dve", lambda e: e.tensor_scalar_mul(V.q1[:, 0:width], V.q1[:, 0:width], 0.5))
        for m in (2.5, 3.5):
            op("vector", "dve", lambda e, m=m: e.scalar_tensor_tensor(
                V.q1[:, 0:width], V.q3[:, 0:width], m, V.q1[:, 0:width],
                op0=ALU.is_gt, op1=ALU.add))
        op("vector", "dve", lambda e: e.tensor_single_scalar(
            V.q2[:, 0:width], V.q3[:, 0:width], 5.0, op=ALU.is_gt))
        op("vector", "dve", lambda e: e.scalar_tensor_tensor(
            V.q1[:, 0:width], V.q2[:, 0:width], 2.0, V.q1[:, 0:width],
            op0=ALU.mult, op1=ALU.add))
        op("vector", "dve", lambda e: e.tensor_single_scalar(
            V.q2[:, 0:width], src_ap, 0.0, op=ALU.is_lt))
        op("vector", "dve", lambda e: e.tensor_scalar(
            V.q2[:, 0:width], V.q2[:, 0:width], -2.0, 1.0, op0=ALU.mult, op1=ALU.add))
        op("vector", "dve", lambda e: e.tensor_tensor(
            V.q1[:, 0:width], V.q1[:, 0:width], V.q2[:, 0:width], op=ALU.mult))
        op("vector", "dve", lambda e: e.tensor_tensor(
            outb_ap.rearrange("p (a b) -> p a b", b=BQ),
            V.q1[:, 0:width].rearrange("p (a b) -> p a b", b=BQ),
            V.qs[:, 0:nb].unsqueeze(2).broadcast_to([128, nb, BQ]), op=ALU.mult))

    for (src_d, dst_d, ntile, wid) in ((DR["wg"], DR["wgq"], ITF, D),
                                       (DR["wu"], DR["wuq"], ITF, D),
                                       (DR["wd"], DR["wdq"], DT, IF)):
        for ti in range(ntile):
            w("sync", "dve")
            dma("sync", V.q0[:, 0:wid], src_d[ti * 128:(ti + 1) * 128, 0:wid])
            w("vector", "ld")
            w("scalar", "ld")
            quant(V.q0[:, 0:wid], wid, V.qb[:, 0:wid])
            for wt in range(wid // 128):
                w("tensor", "dve")
                op("tensor", "pe", lambda e, wt=wt: e.transpose(
                    p_a[0:128, 0:64].bitcast(BF16), V.qb[:, wt * 128:(wt + 1) * 128], T["t_ident"][:, :]))
                w("vector", "pe")
                w("vector", "st")
                op("vector", "dve", lambda e: e.tensor_copy(V.qt[:, 0:128], p_a[0:128, 0:64].bitcast(BF16)))
                w("sync", "dve")
                dma_st(dst_d[wt][:, ti * 128:(ti + 1) * 128], V.qt[:, 0:128])

    checkpoint("ffnq")
    def emit_ln(src_tiles, dst_tiles, scaled=False, add_tiles=None):
        w("sync", "st")
        for scn in range(LC // 512):
            sl = slice(scn * 512, (scn + 1) * 512)
            for kt in range(DT):
                w("sync", "dve")
                dma("sync", V.u[:, kt * 512:(kt + 1) * 512], src_tiles[kt][:, sl])
            LDU = n_("ld")
            w("scalar", "ld", LDU)
            w("tensor", "ld", LDU)
            w("tensor", "dve")
            for kt in range(DT):
                w("scalar", "pe")
                op("scalar", "act", lambda e, kt=kt: e.activation(
                    V.v1, V.u[:, kt * 512:(kt + 1) * 512], ACTF.Square))
                w("tensor", "act")
                op("tensor", "pe", lambda e, kt=kt: e.matmul(
                    p_b[0:1, 0:512], T["t_ones"][:, 0:1], V.u[:, kt * 512:(kt + 1) * 512],
                    start=(kt == 0), stop=(kt == DT - 1)))
                op("tensor", "pe", lambda e, kt=kt: e.matmul(
                    p_b[0:1, 512:1024], T["t_ones"][:, 0:1], V.v1,
                    start=(kt == 0), stop=(kt == DT - 1)))
            rw = V.row
            w("vector", "pe")
            op("vector", "dve", lambda e: e.tensor_scalar_mul(rw[0:1, 0:512], p_b[0:1, 0:512], 1.0 / D))
            op("vector", "dve", lambda e: e.tensor_scalar_mul(rw[0:1, 512:1024], p_b[0:1, 512:1024], 1.0 / D))
            op("vector", "dve", lambda e: e.tensor_tensor(
                V.v2[0:1, 0:512], rw[0:1, 0:512], rw[0:1, 0:512], op=ALU.mult))
            op("vector", "dve", lambda e: e.tensor_sub(
                rw[0:1, 512:1024], rw[0:1, 512:1024], V.v2[0:1, 0:512]))
            op("vector", "dve", lambda e: e.tensor_scalar_add(rw[0:1, 512:1024], rw[0:1, 512:1024], EPS))
            op("vector", "dve", lambda e: e.reciprocal(V.v2[0:1, 0:512], rw[0:1, 512:1024]))
            w("scalar", "dve")
            op("scalar", "act", lambda e: e.activation(V.v1[0:1, 0:512], V.v2[0:1, 0:512], ACTF.Sqrt))
            w("vector", "act")
            op("vector", "dve", lambda e: e.tensor_tensor(
                V.v3[0:1, 0:512], V.v1[0:1, 0:512], V.v1[0:1, 0:512], op=ALU.mult))
            op("vector", "dve", lambda e: e.tensor_tensor(
                V.v3[0:1, 0:512], V.v3[0:1, 0:512], rw[0:1, 512:1024], op=ALU.mult))
            op("vector", "dve", lambda e: e.tensor_scalar(
                V.v3[0:1, 0:512], V.v3[0:1, 0:512], -0.5, 1.5, op0=ALU.mult, op1=ALU.add))
            op("vector", "dve", lambda e: e.tensor_tensor(
                rw[0:1, 512:1024], V.v1[0:1, 0:512], V.v3[0:1, 0:512], op=ALU.mult))
            w("tensor", "dve")
            w("tensor", "act")
            op("tensor", "pe", lambda e: e.matmul(
                p_b[0:128, 1024:1536], T["t_ones1"][0:1, 0:128], rw[0:1, 0:512], start=True, stop=True))
            op("tensor", "pe", lambda e: e.matmul(
                p_b[0:128, 1536:2048], T["t_ones1"][0:1, 0:128], rw[0:1, 512:1024], start=True, stop=True))
            w("vector", "pe")
            for kt in range(DT):
                w("vector", "st")
                op("vector", "dve", lambda e, kt=kt: e.tensor_sub(
                    V.v2, V.u[:, kt * 512:(kt + 1) * 512], p_b[0:128, 1024:1536]))
                op("vector", "dve", lambda e, kt=kt: e.tensor_tensor(
                    V.v3, V.v2, p_b[0:128, 1536:2048], op=ALU.mult))
                if scaled:
                    op("vector", "dve", lambda e, kt=kt: e.tensor_scalar(
                        V.v3, V.v3, T["t_lnw"][:, kt:kt + 1], T["t_lnb"][:, kt:kt + 1],
                        op0=ALU.mult, op1=ALU.add))
                if add_tiles is not None:
                    w("sync", "dve")
                    dma("sync", V.xc1, add_tiles[kt][:, sl])
                    w("vector", "ld")
                    op("vector", "dve", lambda e: e.tensor_add(V.v3, V.v3, V.xc1))
                w("sync", "dve")
                dma_st(dst_tiles[kt][:, sl], V.v3)

    u_t = [DR["u"][kt] for kt in range(DT)]
    x_t = [DR["x_cm"][kt * 128:(kt + 1) * 128, :] for kt in range(DT)]
    x2_t = [DR["x2cm"][kt] for kt in range(DT)]
    emit_ln(x_t, u_t)

    checkpoint("l1")
    for k in range(DEPTH):
        w("gpsimd", "st")
        for kt in range(DT):
            dma("gpsimd", DR["aghs"][k][kt * 128:(kt + 1) * 128, :], DR["u"][kt][:, LC - HW:LC])
        w("gpsimd", "gd")
        op("gpsimd", "ag", lambda e, k=k: e.collective_compute(
            "AllGather", ALU.bypass, ins=[DR["aghs"][k][:]], outs=[DR["aghd"][k][:]],
            replica_groups=RG8))
        w("gpsimd", "ag")
        w("gpsimd", "dve")
        for r_ in range(8):
            dma("gpsimd",
                V.halg.rearrange("p (a r h) -> p a r h", r=8, h=HW)[:, :, r_, :],
                DR["aghd"][k][r_].rearrange("(a p) h -> p a h", p=128))
        w("vector", "gd")
        checkpoint("haloA%d" % k)
        op("vector", "dve", lambda e: e.tensor_tensor(
            V.halt.rearrange("p (a h r) -> p a h r", h=HW, r=8),
            V.halg.rearrange("p (a r h) -> p a r h", r=8, h=HW).transpose([0, 1, 3, 2]),
            T["t_halom"][:, :].rearrange("p (h r) -> p h r", r=8).unsqueeze(1).broadcast_to([128, DT, HW, 8]),
            op=ALU.mult))
        op("vector", "dve", lambda e: e.tensor_reduce(
            T["t_hal"][:, :].rearrange("p (a h) -> p a h", h=HW),
            V.halt.rearrange("p (a h r) -> p a h r", h=HW, r=8),
            axis=AX.X, op=ALU.add))
        HAL_DVE = n_("dve")
        checkpoint("haloB%d" % k)
        w("scalar", "dve", HAL_DVE)
        w("tensor", "dve", HAL_DVE)
        for kt in range(DT):
            op("scalar", "act", lambda e, kt=kt: e.activation(
                T["t_sc1"][:, 0:HW], T["t_hal"][:, kt * HW:(kt + 1) * HW], ACTF.Square))
            w("tensor", "act")
            op("tensor", "pe", lambda e, kt=kt: e.matmul(
                p_b[0:1, 0:HW], T["t_ones"][:, 0:1], T["t_sc1"][:, 0:HW],
                start=(kt == 0), stop=(kt == DT - 1)))
        w("vector", "pe")
        op("vector", "dve", lambda e: e.tensor_scalar(
            V.row[0:1, 0:HW], p_b[0:1, 0:HW], 1.0 / D, EPS, op0=ALU.mult, op1=ALU.add))
        op("vector", "dve", lambda e: e.reciprocal(V.row[0:1, HW:2 * HW], V.row[0:1, 0:HW]))
        w("scalar", "dve")
        op("scalar", "act", lambda e: e.activation(V.xc1[0:1, 0:HW], V.row[0:1, HW:2 * HW], ACTF.Sqrt))
        w("vector", "act")
        op("vector", "dve", lambda e: e.tensor_tensor(
            V.xc2[0:1, 0:HW], V.xc1[0:1, 0:HW], V.xc1[0:1, 0:HW], op=ALU.mult))
        op("vector", "dve", lambda e: e.tensor_tensor(
            V.xc2[0:1, 0:HW], V.xc2[0:1, 0:HW], V.row[0:1, 0:HW], op=ALU.mult))
        op("vector", "dve", lambda e: e.tensor_scalar(
            V.xc2[0:1, 0:HW], V.xc2[0:1, 0:HW], -0.5, 1.5, op0=ALU.mult, op1=ALU.add))
        op("vector", "dve", lambda e: e.tensor_tensor(
            V.row[0:1, 0:HW], V.xc1[0:1, 0:HW], V.xc2[0:1, 0:HW], op=ALU.mult))
        w("tensor", "dve")
        w("tensor", "act")
        op("tensor", "pe", lambda e: e.matmul(
            p_b[0:128, 512:512 + HW], T["t_ones1"][0:1, 0:128], V.row[0:1, 0:HW], start=True, stop=True))
        w("vector", "pe")
        for kt in range(DT):
            op("vector", "dve", lambda e, kt=kt: e.tensor_tensor(
                T["t_halnb"][:, kt * HW:(kt + 1) * HW], T["t_hal"][:, kt * HW:(kt + 1) * HW],
                p_b[0:128, 512:512 + HW], op=ALU.mult))
        HALN_DVE = n_("dve")
        checkpoint("haloC%d" % k)
        w("sync", "pe")
        for kt in range(DT):
            dma("sync", V.win[:, kt * 4096:(kt + 1) * 4096], DR["wib"][kt])
        LD_WIN = n_("ld")
        w("tensor", "dve", HALN_DVE)
        w("tensor", "ld", LD_WIN)
        for et in range(ET):
            pst = (et % 2) * 512
            if et >= 2:
                w("tensor", "dve")
            for kt in range(DT):
                op("tensor", "pe", lambda e, et=et, kt=kt, pst=pst: e.matmul(
                    p_a[0:128, pst:pst + HW],
                    V.win[:, kt * 4096 + et * 128: kt * 4096 + (et + 1) * 128],
                    T["t_halnb"][:, kt * HW:(kt + 1) * HW],
                    start=(kt == 0), stop=(kt == DT - 1)))
            w("vector", "pe")
            op("vector", "dve", lambda e, et=et, pst=pst: e.tensor_copy(
                V.halxs[:, et * HW:(et + 1) * HW], p_a[0:128, pst:pst + HW]))
        HALXS_DVE = n_("dve")
        checkpoint("halo%d" % k)

        w("sync", "st")
        for scn in range(LC // 512):
            sl = slice(scn * 512, (scn + 1) * 512)
            for kt in range(DT):
                w("sync", "dve")
                dma("sync", V.u[:, kt * 512:(kt + 1) * 512], DR["u"][kt][:, sl])
            LDU = n_("ld")
            w("scalar", "ld", LDU)
            w("tensor", "ld", LDU)
            w("tensor", "dve")
            for kt in range(DT):
                op("scalar", "act", lambda e, kt=kt: e.activation(
                    V.v1, V.u[:, kt * 512:(kt + 1) * 512], ACTF.Square))
                w("tensor", "act")
                op("tensor", "pe", lambda e, kt=kt: e.matmul(
                    p_b[0:1, 0:512], T["t_ones"][:, 0:1], V.v1,
                    start=(kt == 0), stop=(kt == DT - 1)))
            w("vector", "pe")
            op("vector", "dve", lambda e: e.tensor_scalar(
                V.row[0:1, 0:512], p_b[0:1, 0:512], 1.0 / D, EPS, op0=ALU.mult, op1=ALU.add))
            op("vector", "dve", lambda e: e.reciprocal(V.row[0:1, 512:1024], V.row[0:1, 0:512]))
            w("scalar", "dve")
            op("scalar", "act", lambda e: e.activation(V.xc1[0:1, 0:512], V.row[0:1, 512:1024], ACTF.Sqrt))
            w("vector", "act")
            op("vector", "dve", lambda e: e.tensor_tensor(
                V.xc2[0:1, 0:512], V.xc1[0:1, 0:512], V.xc1[0:1, 0:512], op=ALU.mult))
            op("vector", "dve", lambda e: e.tensor_tensor(
                V.xc2[0:1, 0:512], V.xc2[0:1, 0:512], V.row[0:1, 0:512], op=ALU.mult))
            op("vector", "dve", lambda e: e.tensor_scalar(
                V.xc2[0:1, 0:512], V.xc2[0:1, 0:512], -0.5, 1.5, op0=ALU.mult, op1=ALU.add))
            op("vector", "dve", lambda e: e.tensor_tensor(
                V.row[0:1, 0:512], V.xc1[0:1, 0:512], V.xc2[0:1, 0:512], op=ALU.mult))
            w("tensor", "dve")
            op("tensor", "pe", lambda e: e.matmul(
                p_b[0:128, 512:1024], T["t_ones1"][0:1, 0:128], V.row[0:1, 0:512], start=True, stop=True))
            w("vector", "pe")
            for kt in range(DT):
                op("vector", "dve", lambda e, kt=kt: e.tensor_tensor(
                    V.un[:, kt * 512:(kt + 1) * 512], V.u[:, kt * 512:(kt + 1) * 512],
                    p_b[0:128, 512:1024], op=ALU.mult))
            UN_DVE = n_("dve")
            w("tensor", "ld", LD_WIN)
            w("tensor", "dve", UN_DVE)
            for jt in range(2 * ET):
                pslot = jt % 2
                pap = p_a[0:128, pslot * 512:(pslot + 1) * 512]
                if jt >= 2:
                    if pslot == 0:
                        w("tensor", "dve")
                    else:
                        w("tensor", "act")
                for kt in range(DT):
                    op("tensor", "pe", lambda e, jt=jt, kt=kt, pap=pap: e.matmul(
                        pap, V.win[:, kt * 4096 + jt * 128: kt * 4096 + (jt + 1) * 128],
                        V.un[:, kt * 512:(kt + 1) * 512],
                        start=(kt == 0), stop=(kt == DT - 1)))
                if jt < ET:
                    et = jt
                    w("vector", "pe")
                    op("vector", "dve", lambda e, et=et, pap=pap: e.tensor_copy(
                        V.xs[:, et * 520 + KC - 1: et * 520 + KC - 1 + 512], pap))
                else:
                    et = jt - ET
                    w("scalar", "pe")
                    w("scalar", "st")
                    op("scalar", "act", lambda e, pap=pap: e.activation(V.gst, pap, ACTF.Silu))
                    w("sync", "act")
                    dma_st(DR["g"][et][:, sl], V.gst)
            for et in range(ET):
                base = et * 520
                if scn == 0:
                    op("vector", "dve", lambda e, et=et, base=base: e.tensor_copy(
                        V.xs[:, base:base + KC - 1],
                        V.halxs[:, et * HW + HW - (KC - 1): (et + 1) * HW]))
                op("vector", "dve", lambda e, et=et, base=base: e.tensor_scalar_mul(
                    V.xc1, V.xs[:, base:base + 512], T["t_cw"][:, et * KC:et * KC + 1]))
                for kk in range(1, KC):
                    op("vector", "dve", lambda e, et=et, base=base, kk=kk: e.scalar_tensor_tensor(
                        V.xc1, V.xs[:, base + kk:base + kk + 512],
                        T["t_cw"][:, et * KC + kk:et * KC + kk + 1], V.xc1,
                        op0=ALU.mult, op1=ALU.add))
                w("scalar", "dve")
                w("scalar", "st")
                w("scalar", "pe")
                op("scalar", "act", lambda e, et=et: e.activation(
                    V.xc2, V.xc1, ACTF.Silu, bias=T["t_cb"][:, et:et + 1]))
                w("sync", "act")
                dma_st(DR["xc"][et][:, sl], V.xc2)
                w("tensor", "act")
                op("tensor", "pe", lambda e, et=et: e.matmul(
                    p_b[0:96, 1024:1536], T["t_wx"][:, et * 96:(et + 1) * 96], V.xc2,
                    start=(et == 0), stop=(et == ET - 1)))
                op("vector", "dve", lambda e, et=et, base=base: e.tensor_copy(
                    T["t_tail"][:, et * 4:et * 4 + KC - 1],
                    V.xs[:, base + 512: base + 512 + KC - 1]))
            w("vector", "pe")
            op("vector", "dve", lambda e, sl=sl: e.tensor_copy(T["t_dbc"][0:96, sl], p_b[0:96, 1024:1536]))
            if scn < LC // 512 - 1:
                for et in range(ET):
                    op("vector", "dve", lambda e, et=et: e.tensor_copy(
                        V.xs[:, et * 520:et * 520 + KC - 1],
                        T["t_tail"][:, et * 4:et * 4 + KC - 1]))
        op("vector", "dve", lambda e: e.tensor_copy(V.dbcb[0:96, 0:2048], T["t_dbc"][0:96, :]))
        DBC_DVE = n_("dve")
        XCG_ST = n_("st")
        checkpoint("pass1_%d" % k)

        for s in range(SR):
            t = s % ET
            w("sync", "st", XCG_ST)
            w("sync", "dve")
            dma("sync", V.xct, DR["xc"][t])
            dma("sync", V.gt, DR["g"][t])
            LD_SLOT = n_("ld")
            w("tensor", "dve")
            w("tensor", "act")
            for lb in range(4):
                op("tensor", "pe", lambda e, t=t, lb=lb: e.matmul(
                    p_a[0:128, lb * 512:(lb + 1) * 512],
                    T["t_wdt"][:, t * 128:(t + 1) * 128],
                    T["t_dbc"][0:64, lb * 512:(lb + 1) * 512], start=True, stop=True))
            PE_DEL = n_("pe")
            w("scalar", "pe", PE_DEL)
            for lb in range(4):
                sl2 = slice(lb * 512, (lb + 1) * 512)
                op("scalar", "act", lambda e, t=t, sl2=sl2: e.activation(
                    V.m[:, sl2], p_a[0:128, sl2], ACTF.Relu, bias=T["t_bdt"][:, t:t + 1]))
                op("scalar", "act", lambda e, t=t, sl2=sl2: e.activation(
                    V.tmp2[:, sl2], p_a[0:128, sl2], ACTF.Abs, bias=T["t_bdt"][:, t:t + 1]))
                op("scalar", "act", lambda e, sl2=sl2: e.activation(
                    V.tmp2[:, sl2], V.tmp2[:, sl2], ACTF.Exp, scale=-1.0))
                op("scalar", "act", lambda e, sl2=sl2: e.activation(
                    V.tmp2[:, sl2], V.tmp2[:, sl2], ACTF.Ln, bias=1.0))
            ACT_SP = n_("act")
            w("vector", "act", ACT_SP)
            op("vector", "dve", lambda e: e.tensor_add(V.dl, V.m, V.tmp2))
            w("vector", "ld", LD_SLOT)
            op("vector", "dve", lambda e: e.tensor_tensor(
                V.w, V.dl, V.xct, op=ALU.mult))
            DVE_W = n_("dve")
            SCAN_DVE = {}
            for n2 in range(N):
                dab = V.da if n2 % 2 == 0 else V.tmp2
                if n2 == 0:
                    w("tensor", "act", ACT_SP)
                    w("tensor", "dve", DVE_W)
                else:
                    w("tensor", "dve")
                for lb in range(4):
                    op("tensor", "pe", lambda e, n2=n2, lb=lb: e.matmul(
                        p_a[0:128, lb * 512:(lb + 1) * 512],
                        T["t_selbc"][0:96, n2 * 128:(n2 + 1) * 128],
                        V.dbcb[0:96, lb * 512:(lb + 1) * 512], start=True, stop=True))
                PE_B = n_("pe")
                for lb in range(4):
                    op("tensor", "pe", lambda e, n2=n2, lb=lb: e.matmul(
                        p_b[0:128, lb * 512:(lb + 1) * 512],
                        T["t_selbc"][0:96, (N + n2) * 128:(N + n2 + 1) * 128],
                        V.dbcb[0:96, lb * 512:(lb + 1) * 512], start=True, stop=True))
                PE_C = n_("pe")
                w("scalar", "dve", SCAN_DVE[n2 - 2] if n2 >= 2 else DVE_W)
                op("scalar", "act", lambda e, t=t, n2=n2, dab=dab: e.activation(
                    dab, V.dl, ACTF.Exp,
                    scale=T["t_alog"][:, t * N + n2:t * N + n2 + 1]))
                ACT_DA = n_("act")
                w("vector", "pe", PE_B)
                op("vector", "dve", lambda e: e.tensor_tensor(
                    V.dbx, V.w, p_a[0:128, 0:2048], op=ALU.mult))
                if n2 == 0:
                    w("vector", "gd")
                    op("vector", "dve", lambda e: e.tensor_tensor(
                        V.v1[:, 0:128].rearrange("p (n r) -> p n r", r=8),
                        V.v1[:, 128:256].rearrange("p (r n) -> p n r", n=N),
                        T["t_selmask"][:, :].rearrange("p (n r) -> p n r", r=8), op=ALU.mult))
                    op("vector", "dve", lambda e: e.tensor_reduce(
                        T["t_carry"][:, :], V.v1[:, 0:128].rearrange("p (n r) -> p n r", r=8),
                        axis=AX.X, op=ALU.add))
                w("vector", "act", ACT_DA)
                op("vector", "dve", lambda e, n2=n2, dab=dab: e.tensor_tensor_scan(
                    V.h, dab, V.dbx, T["t_carry"][:, n2:n2 + 1], op0=ALU.mult, op1=ALU.add))
                SCAN_DVE[n2] = n_("dve")
                op("vector", "dve", lambda e, n2=n2: e.tensor_copy(
                    T["t_hend"][:, n2:n2 + 1], V.h[:, LC - 1:LC]))
                HEND_DVE = n_("dve")
                w("vector", "pe", PE_C)
                op("vector", "dve", lambda e: e.tensor_tensor(
                    V.tmp1, V.h, p_b[0:128, 0:2048], op=ALU.mult))
                if n2 == 0:
                    op("vector", "dve", lambda e: e.tensor_copy(V.yacc, V.tmp1))
                else:
                    op("vector", "dve", lambda e: e.tensor_add(V.yacc, V.yacc, V.tmp1))
            op("vector", "dve", lambda e, t=t: e.scalar_tensor_tensor(
                V.yacc, V.xct, T["t_dp"][:, t:t + 1],
                V.yacc, op0=ALU.mult, op1=ALU.add))
            if s < SR - 1:
                w("gpsimd", "dve", HEND_DVE)
                dma("gpsimd", DR["agrs"][k][s][:], T["t_hend"][:, :])
                w("gpsimd", "gd")
                op("gpsimd", "ag", lambda e, k=k, s=s: e.collective_compute(
                    "AllGather", ALU.bypass, ins=[DR["agrs"][k][s][:]], outs=[DR["agrd"][k][s][:]],
                    replica_groups=RG8))
                w("gpsimd", "ag")
                w("gpsimd", "dve")
                for r_ in range(8):
                    dma("gpsimd", V.v1[:, 128 + r_ * N:128 + (r_ + 1) * N], DR["agrd"][k][s][r_])
            op("vector", "dve", lambda e, t=t: e.tensor_tensor(
                V.ygall[:, t * LC:(t + 1) * LC], V.yacc, V.gt, op=ALU.mult))
            YG_DVE = n_("dve")
            checkpoint("round%d_%d" % (k, s))

        W_YG = n_("dve")
        w("sync", "dve", W_YG)
        w("sync", "st")
        for t2 in range(ET):
            dma("sync", V.wo[:, t2 * D:(t2 + 1) * D], DR["wob"][t2])
        WO_LD = n_("ld")
        for lb in range(4):
            sl2 = slice(lb * 512, (lb + 1) * 512)
            for dtile in range(DT):
                pb = p_a if dtile < 4 else p_b
                po = (dtile % 4) * 512
                w("tensor", "ld", WO_LD)
                if lb > 0 and dtile == 0:
                    w("tensor", "dve")
                for t2 in range(ET):
                    op("tensor", "pe", lambda e, t2=t2, dtile=dtile, lb=lb, pb=pb, po=po: e.matmul(
                        pb[0:128, po:po + 512],
                        V.wo[:, t2 * D + dtile * 128: t2 * D + (dtile + 1) * 128],
                        V.ygall[:, t2 * LC + lb * 512: t2 * LC + (lb + 1) * 512],
                        start=(t2 == 0), stop=(t2 == ET - 1)))
            for dtile in range(DT):
                pb = p_a if dtile < 4 else p_b
                po = (dtile % 4) * 512
                w("sync", "dve")
                dma("sync", V.uw, DR["u"][dtile][:, sl2])
                w("vector", "ld")
                w("vector", "pe")
                w("vector", "st")
                op("vector", "dve", lambda e, pb=pb, po=po: e.scalar_tensor_tensor(
                    V.uo, V.uw, 2.0, pb[0:128, po:po + 512], op0=ALU.mult, op1=ALU.add))
                w("sync", "dve")
                dma_st(DR["u"][dtile][:, sl2], V.uo)
        checkpoint("iter%d" % k)

    emit_ln(u_t, x2_t, scaled=True, add_tiles=x_t)
    emit_ln(x2_t, u_t)
    checkpoint("epi")

    w("sync", "st")
    for lt in range(LT):
        for kt in range(DT):
            w("sync", "dve")
            dma("sync", V.v1[:, 0:128], DR["u"][kt][:, lt * 128:(lt + 1) * 128])
            w("tensor", "ld")
            w("tensor", "dve")
            op("tensor", "pe", lambda e: e.transpose(
                p_a[0:128, 0:128], V.v1[:, 0:128], T["t_identf"][:, :]))
            w("vector", "pe")
            op("vector", "dve", lambda e, kt=kt: e.tensor_copy(
                V.q0[:, kt * 128:(kt + 1) * 128], p_a[0:128, 0:128]))
        quant(V.q0[:, 0:D], D, V.qb[:, 0:D])
        for kt in range(DT):
            w("tensor", "dve")
            op("tensor", "pe", lambda e, kt=kt: e.transpose(
                p_a[0:128, 128:192].bitcast(BF16), V.qb[:, kt * 128:(kt + 1) * 128], T["t_ident"][:, :]))
            w("vector", "pe")
            w("vector", "st")
            op("vector", "dve", lambda e: e.tensor_copy(V.qt[:, 0:128], p_a[0:128, 128:192].bitcast(BF16)))
            w("sync", "dve")
            dma_st(DR["xqt"][kt][:, lt * 128:(lt + 1) * 128], V.qt[:, 0:128])

    checkpoint("qx")
    w("sync", "st")
    nib = (IF + 511) // 512
    for lt in range(LT):
        w("sync", "dve")
        for kt in range(DT):
            dma("sync", V.xq[:, kt * 128:(kt + 1) * 128], DR["xqt"][kt][:, lt * 128:(lt + 1) * 128])
        for which, wsrc in ((0, DR["wgq"]), (1, DR["wuq"])):
            for ib in range(nib):
                iw = min(512, IF - ib * 512)
                w("sync", "pe")
                for kt in range(DT):
                    dma("sync", V.wfs[:, kt * 512:kt * 512 + iw], wsrc[kt][:, ib * 512:ib * 512 + iw])
                w("tensor", "ld")
                if which == 0:
                    w("tensor", "act")
                else:
                    w("tensor", "dve")
                for kt in range(DT):
                    op("tensor", "pe", lambda e, kt=kt, iw=iw: e.matmul(
                        p_a[0:128, 0:iw],
                        V.xq[:, kt * 128:(kt + 1) * 128],
                        V.wfs[:, kt * 512:kt * 512 + iw],
                        start=(kt == 0), stop=(kt == DT - 1)))
                if which == 0:
                    w("scalar", "pe")
                    op("scalar", "act", lambda e, ib=ib, iw=iw: e.activation(
                        V.q2[:, ib * 512:ib * 512 + iw], p_a[0:128, 0:iw], ACTF.Silu))
                else:
                    w("vector", "pe")
                    w("vector", "act")
                    op("vector", "dve", lambda e, ib=ib, iw=iw: e.tensor_tensor(
                        V.h2[:, ib * 512:ib * 512 + iw], V.q2[:, ib * 512:ib * 512 + iw],
                        p_a[0:128, 0:iw], op=ALU.mult))
        quant(V.h2[:, 0:IF], IF, V.qb[:, 0:IF])
        for it in range(ITF):
            w("tensor", "dve")
            op("tensor", "pe", lambda e, it=it: e.transpose(
                p_a[0:128, 0:64].bitcast(BF16), V.qb[:, it * 128:(it + 1) * 128], T["t_ident"][:, :]))
            w("vector", "pe")
            op("vector", "dve", lambda e, it=it: e.tensor_copy(
                V.hqt[:, it * 128:(it + 1) * 128], p_a[0:128, 0:64].bitcast(BF16)))
        HQ_DVE = n_("dve")
        for db in range(2):
            w("sync", "pe")
            w("vector", "st")
            for it in range(ITF):
                dma("sync", V.wfs[:, it * 512:(it + 1) * 512], DR["wdq"][it][:, db * 512:(db + 1) * 512])
            w("tensor", "ld")
            w("tensor", "dve")
            for it in range(ITF):
                op("tensor", "pe", lambda e, it=it: e.matmul(
                    p_b[0:128, 0:512], V.hqt[:, it * 128:(it + 1) * 128],
                    V.wfs[:, it * 512:(it + 1) * 512],
                    start=(it == 0), stop=(it == ITF - 1)))
            for kt2 in range(4):
                kt = db * 4 + kt2
                w("sync", "pe")
                dma("sync", V.fv1[:, 0:128], DR["x2cm"][kt][:, lt * 128:(lt + 1) * 128])
                w("tensor", "ld")
                op("tensor", "pe", lambda e, kt2=kt2: e.transpose(
                    p_b[0:128, 1024 + kt2 * 128:1024 + (kt2 + 1) * 128],
                    V.fv1[:, 0:128], T["t_identf"][:, :]))
            w("vector", "pe")
            op("vector", "dve", lambda e: e.tensor_copy(V.q0[:, 0:512], p_b[0:128, 1024:1536]))
            op("vector", "dve", lambda e, db=db: e.tensor_add(
                V.o1[:, db * 512:(db + 1) * 512], p_b[0:128, 0:512], V.q0[:, 0:512]))
        w("sync", "dve")
        dma_st(DR["out"][lt * 128:(lt + 1) * 128, :], V.o1[:, 0:D])
    w("sync", "st")
    return cnt


def build_nc():
    nc = bass.Bass("TRN2", target_bir_lowering=False)
    DR = {}
    DR["x_cm"] = nc.dram_tensor("x_cm", [D, LC], F32, kind="ExternalInput").ap()
    DR["win"] = nc.dram_tensor("Win", [D, 2 * ED], F32, kind="ExternalInput").ap()
    DR["rmsw"] = nc.dram_tensor("rmsw", [128, DT], F32, kind="ExternalInput").ap()
    DR["convw"] = nc.dram_tensor("convw", [ED, KC], F32, kind="ExternalInput").ap()
    DR["convb"] = nc.dram_tensor("convb", [128, ET], F32, kind="ExternalInput").ap()
    DR["wx"] = nc.dram_tensor("Wx", [ED, 96], F32, kind="ExternalInput").ap()
    DR["wdt"] = nc.dram_tensor("Wdt", [R, ED], F32, kind="ExternalInput").ap()
    DR["bdt"] = nc.dram_tensor("bdt", [128, ET], F32, kind="ExternalInput").ap()
    DR["alog"] = nc.dram_tensor("Alog", [ED, N], F32, kind="ExternalInput").ap()
    DR["dp"] = nc.dram_tensor("Dp", [128, ET], F32, kind="ExternalInput").ap()
    DR["wout"] = nc.dram_tensor("Wout", [ED, D], F32, kind="ExternalInput").ap()
    DR["lnw"] = nc.dram_tensor("lnw", [128, DT], F32, kind="ExternalInput").ap()
    DR["lnb"] = nc.dram_tensor("lnb", [128, DT], F32, kind="ExternalInput").ap()
    DR["wg"] = nc.dram_tensor("wg", [IF, D], F32, kind="ExternalInput").ap()
    DR["wu"] = nc.dram_tensor("wu", [IF, D], F32, kind="ExternalInput").ap()
    DR["wd"] = nc.dram_tensor("wd", [D, IF], F32, kind="ExternalInput").ap()
    DR["selmask"] = nc.dram_tensor("selmask", [128, N * 8], F32, kind="ExternalInput").ap()
    DR["halom"] = nc.dram_tensor("halom", [128, HW * 8], F32, kind="ExternalInput").ap()
    DR["selbc"] = nc.dram_tensor("selbc", [32, 96, 128], BF16, kind="ExternalInput").ap()
    DR["ident"] = nc.dram_tensor("ident", [128, 128], BF16, kind="ExternalInput").ap()
    DR["identf"] = nc.dram_tensor("identf", [128, 128], F32, kind="ExternalInput").ap()
    DR["out"] = nc.dram_tensor("out", [LC, D], F32, kind="ExternalOutput").ap()

    DR["u"] = [nc.dram_tensor(f"u_scr{i}", [128, LC], F32, kind="Internal").ap() for i in range(DT)]
    DR["x2cm"] = [nc.dram_tensor(f"x2_scr{i}", [128, LC], F32, kind="Internal").ap() for i in range(DT)]
    DR["xc"] = [nc.dram_tensor(f"xc_scr{i}", [128, LC], F32, kind="Internal").ap() for i in range(ET)]
    DR["g"] = [nc.dram_tensor(f"g_scr{i}", [128, LC], F32, kind="Internal").ap() for i in range(ET)]
    DR["wib"] = [nc.dram_tensor(f"wib{i}", [128, 2 * ED], BF16, kind="Internal").ap() for i in range(DT)]
    DR["wob"] = [nc.dram_tensor(f"wob{i}", [128, D], BF16, kind="Internal").ap() for i in range(ET)]
    DR["wobl"] = [nc.dram_tensor(f"wobl{i}", [128, D], BF16, kind="Internal").ap() for i in range(ET)]
    DR["ylo"] = [nc.dram_tensor(f"ylo{i}", [128, LC], BF16, kind="Internal").ap() for i in range(ET)]
    DR["wgq"] = [nc.dram_tensor(f"wgq{i}", [128, IF], BF16, kind="Internal").ap() for i in range(DT)]
    DR["wuq"] = [nc.dram_tensor(f"wuq{i}", [128, IF], BF16, kind="Internal").ap() for i in range(DT)]
    DR["wdq"] = [nc.dram_tensor(f"wdq{i}", [128, D], BF16, kind="Internal").ap() for i in range(ITF)]
    DR["xqt"] = [nc.dram_tensor(f"xqt{i}", [128, LC], BF16, kind="Internal").ap() for i in range(DT)]
    DR["aghs"] = [nc.dram_tensor(f"aghs{k}", [D, HW], F32, kind="Internal").ap() for k in range(DEPTH)]
    DR["aghd"] = [nc.dram_tensor(f"aghd{k}", [8, D, HW], F32, kind="Internal").ap() for k in range(DEPTH)]
    DR["agrs"] = [[nc.dram_tensor(f"agrs{k}_{s}", [128, N], F32, kind="Internal").ap()
                   for s in range(SR)] for k in range(DEPTH)]
    DR["agrd"] = [[nc.dram_tensor(f"agrd{k}_{s}", [8, 128, N], F32, kind="Internal").ap()
                   for s in range(SR)] for k in range(DEPTH)]

    specs = [
        ("t_af", [128, 24576], F32), ("t_ab", [128, 32768], BF16),
        ("t_rmsw", [128, DT], F32), ("t_lnw", [128, DT], F32), ("t_lnb", [128, DT], F32),
        ("t_cw", [128, ET * KC], F32), ("t_cb", [128, ET], F32), ("t_bdt", [128, ET], F32),
        ("t_dp", [128, ET], F32), ("t_alog", [128, ET * N], F32),
        ("t_wx", [128, ET * 96], F32), ("t_wdt", [64, ET * 128], F32),
        ("t_selbc", [96, 32 * 128], BF16), ("t_ident", [128, 128], BF16),
        ("t_identf", [128, 128], F32),
        ("t_selmask", [128, N * 8], F32), ("t_halom", [128, HW * 8], F32),
        ("t_ones", [128, 1], F32), ("t_ones1", [1, 128], F32),
        ("t_sc1", [128, 256], F32),
        ("t_dbc", [96, LC], F32),
                ("t_hal", [128, DT * HW], F32), ("t_halnb", [128, DT * HW], BF16),
        ("t_tail", [128, ET * 4], F32),
        ("t_hend", [128, N], F32), ("t_carry", [128, N], F32),
    ]
    ctxs = []
    T = {}
    for (nm, sh, dt_) in specs:
        c = nc.sbuf_tensor(nm, sh, dt_)
        ctxs.append((nm, c))
    for (nm, sh) in [("p_a", [128, 2048]), ("p_b", [128, 2048])]:
        c = nc.psum_tensor(nm, sh, F32)
        ctxs.append((nm, c))
    semnames = ["ld", "st", "gd", "pe", "act", "dve", "gps", "ag"]
    for s_ in semnames:
        ctxs.append(("sem_" + s_, nc.semaphore("sem_" + s_)))
    entered = {}
    for nm, c in ctxs:
        entered[nm] = c.__enter__()
    T = {nm: entered[nm] for nm in entered if not nm.startswith("sem_")}
    S = {s_: entered["sem_" + s_] for s_ in semnames}
    T = {nm: (h.ap() if hasattr(h, "ap") else h) for nm, h in T.items()}

    counts = {}
    with nc.Block() as block:
        @block.sync
        def _(e):
            counts["sync"] = emit_program("sync", e, T, S, DR)

        @block.tensor
        def _(e):
            counts["tensor"] = emit_program("tensor", e, T, S, DR)

        @block.scalar
        def _(e):
            counts["scalar"] = emit_program("scalar", e, T, S, DR)

        @block.vector
        def _(e):
            counts["vector"] = emit_program("vector", e, T, S, DR)

        @block.gpsimd
        def _(e):
            counts["gpsimd"] = emit_program("gpsimd", e, T, S, DR)

    base = counts["sync"]
    for r_, c_ in counts.items():
        assert c_ == base, f"counter divergence in {r_}: {c_} vs {base}"
    for nm, c in reversed(ctxs):
        c.__exit__(None, None, None)
    return nc


def kernel(**inputs):
    bf = ml_dtypes.bfloat16
    x = np.asarray(inputs["L0_attn_x"], dtype=np.float32)[0]
    rmsw = np.asarray(inputs["L0_attn_rms_w"], np.float32)
    Win = np.asarray(inputs["L0_attn_W_in"], np.float32)
    convw = np.asarray(inputs["L0_attn_conv_w"], np.float32)
    convb = np.asarray(inputs["L0_attn_conv_b"], np.float32)
    Wx = np.asarray(inputs["L0_attn_W_x"], np.float32)
    Wdt = np.asarray(inputs["L0_attn_W_dt"], np.float32)
    bdt = np.asarray(inputs["L0_attn_b_dt"], np.float32)
    Alog = np.asarray(inputs["L0_attn_A_log"], np.float32)
    Dp = np.asarray(inputs["L0_attn_Dp"], np.float32)
    Wout = np.asarray(inputs["L0_attn_W_out"], np.float32)
    lnw = np.asarray(inputs["L0_attn_ln_w"], np.float32)
    lnb = np.asarray(inputs["L0_attn_ln_b"], np.float32)
    wg = np.asarray(inputs["L0_ffn_w_gate"], np.float32)
    wu = np.asarray(inputs["L0_ffn_w_up"], np.float32)
    wd = np.asarray(inputs["L0_ffn_w_down"], np.float32)

    ident = np.eye(128, dtype=bf)
    identf = np.eye(128, dtype=np.float32)
    selbc = np.zeros((32, 96, 128), dtype=bf)
    for n2 in range(N):
        selbc[n2, 64 + n2, :] = 1.0
        selbc[N + n2, 80 + n2, :] = 1.0

    in_maps = []
    for c in range(NCORE):
        perm = np.concatenate([np.arange(((t - c) % ET) * 128, ((t - c) % ET) * 128 + 128)
                               for t in range(ET)])
        selmask = np.zeros((128, N, 8), np.float32)
        halom = np.zeros((128, HW, 8), np.float32)
        if c > 0:
            selmask[:, :, c - 1] = 1.0
            halom[:, :, c - 1] = 1.0
        xs_rows = x[c * LC:(c + 1) * LC]
        m = {
            "x_cm": np.ascontiguousarray(xs_rows.T),
            "Win": np.ascontiguousarray(
                np.concatenate([Win[:, :ED][:, perm], Win[:, ED:][:, perm]], axis=1)),
            "rmsw": np.ascontiguousarray(rmsw.reshape(DT, 128).T),
            "convw": np.ascontiguousarray(convw[perm]),
            "convb": np.ascontiguousarray(convb[perm].reshape(ET, 128).T),
            "Wx": np.ascontiguousarray(Wx[perm]),
            "Wdt": np.ascontiguousarray(Wdt[:, perm]),
            "bdt": np.ascontiguousarray(bdt[perm].reshape(ET, 128).T),
            "Alog": np.ascontiguousarray(Alog[perm]),
            "Dp": np.ascontiguousarray(Dp[perm].reshape(ET, 128).T),
            "Wout": np.ascontiguousarray(Wout[perm]),
            "lnw": np.ascontiguousarray(lnw.reshape(DT, 128).T),
            "lnb": np.ascontiguousarray(lnb.reshape(DT, 128).T),
            "wg": wg, "wu": wu, "wd": wd,
            "selmask": selmask.reshape(128, N * 8),
            "halom": halom.reshape(128, HW * 8),
            "selbc": selbc, "ident": ident, "identf": identf,
        }
        in_maps.append(m)

    if "nc" not in _NC_CACHE:
        _NC_CACHE["nc"] = build_nc()
    nc = _NC_CACHE["nc"]
    res = run_bass_kernel_spmd(nc, in_maps, core_ids=list(range(NCORE)))
    out = np.concatenate([res.results[c]["out"] for c in range(NCORE)], axis=0)
    return out.reshape(1, L, D).astype(np.float32)


_NC_CACHE = {}


# version 21 (final)
# speedup vs baseline: 1.4324x; 1.0736x over previous
import numpy as np
import ml_dtypes
import concourse.bass as bass
import concourse.mybir as mybir
from concourse.bass_utils import run_bass_kernel_spmd

F32 = mybir.dt.float32
BF16 = mybir.dt.bfloat16
ALU = mybir.AluOpType
ACTF = mybir.ActivationFunctionType
AX = mybir.AxisListType

NCORE = 8
L, D, ED, N, R, KC = 16384, 1024, 2048, 16, 64, 4
DEPTH = 6
LC = L // NCORE
ET, DT = ED // 128, D // 128
SR = ET + NCORE - 1
EPS = 1e-5
IF = 2816
ITF = IF // 128
LT = LC // 128
HW = 64
BQ = 16
RG8 = [[0, 1, 2, 3, 4, 5, 6, 7]]
STOP_AFTER = None
DUMP_U = False
DUMP_SRC = "u"


class Views:
    pass


def mkviews(T):
    V = Views()
    af = T["t_af"]
    ab = T["t_ab"]

    def f(a, b):
        return af[:, a:b]

    V.xs = f(0, 8320)
    V.u = f(8320, 12416)
    V.xc1 = f(12416, 12928)
    V.xc2 = f(12928, 13440)
    V.gst = f(13440, 13952)
    V.tail = f(13952, 14976)
    V.un = f(14976, 17024).bitcast(BF16)
    V.unlo = f(17024, 19072).bitcast(BF16)
    V.unf = f(19072, 21120)
    V.dbcb = f(23168, 24192).bitcast(BF16)
    V.xct = f(17024, 19072)
    V.gt = f(19072, 21120)
    V.dl = f(0, 2048)
    V.w = f(2048, 4096)
    V.m = f(4096, 6144)
    V.tmp2 = f(6144, 8192)
    V.da = f(8192, 10240)
    V.dbx = f(10240, 12288)
    V.h = f(12288, 14336)
    V.yacc = f(14336, 16384)
    V.tmp1 = f(21120, 23168)
    V.row = f(23168, 24192)
    V.v1 = f(16384, 16896)
    V.v2 = f(0, 512)
    V.v3 = f(512, 1024)
    V.halg = f(0, 4096)
    V.halt = f(4096, 8192)
    V.halxs = f(21120, 22144)
    V.wo = f(0, 8192).bitcast(BF16)
    V.wo2 = f(8192, 16384).bitcast(BF16)
    V.uw = f(17408, 17920)
    V.uo = f(17920, 18432)
    V.ylos = f(16384, 17408).bitcast(BF16)
    V.ygf = f(21120, 23168)
    V.ylo_st = f(5120, 6144).bitcast(BF16)
    V.q0 = f(0, 2816)
    V.q1 = f(2816, 5632)
    V.q2 = f(5632, 8448)
    V.q3 = f(8448, 11264)
    V.h2 = f(11264, 14080)
    V.qs = f(14080, 14464)
    V.qb = f(14464, 15872).bitcast(BF16)
    V.qt = f(15872, 15936).bitcast(BF16)
    V.hqt = f(15936, 17344).bitcast(BF16)
    V.xq = f(17344, 17856).bitcast(BF16)
    V.o1 = f(17856, 18880)
    V.wfs = f(18880, 24512).bitcast(BF16)
    V.fv1 = f(24448, 24576)
    V.big1 = f(0, 4096)
    V.bigb = f(4096, 6144).bitcast(BF16)
    V.win = ab[:, :]
    V.ygall = ab[:, :]
    return V


def emit_program(role, E, T, S, DR):
    V = mkviews(T)
    cnt = {}
    stop = {"hit": False}

    def n_(sem):
        return cnt.get(sem, 0)

    def w(r, sem, val=None):
        if role == r:
            E.wait_ge(S[sem], n_(sem) if val is None else val)

    def op(r, sem, fn, amt=1):
        inst = fn(E) if role == r else None
        cnt[sem] = cnt.get(sem, 0) + amt
        if inst is not None:
            inst.then_inc(S[sem], amt)

    def dma(r, out_ap, in_ap):
        sem = {"sync": "ld", "gpsimd": "gd"}[r]
        op(r, sem, lambda e: e.dma_start(out=out_ap, in_=in_ap), 16)

    def dma_st(out_ap, in_ap):
        op("sync", "st", lambda e: e.dma_start(out=out_ap, in_=in_ap), 16)

    class StopEmit(Exception):
        pass

    def checkpoint(name):
        if STOP_AFTER is not None and name == STOP_AFTER:
            raise StopEmit()

    def finish_dummy():
        if DUMP_U:
            w("sync", "st")
            w("sync", "dve")
            w("sync", "act")
            w("sync", "pe")
            w("sync", "gd")
            if DUMP_SRC in ("u", "x2"):
                srcb = DR["u"] if DUMP_SRC == "u" else DR["x2cm"]
                for kt in range(DT):
                    for j in range(2):
                        dma_st(DR["out"][(kt + j * DT) * 128:(kt + j * DT + 1) * 128, :],
                               srcb[kt][:, j * 1024:(j + 1) * 1024])
            else:
                srcb = DR["xc"] if DUMP_SRC == "xc" else DR["g"]
                for et in range(8):
                    for j in range(2):
                        dma_st(DR["out"][(et * 2 + j) * 128:(et * 2 + j + 1) * 128, :],
                               srcb[et][:, j * 1024:(j + 1) * 1024])
        else:
            op("gpsimd", "gps", lambda e: e.memset(T["t_sc1"][:, :], 0.0))
            w("sync", "gps")
            for lt in range(LT):
                for j in range(2):
                    dma_st(DR["out"][lt * 128:(lt + 1) * 128, j * 512:(j + 1) * 512],
                           T["t_sc1"][:, 0:512])
        w("sync", "st")

    p_a, p_b = T["p_a"], T["p_b"]

    try:
        return _emit_body(role, E, T, S, DR, V, cnt, n_, w, op, dma, dma_st,
                          checkpoint, StopEmit, finish_dummy, p_a, p_b)
    except StopEmit:
        finish_dummy()
        return cnt


def _noop():
    pass


def _emit_body(role, E, T, S, DR, V, cnt, n_, w, op, dma, dma_st, checkpoint,
               StopEmit, finish_dummy, p_a, p_b):
    dma("sync", T["t_rmsw"][:, :], DR["rmsw"][:])
    dma("sync", T["t_lnw"][:, :], DR["lnw"][:])
    dma("sync", T["t_lnb"][:, :], DR["lnb"][:])
    dma("sync", T["t_cw"][:, :].rearrange("p (e k) -> p e k", k=KC),
        DR["convw"].rearrange("(e p) k -> p e k", p=128))
    dma("sync", T["t_cb"][:, :], DR["convb"][:])
    dma("sync", T["t_bdt"][:, :], DR["bdt"][:])
    dma("sync", T["t_dp"][:, :], DR["dp"][:])
    dma("sync", T["t_alog"][:, :].rearrange("p (e n) -> p e n", n=N),
        DR["alog"].rearrange("(e p) n -> p e n", p=128))
    dma("sync", T["t_wx"][:, :].rearrange("p (e k) -> p e k", k=96),
        DR["wx"].rearrange("(e p) k -> p e k", p=128))
    dma("sync", T["t_wdt"][:, :].rearrange("p (e k) -> p e k", k=128),
        DR["wdt"].rearrange("r (e k) -> r e k", k=128))
    dma("sync", T["t_selbc"][:, :].rearrange("p (s k) -> p s k", k=128),
        DR["selbc"].transpose([1, 0, 2]))
    dma("sync", T["t_ident"][:, :], DR["ident"][:])
    dma("sync", T["t_identf"][:, :], DR["identf"][:])
    dma("sync", T["t_selmask"][:, :], DR["selmask"][:])
    dma("sync", T["t_halom"][:, :], DR["halom"][:])
    LDC = n_("ld")
    op("gpsimd", "gps", lambda e: e.memset(T["t_ones"][:, :], 1.0))
    op("gpsimd", "gps", lambda e: e.memset(T["t_ones1"][0:1, :], 1.0))

    w("scalar", "ld", LDC)
    for e_ in range(ET):
        op("scalar", "act", lambda eng, e_=e_: eng.activation(
            T["t_sc1"][:, e_ * N:(e_ + 1) * N], T["t_alog"][:, e_ * N:(e_ + 1) * N], ACTF.Exp))
    w("vector", "act")
    for e_ in range(ET):
        op("vector", "dve", lambda eng, e_=e_: eng.tensor_scalar_mul(
            T["t_alog"][:, e_ * N:(e_ + 1) * N], T["t_sc1"][:, e_ * N:(e_ + 1) * N], -1.0))

    w("sync", "ld", LDC)
    for kt in range(DT):
        w("sync", "dve")
        dma("sync", V.big1, DR["win"][kt * 128:(kt + 1) * 128, :])
        w("vector", "ld")
        w("vector", "st")
        op("vector", "dve", lambda e, kt=kt: e.tensor_scalar_mul(
            V.bigb, V.big1, T["t_rmsw"][:, kt:kt + 1]))
        w("sync", "dve")
        dma_st(DR["wib"][kt], V.bigb)
    for et in range(ET):
        w("sync", "dve")
        dma("sync", V.big1[:, 0:D], DR["wout"][et * 128:(et + 1) * 128, :])
        w("vector", "ld")
        w("vector", "st")
        op("vector", "dve", lambda e, et=et: e.tensor_copy(V.bigb[:, 0:D], V.big1[:, 0:D]))
        op("vector", "dve", lambda e, et=et: e.tensor_sub(
            V.bigb[:, D:2 * D], V.big1[:, 0:D], V.bigb[:, 0:D]))
        w("sync", "dve")
        dma_st(DR["wob"][et], V.bigb[:, 0:D])
        dma_st(DR["wobl"][et], V.bigb[:, D:2 * D])

    checkpoint("wconv")
    def quant(src_ap, width, outb_ap):
        nb = width // BQ
        op("vector", "dve", lambda e: e.tensor_reduce(
            V.qs[:, 0:nb], src_ap.rearrange("p (a b) -> p a b", b=BQ),
            axis=AX.X, op=ALU.max, apply_absolute_value=True))
        op("vector", "dve", lambda e: e.tensor_scalar(
            V.qs[:, 0:nb], V.qs[:, 0:nb], 1.0 / 6.0, 1e-12, op0=ALU.mult, op1=ALU.max))
        op("vector", "dve", lambda e: e.reciprocal(V.qs[:, nb:2 * nb], V.qs[:, 0:nb]))
        op("vector", "dve", lambda e: e.tensor_tensor(
            V.q2[:, 0:width].rearrange("p (a b) -> p a b", b=BQ),
            src_ap.rearrange("p (a b) -> p a b", b=BQ),
            V.qs[:, nb:2 * nb].unsqueeze(2).broadcast_to([128, nb, BQ]), op=ALU.mult))
        w("scalar", "dve")
        op("scalar", "act", lambda e: e.activation(V.q3[:, 0:width], V.q2[:, 0:width], ACTF.Abs))
        w("vector", "act")
        op("vector", "dve", lambda e: e.tensor_single_scalar(
            V.q1[:, 0:width], V.q3[:, 0:width], 0.25, op=ALU.is_gt))
        for m in (0.75, 1.25, 1.75):
            op("vector", "dve", lambda e, m=m: e.scalar_tensor_tensor(
                V.q1[:, 0:width], V.q3[:, 0:width], m, V.q1[:, 0:width],
                op0=ALU.is_gt, op1=ALU.add))
        op("vector", "dve", lambda e: e.tensor_scalar_mul(V.q1[:, 0:width], V.q1[:, 0:width], 0.5))
        for m in (2.5, 3.5):
            op("vector", "dve", lambda e, m=m: e.scalar_tensor_tensor(
                V.q1[:, 0:width], V.q3[:, 0:width], m, V.q1[:, 0:width],
                op0=ALU.is_gt, op1=ALU.add))
        op("vector", "dve", lambda e: e.tensor_single_scalar(
            V.q2[:, 0:width], V.q3[:, 0:width], 5.0, op=ALU.is_gt))
        op("vector", "dve", lambda e: e.scalar_tensor_tensor(
            V.q1[:, 0:width], V.q2[:, 0:width], 2.0, V.q1[:, 0:width],
            op0=ALU.mult, op1=ALU.add))
        op("vector", "dve", lambda e: e.tensor_single_scalar(
            V.q2[:, 0:width], src_ap, 0.0, op=ALU.is_lt))
        op("vector", "dve", lambda e: e.tensor_scalar(
            V.q2[:, 0:width], V.q2[:, 0:width], -2.0, 1.0, op0=ALU.mult, op1=ALU.add))
        op("vector", "dve", lambda e: e.tensor_tensor(
            V.q1[:, 0:width], V.q1[:, 0:width], V.q2[:, 0:width], op=ALU.mult))
        op("vector", "dve", lambda e: e.tensor_tensor(
            outb_ap.rearrange("p (a b) -> p a b", b=BQ),
            V.q1[:, 0:width].rearrange("p (a b) -> p a b", b=BQ),
            V.qs[:, 0:nb].unsqueeze(2).broadcast_to([128, nb, BQ]), op=ALU.mult))

    for (src_d, dst_d, ntile, wid) in ((DR["wg"], DR["wgq"], ITF, D),
                                       (DR["wu"], DR["wuq"], ITF, D),
                                       (DR["wd"], DR["wdq"], DT, IF)):
        for ti in range(ntile):
            w("sync", "dve")
            dma("sync", V.q0[:, 0:wid], src_d[ti * 128:(ti + 1) * 128, 0:wid])
            w("vector", "ld")
            w("scalar", "ld")
            quant(V.q0[:, 0:wid], wid, V.qb[:, 0:wid])
            for wt in range(wid // 128):
                w("tensor", "dve")
                op("tensor", "pe", lambda e, wt=wt: e.transpose(
                    p_a[0:128, 0:64].bitcast(BF16), V.qb[:, wt * 128:(wt + 1) * 128], T["t_ident"][:, :]))
                w("vector", "pe")
                w("vector", "st")
                op("vector", "dve", lambda e: e.tensor_copy(V.qt[:, 0:128], p_a[0:128, 0:64].bitcast(BF16)))
                w("sync", "dve")
                dma_st(dst_d[wt][:, ti * 128:(ti + 1) * 128], V.qt[:, 0:128])

    checkpoint("ffnq")
    def emit_ln(src_tiles, dst_tiles, scaled=False, add_tiles=None):
        w("sync", "st")
        for scn in range(LC // 512):
            sl = slice(scn * 512, (scn + 1) * 512)
            for kt in range(DT):
                w("sync", "dve")
                dma("sync", V.u[:, kt * 512:(kt + 1) * 512], src_tiles[kt][:, sl])
            LDU = n_("ld")
            w("scalar", "ld", LDU)
            w("tensor", "ld", LDU)
            w("tensor", "dve")
            for kt in range(DT):
                w("scalar", "pe")
                op("scalar", "act", lambda e, kt=kt: e.activation(
                    V.v1, V.u[:, kt * 512:(kt + 1) * 512], ACTF.Square))
                w("tensor", "act")
                op("tensor", "pe", lambda e, kt=kt: e.matmul(
                    p_b[0:1, 0:512], T["t_ones"][:, 0:1], V.u[:, kt * 512:(kt + 1) * 512],
                    start=(kt == 0), stop=(kt == DT - 1)))
                op("tensor", "pe", lambda e, kt=kt: e.matmul(
                    p_b[0:1, 512:1024], T["t_ones"][:, 0:1], V.v1,
                    start=(kt == 0), stop=(kt == DT - 1)))
            rw = V.row
            w("vector", "pe")
            op("vector", "dve", lambda e: e.tensor_scalar_mul(rw[0:1, 0:512], p_b[0:1, 0:512], 1.0 / D))
            op("vector", "dve", lambda e: e.tensor_scalar_mul(rw[0:1, 512:1024], p_b[0:1, 512:1024], 1.0 / D))
            op("vector", "dve", lambda e: e.tensor_tensor(
                V.v2[0:1, 0:512], rw[0:1, 0:512], rw[0:1, 0:512], op=ALU.mult))
            op("vector", "dve", lambda e: e.tensor_sub(
                rw[0:1, 512:1024], rw[0:1, 512:1024], V.v2[0:1, 0:512]))
            op("vector", "dve", lambda e: e.tensor_scalar_add(rw[0:1, 512:1024], rw[0:1, 512:1024], EPS))
            op("vector", "dve", lambda e: e.reciprocal(V.v2[0:1, 0:512], rw[0:1, 512:1024]))
            w("scalar", "dve")
            op("scalar", "act", lambda e: e.activation(V.v1[0:1, 0:512], V.v2[0:1, 0:512], ACTF.Sqrt))
            w("vector", "act")
            op("vector", "dve", lambda e: e.tensor_tensor(
                V.v3[0:1, 0:512], V.v1[0:1, 0:512], V.v1[0:1, 0:512], op=ALU.mult))
            op("vector", "dve", lambda e: e.tensor_tensor(
                V.v3[0:1, 0:512], V.v3[0:1, 0:512], rw[0:1, 512:1024], op=ALU.mult))
            op("vector", "dve", lambda e: e.tensor_scalar(
                V.v3[0:1, 0:512], V.v3[0:1, 0:512], -0.5, 1.5, op0=ALU.mult, op1=ALU.add))
            op("vector", "dve", lambda e: e.tensor_tensor(
                rw[0:1, 512:1024], V.v1[0:1, 0:512], V.v3[0:1, 0:512], op=ALU.mult))
            w("tensor", "dve")
            w("tensor", "act")
            op("tensor", "pe", lambda e: e.matmul(
                p_b[0:128, 1024:1536], T["t_ones1"][0:1, 0:128], rw[0:1, 0:512], start=True, stop=True))
            op("tensor", "pe", lambda e: e.matmul(
                p_b[0:128, 1536:2048], T["t_ones1"][0:1, 0:128], rw[0:1, 512:1024], start=True, stop=True))
            w("vector", "pe")
            for kt in range(DT):
                w("vector", "st")
                op("vector", "dve", lambda e, kt=kt: e.tensor_sub(
                    V.v2, V.u[:, kt * 512:(kt + 1) * 512], p_b[0:128, 1024:1536]))
                op("vector", "dve", lambda e, kt=kt: e.tensor_tensor(
                    V.v3, V.v2, p_b[0:128, 1536:2048], op=ALU.mult))
                if scaled:
                    op("vector", "dve", lambda e, kt=kt: e.tensor_scalar(
                        V.v3, V.v3, T["t_lnw"][:, kt:kt + 1], T["t_lnb"][:, kt:kt + 1],
                        op0=ALU.mult, op1=ALU.add))
                if add_tiles is not None:
                    w("sync", "dve")
                    dma("sync", V.xc1, add_tiles[kt][:, sl])
                    w("vector", "ld")
                    op("vector", "dve", lambda e: e.tensor_add(V.v3, V.v3, V.xc1))
                w("sync", "dve")
                dma_st(dst_tiles[kt][:, sl], V.v3)

    u_t = [DR["u"][kt] for kt in range(DT)]
    x_t = [DR["x_cm"][kt * 128:(kt + 1) * 128, :] for kt in range(DT)]
    x2_t = [DR["x2cm"][kt] for kt in range(DT)]
    emit_ln(x_t, u_t)

    checkpoint("l1")
    for k in range(DEPTH):
        w("gpsimd", "st")
        for kt in range(DT):
            dma("gpsimd", DR["aghs"][k][kt * 128:(kt + 1) * 128, :], DR["u"][kt][:, LC - HW:LC])
        w("gpsimd", "gd")
        op("gpsimd", "ag", lambda e, k=k: e.collective_compute(
            "AllGather", ALU.bypass, ins=[DR["aghs"][k][:]], outs=[DR["aghd"][k][:]],
            replica_groups=RG8))
        w("gpsimd", "ag")
        w("gpsimd", "dve")
        for r_ in range(8):
            dma("gpsimd",
                V.halg.rearrange("p (a r h) -> p a r h", r=8, h=HW)[:, :, r_, :],
                DR["aghd"][k][r_].rearrange("(a p) h -> p a h", p=128))
        w("vector", "gd")
        checkpoint("haloA%d" % k)
        op("vector", "dve", lambda e: e.tensor_tensor(
            V.halt.rearrange("p (a h r) -> p a h r", h=HW, r=8),
            V.halg.rearrange("p (a r h) -> p a r h", r=8, h=HW).transpose([0, 1, 3, 2]),
            T["t_halom"][:, :].rearrange("p (h r) -> p h r", r=8).unsqueeze(1).broadcast_to([128, DT, HW, 8]),
            op=ALU.mult))
        op("vector", "dve", lambda e: e.tensor_reduce(
            T["t_hal"][:, :].rearrange("p (a h) -> p a h", h=HW),
            V.halt.rearrange("p (a h r) -> p a h r", h=HW, r=8),
            axis=AX.X, op=ALU.add))
        HAL_DVE = n_("dve")
        checkpoint("haloB%d" % k)
        w("scalar", "dve", HAL_DVE)
        w("tensor", "dve", HAL_DVE)
        for kt in range(DT):
            op("scalar", "act", lambda e, kt=kt: e.activation(
                T["t_sc1"][:, 0:HW], T["t_hal"][:, kt * HW:(kt + 1) * HW], ACTF.Square))
            w("tensor", "act")
            op("tensor", "pe", lambda e, kt=kt: e.matmul(
                p_b[0:1, 0:HW], T["t_ones"][:, 0:1], T["t_sc1"][:, 0:HW],
                start=(kt == 0), stop=(kt == DT - 1)))
        w("vector", "pe")
        op("vector", "dve", lambda e: e.tensor_scalar(
            V.row[0:1, 0:HW], p_b[0:1, 0:HW], 1.0 / D, EPS, op0=ALU.mult, op1=ALU.add))
        op("vector", "dve", lambda e: e.reciprocal(V.row[0:1, HW:2 * HW], V.row[0:1, 0:HW]))
        w("scalar", "dve")
        op("scalar", "act", lambda e: e.activation(V.xc1[0:1, 0:HW], V.row[0:1, HW:2 * HW], ACTF.Sqrt))
        w("vector", "act")
        op("vector", "dve", lambda e: e.tensor_tensor(
            V.xc2[0:1, 0:HW], V.xc1[0:1, 0:HW], V.xc1[0:1, 0:HW], op=ALU.mult))
        op("vector", "dve", lambda e: e.tensor_tensor(
            V.xc2[0:1, 0:HW], V.xc2[0:1, 0:HW], V.row[0:1, 0:HW], op=ALU.mult))
        op("vector", "dve", lambda e: e.tensor_scalar(
            V.xc2[0:1, 0:HW], V.xc2[0:1, 0:HW], -0.5, 1.5, op0=ALU.mult, op1=ALU.add))
        op("vector", "dve", lambda e: e.tensor_tensor(
            V.row[0:1, 0:HW], V.xc1[0:1, 0:HW], V.xc2[0:1, 0:HW], op=ALU.mult))
        w("tensor", "dve")
        w("tensor", "act")
        op("tensor", "pe", lambda e: e.matmul(
            p_b[0:128, 512:512 + HW], T["t_ones1"][0:1, 0:128], V.row[0:1, 0:HW], start=True, stop=True))
        w("vector", "pe")
        for kt in range(DT):
            op("vector", "dve", lambda e, kt=kt: e.tensor_tensor(
                T["t_halnb"][:, kt * HW:(kt + 1) * HW], T["t_hal"][:, kt * HW:(kt + 1) * HW],
                p_b[0:128, 512:512 + HW], op=ALU.mult))
        HALN_DVE = n_("dve")
        checkpoint("haloC%d" % k)
        w("sync", "pe")
        for kt in range(DT):
            dma("sync", V.win[:, kt * 4096:(kt + 1) * 4096], DR["wib"][kt])
        LD_WIN = n_("ld")
        w("tensor", "dve", HALN_DVE)
        w("tensor", "ld", LD_WIN)
        for et in range(ET):
            pst = (et % 2) * 512
            if et >= 2:
                w("tensor", "dve")
            for kt in range(DT):
                op("tensor", "pe", lambda e, et=et, kt=kt, pst=pst: e.matmul(
                    p_a[0:128, pst:pst + HW],
                    V.win[:, kt * 4096 + et * 128: kt * 4096 + (et + 1) * 128],
                    T["t_halnb"][:, kt * HW:(kt + 1) * HW],
                    start=(kt == 0), stop=(kt == DT - 1)))
            w("vector", "pe")
            op("vector", "dve", lambda e, et=et, pst=pst: e.tensor_copy(
                V.halxs[:, et * HW:(et + 1) * HW], p_a[0:128, pst:pst + HW]))
        HALXS_DVE = n_("dve")
        checkpoint("halo%d" % k)

        w("sync", "st")
        for scn in range(LC // 512):
            sl = slice(scn * 512, (scn + 1) * 512)
            for kt in range(DT):
                w("sync", "dve")
                dma("sync", V.u[:, kt * 512:(kt + 1) * 512], DR["u"][kt][:, sl])
            LDU = n_("ld")
            w("scalar", "ld", LDU)
            w("tensor", "ld", LDU)
            w("tensor", "dve")
            for kt in range(DT):
                op("scalar", "act", lambda e, kt=kt: e.activation(
                    V.v1, V.u[:, kt * 512:(kt + 1) * 512], ACTF.Square))
                w("tensor", "act")
                op("tensor", "pe", lambda e, kt=kt: e.matmul(
                    p_b[0:1, 0:512], T["t_ones"][:, 0:1], V.v1,
                    start=(kt == 0), stop=(kt == DT - 1)))
            w("vector", "pe")
            op("vector", "dve", lambda e: e.tensor_scalar(
                V.row[0:1, 0:512], p_b[0:1, 0:512], 1.0 / D, EPS, op0=ALU.mult, op1=ALU.add))
            op("vector", "dve", lambda e: e.reciprocal(V.row[0:1, 512:1024], V.row[0:1, 0:512]))
            w("scalar", "dve")
            op("scalar", "act", lambda e: e.activation(V.xc1[0:1, 0:512], V.row[0:1, 512:1024], ACTF.Sqrt))
            w("vector", "act")
            op("vector", "dve", lambda e: e.tensor_tensor(
                V.xc2[0:1, 0:512], V.xc1[0:1, 0:512], V.xc1[0:1, 0:512], op=ALU.mult))
            op("vector", "dve", lambda e: e.tensor_tensor(
                V.xc2[0:1, 0:512], V.xc2[0:1, 0:512], V.row[0:1, 0:512], op=ALU.mult))
            op("vector", "dve", lambda e: e.tensor_scalar(
                V.xc2[0:1, 0:512], V.xc2[0:1, 0:512], -0.5, 1.5, op0=ALU.mult, op1=ALU.add))
            op("vector", "dve", lambda e: e.tensor_tensor(
                V.row[0:1, 0:512], V.xc1[0:1, 0:512], V.xc2[0:1, 0:512], op=ALU.mult))
            w("tensor", "dve")
            op("tensor", "pe", lambda e: e.matmul(
                p_b[0:128, 512:1024], T["t_ones1"][0:1, 0:128], V.row[0:1, 0:512], start=True, stop=True))
            w("vector", "pe")
            for kt in range(DT):
                op("vector", "dve", lambda e, kt=kt: e.tensor_tensor(
                    V.un[:, kt * 512:(kt + 1) * 512], V.u[:, kt * 512:(kt + 1) * 512],
                    p_b[0:128, 512:1024], op=ALU.mult))
            UN_DVE = n_("dve")
            w("tensor", "ld", LD_WIN)
            w("tensor", "dve", UN_DVE)
            for jt in range(2 * ET):
                pslot = jt % 2
                pap = p_a[0:128, pslot * 512:(pslot + 1) * 512]
                if jt >= 2:
                    if pslot == 0:
                        w("tensor", "dve")
                    else:
                        w("tensor", "act")
                for kt in range(DT):
                    op("tensor", "pe", lambda e, jt=jt, kt=kt, pap=pap: e.matmul(
                        pap, V.win[:, kt * 4096 + jt * 128: kt * 4096 + (jt + 1) * 128],
                        V.un[:, kt * 512:(kt + 1) * 512],
                        start=(kt == 0), stop=(kt == DT - 1)))
                if jt < ET:
                    et = jt
                    w("vector", "pe")
                    op("vector", "dve", lambda e, et=et, pap=pap: e.tensor_copy(
                        V.xs[:, et * 520 + KC - 1: et * 520 + KC - 1 + 512], pap))
                else:
                    et = jt - ET
                    w("scalar", "pe")
                    w("scalar", "st")
                    op("scalar", "act", lambda e, pap=pap: e.activation(V.gst, pap, ACTF.Silu))
                    w("sync", "act")
                    dma_st(DR["g"][et][:, sl], V.gst)
            for et in range(ET):
                base = et * 520
                if scn == 0:
                    op("vector", "dve", lambda e, et=et, base=base: e.tensor_copy(
                        V.xs[:, base:base + KC - 1],
                        V.halxs[:, et * HW + HW - (KC - 1): (et + 1) * HW]))
                op("vector", "dve", lambda e, et=et, base=base: e.tensor_scalar_mul(
                    V.xc1, V.xs[:, base:base + 512], T["t_cw"][:, et * KC:et * KC + 1]))
                for kk in range(1, KC):
                    op("vector", "dve", lambda e, et=et, base=base, kk=kk: e.scalar_tensor_tensor(
                        V.xc1, V.xs[:, base + kk:base + kk + 512],
                        T["t_cw"][:, et * KC + kk:et * KC + kk + 1], V.xc1,
                        op0=ALU.mult, op1=ALU.add))
                w("scalar", "dve")
                w("scalar", "st")
                w("scalar", "pe")
                op("scalar", "act", lambda e, et=et: e.activation(
                    V.xc2, V.xc1, ACTF.Silu, bias=T["t_cb"][:, et:et + 1]))
                w("sync", "act")
                dma_st(DR["xc"][et][:, sl], V.xc2)
                w("tensor", "act")
                op("tensor", "pe", lambda e, et=et: e.matmul(
                    p_b[0:96, 1024:1536], T["t_wx"][:, et * 96:(et + 1) * 96], V.xc2,
                    start=(et == 0), stop=(et == ET - 1)))
                op("vector", "dve", lambda e, et=et, base=base: e.tensor_copy(
                    T["t_tail"][:, et * 4:et * 4 + KC - 1],
                    V.xs[:, base + 512: base + 512 + KC - 1]))
            w("vector", "pe")
            op("vector", "dve", lambda e, sl=sl: e.tensor_copy(T["t_dbc"][0:96, sl], p_b[0:96, 1024:1536]))
            if scn < LC // 512 - 1:
                for et in range(ET):
                    op("vector", "dve", lambda e, et=et: e.tensor_copy(
                        V.xs[:, et * 520:et * 520 + KC - 1],
                        T["t_tail"][:, et * 4:et * 4 + KC - 1]))
        op("vector", "dve", lambda e: e.tensor_copy(V.dbcb[0:96, 0:2048], T["t_dbc"][0:96, :]))
        DBC_DVE = n_("dve")
        XCG_ST = n_("st")
        checkpoint("pass1_%d" % k)

        for s in range(SR):
            t = s % ET
            w("sync", "st", XCG_ST)
            w("sync", "dve")
            dma("sync", V.xct, DR["xc"][t])
            dma("sync", V.gt, DR["g"][t])
            LD_SLOT = n_("ld")
            w("tensor", "dve")
            w("tensor", "act")
            for lb in range(4):
                op("tensor", "pe", lambda e, t=t, lb=lb: e.matmul(
                    p_a[0:128, lb * 512:(lb + 1) * 512],
                    T["t_wdt"][:, t * 128:(t + 1) * 128],
                    T["t_dbc"][0:64, lb * 512:(lb + 1) * 512], start=True, stop=True))
            PE_DEL = n_("pe")
            w("scalar", "pe", PE_DEL)
            for lb in range(4):
                sl2 = slice(lb * 512, (lb + 1) * 512)
                op("scalar", "act", lambda e, t=t, sl2=sl2: e.activation(
                    V.m[:, sl2], p_a[0:128, sl2], ACTF.Relu, bias=T["t_bdt"][:, t:t + 1]))
                op("scalar", "act", lambda e, t=t, sl2=sl2: e.activation(
                    V.tmp2[:, sl2], p_a[0:128, sl2], ACTF.Abs, bias=T["t_bdt"][:, t:t + 1]))
                op("scalar", "act", lambda e, sl2=sl2: e.activation(
                    V.tmp2[:, sl2], V.tmp2[:, sl2], ACTF.Exp, scale=-1.0))
                op("scalar", "act", lambda e, sl2=sl2: e.activation(
                    V.tmp2[:, sl2], V.tmp2[:, sl2], ACTF.Ln, bias=1.0))
            ACT_SP = n_("act")
            w("vector", "act", ACT_SP)
            op("vector", "dve", lambda e: e.tensor_add(V.dl, V.m, V.tmp2))
            w("vector", "ld", LD_SLOT)
            op("vector", "dve", lambda e: e.tensor_tensor(
                V.w, V.dl, V.xct, op=ALU.mult))
            DVE_W = n_("dve")
            SCAN_DVE = {}
            GPS_ADD = {}
            for n2 in range(N):
                dab = V.da if n2 % 2 == 0 else V.tmp2
                if n2 == 0:
                    w("tensor", "act", ACT_SP)
                    w("tensor", "dve", DVE_W)
                else:
                    w("tensor", "dve")
                for lb in range(4):
                    op("tensor", "pe", lambda e, n2=n2, lb=lb: e.matmul(
                        p_a[0:128, lb * 512:(lb + 1) * 512],
                        T["t_selbc"][0:96, n2 * 128:(n2 + 1) * 128],
                        V.dbcb[0:96, lb * 512:(lb + 1) * 512], start=True, stop=True))
                PE_B = n_("pe")
                for lb in range(4):
                    op("tensor", "pe", lambda e, n2=n2, lb=lb: e.matmul(
                        p_b[0:128, lb * 512:(lb + 1) * 512],
                        T["t_selbc"][0:96, (N + n2) * 128:(N + n2 + 1) * 128],
                        V.dbcb[0:96, lb * 512:(lb + 1) * 512], start=True, stop=True))
                PE_C = n_("pe")
                w("scalar", "dve", SCAN_DVE[n2 - 2] if n2 >= 2 else DVE_W)
                op("scalar", "act", lambda e, t=t, n2=n2, dab=dab: e.activation(
                    dab, V.dl, ACTF.Exp,
                    scale=T["t_alog"][:, t * N + n2:t * N + n2 + 1]))
                ACT_DA = n_("act")
                w("vector", "pe", PE_B)
                op("vector", "dve", lambda e: e.tensor_tensor(
                    V.dbx, V.w, p_a[0:128, 0:2048], op=ALU.mult))
                if n2 == 0:
                    w("vector", "gd")
                    op("vector", "dve", lambda e: e.tensor_tensor(
                        V.v1[:, 0:128].rearrange("p (n r) -> p n r", r=8),
                        V.v1[:, 128:256].rearrange("p (r n) -> p n r", n=N),
                        T["t_selmask"][:, :].rearrange("p (n r) -> p n r", r=8), op=ALU.mult))
                    op("vector", "dve", lambda e: e.tensor_reduce(
                        T["t_carry"][:, :], V.v1[:, 0:128].rearrange("p (n r) -> p n r", r=8),
                        axis=AX.X, op=ALU.add))
                w("vector", "act", ACT_DA)
                op("vector", "dve", lambda e, n2=n2, dab=dab: e.tensor_tensor_scan(
                    V.h, dab, V.dbx, T["t_carry"][:, n2:n2 + 1], op0=ALU.mult, op1=ALU.add))
                SCAN_DVE[n2] = n_("dve")
                op("vector", "dve", lambda e, n2=n2: e.tensor_copy(
                    T["t_hend"][:, n2:n2 + 1], V.h[:, LC - 1:LC]))
                HEND_DVE = n_("dve")
                w("vector", "pe", PE_C)
                tb = V.tmp1 if n2 % 2 == 0 else V.m
                if n2 >= 2:
                    w("vector", "gps", GPS_ADD[n2 - 2])
                op("vector", "dve", lambda e, tb=tb: e.tensor_tensor(
                    tb, V.h, p_b[0:128, 0:2048], op=ALU.mult))
                YM_DVE = n_("dve")
                w("gpsimd", "dve", YM_DVE)
                if n2 == 0:
                    op("gpsimd", "gps", lambda e, tb=tb: e.tensor_copy(V.yacc, tb))
                else:
                    op("gpsimd", "gps", lambda e, tb=tb: e.tensor_add(V.yacc, V.yacc, tb))
                GPS_ADD[n2] = n_("gps")
            w("vector", "gps", GPS_ADD[N - 1])
            op("vector", "dve", lambda e, t=t: e.scalar_tensor_tensor(
                V.yacc, V.xct, T["t_dp"][:, t:t + 1],
                V.yacc, op0=ALU.mult, op1=ALU.add))
            if s < SR - 1:
                w("gpsimd", "dve", HEND_DVE)
                dma("gpsimd", DR["agrs"][k][s][:], T["t_hend"][:, :])
                w("gpsimd", "gd")
                op("gpsimd", "ag", lambda e, k=k, s=s: e.collective_compute(
                    "AllGather", ALU.bypass, ins=[DR["agrs"][k][s][:]], outs=[DR["agrd"][k][s][:]],
                    replica_groups=RG8))
                w("gpsimd", "ag")
                w("gpsimd", "dve")
                for r_ in range(8):
                    dma("gpsimd", V.v1[:, 128 + r_ * N:128 + (r_ + 1) * N], DR["agrd"][k][s][r_])
            op("vector", "dve", lambda e, t=t: e.tensor_tensor(
                V.ygall[:, t * LC:(t + 1) * LC], V.yacc, V.gt, op=ALU.mult))
            YG_DVE = n_("dve")
            checkpoint("round%d_%d" % (k, s))

        W_YG = n_("dve")
        w("sync", "dve", W_YG)
        w("sync", "st")
        for t2 in range(ET):
            dma("sync", V.wo[:, t2 * D:(t2 + 1) * D], DR["wob"][t2])
        WO_LD = n_("ld")
        for lb in range(4):
            sl2 = slice(lb * 512, (lb + 1) * 512)
            for dtile in range(DT):
                pb = p_a if dtile < 4 else p_b
                po = (dtile % 4) * 512
                w("tensor", "ld", WO_LD)
                if lb > 0 and dtile == 0:
                    w("tensor", "dve")
                for t2 in range(ET):
                    op("tensor", "pe", lambda e, t2=t2, dtile=dtile, lb=lb, pb=pb, po=po: e.matmul(
                        pb[0:128, po:po + 512],
                        V.wo[:, t2 * D + dtile * 128: t2 * D + (dtile + 1) * 128],
                        V.ygall[:, t2 * LC + lb * 512: t2 * LC + (lb + 1) * 512],
                        start=(t2 == 0), stop=(t2 == ET - 1)))
            for dtile in range(DT):
                pb = p_a if dtile < 4 else p_b
                po = (dtile % 4) * 512
                w("sync", "dve")
                dma("sync", V.uw, DR["u"][dtile][:, sl2])
                w("vector", "ld")
                w("vector", "pe")
                w("vector", "st")
                op("vector", "dve", lambda e, pb=pb, po=po: e.scalar_tensor_tensor(
                    V.uo, V.uw, 2.0, pb[0:128, po:po + 512], op0=ALU.mult, op1=ALU.add))
                w("sync", "dve")
                dma_st(DR["u"][dtile][:, sl2], V.uo)
        checkpoint("iter%d" % k)

    emit_ln(u_t, x2_t, scaled=True, add_tiles=x_t)
    emit_ln(x2_t, u_t)
    checkpoint("epi")

    w("sync", "st")
    for lt in range(LT):
        for kt in range(DT):
            w("sync", "dve")
            dma("sync", V.v1[:, 0:128], DR["u"][kt][:, lt * 128:(lt + 1) * 128])
            w("tensor", "ld")
            w("tensor", "dve")
            op("tensor", "pe", lambda e: e.transpose(
                p_a[0:128, 0:128], V.v1[:, 0:128], T["t_identf"][:, :]))
            w("vector", "pe")
            op("vector", "dve", lambda e, kt=kt: e.tensor_copy(
                V.q0[:, kt * 128:(kt + 1) * 128], p_a[0:128, 0:128]))
        quant(V.q0[:, 0:D], D, V.qb[:, 0:D])
        for kt in range(DT):
            w("tensor", "dve")
            op("tensor", "pe", lambda e, kt=kt: e.transpose(
                p_a[0:128, 128:192].bitcast(BF16), V.qb[:, kt * 128:(kt + 1) * 128], T["t_ident"][:, :]))
            w("vector", "pe")
            w("vector", "st")
            op("vector", "dve", lambda e: e.tensor_copy(V.qt[:, 0:128], p_a[0:128, 128:192].bitcast(BF16)))
            w("sync", "dve")
            dma_st(DR["xqt"][kt][:, lt * 128:(lt + 1) * 128], V.qt[:, 0:128])

    checkpoint("qx")
    w("sync", "st")
    nib = (IF + 511) // 512
    for lt in range(LT):
        w("sync", "dve")
        for kt in range(DT):
            dma("sync", V.xq[:, kt * 128:(kt + 1) * 128], DR["xqt"][kt][:, lt * 128:(lt + 1) * 128])
        for which, wsrc in ((0, DR["wgq"]), (1, DR["wuq"])):
            for ib in range(nib):
                iw = min(512, IF - ib * 512)
                w("sync", "pe")
                for kt in range(DT):
                    dma("sync", V.wfs[:, kt * 512:kt * 512 + iw], wsrc[kt][:, ib * 512:ib * 512 + iw])
                w("tensor", "ld")
                if which == 0:
                    w("tensor", "act")
                else:
                    w("tensor", "dve")
                for kt in range(DT):
                    op("tensor", "pe", lambda e, kt=kt, iw=iw: e.matmul(
                        p_a[0:128, 0:iw],
                        V.xq[:, kt * 128:(kt + 1) * 128],
                        V.wfs[:, kt * 512:kt * 512 + iw],
                        start=(kt == 0), stop=(kt == DT - 1)))
                if which == 0:
                    w("scalar", "pe")
                    op("scalar", "act", lambda e, ib=ib, iw=iw: e.activation(
                        V.q2[:, ib * 512:ib * 512 + iw], p_a[0:128, 0:iw], ACTF.Silu))
                else:
                    w("vector", "pe")
                    w("vector", "act")
                    op("vector", "dve", lambda e, ib=ib, iw=iw: e.tensor_tensor(
                        V.h2[:, ib * 512:ib * 512 + iw], V.q2[:, ib * 512:ib * 512 + iw],
                        p_a[0:128, 0:iw], op=ALU.mult))
        quant(V.h2[:, 0:IF], IF, V.qb[:, 0:IF])
        for it in range(ITF):
            w("tensor", "dve")
            op("tensor", "pe", lambda e, it=it: e.transpose(
                p_a[0:128, 0:64].bitcast(BF16), V.qb[:, it * 128:(it + 1) * 128], T["t_ident"][:, :]))
            w("vector", "pe")
            op("vector", "dve", lambda e, it=it: e.tensor_copy(
                V.hqt[:, it * 128:(it + 1) * 128], p_a[0:128, 0:64].bitcast(BF16)))
        HQ_DVE = n_("dve")
        for db in range(2):
            w("sync", "pe")
            w("vector", "st")
            for it in range(ITF):
                dma("sync", V.wfs[:, it * 512:(it + 1) * 512], DR["wdq"][it][:, db * 512:(db + 1) * 512])
            w("tensor", "ld")
            w("tensor", "dve")
            for it in range(ITF):
                op("tensor", "pe", lambda e, it=it: e.matmul(
                    p_b[0:128, 0:512], V.hqt[:, it * 128:(it + 1) * 128],
                    V.wfs[:, it * 512:(it + 1) * 512],
                    start=(it == 0), stop=(it == ITF - 1)))
            for kt2 in range(4):
                kt = db * 4 + kt2
                w("sync", "pe")
                dma("sync", V.fv1[:, 0:128], DR["x2cm"][kt][:, lt * 128:(lt + 1) * 128])
                w("tensor", "ld")
                op("tensor", "pe", lambda e, kt2=kt2: e.transpose(
                    p_b[0:128, 1024 + kt2 * 128:1024 + (kt2 + 1) * 128],
                    V.fv1[:, 0:128], T["t_identf"][:, :]))
            w("vector", "pe")
            op("vector", "dve", lambda e: e.tensor_copy(V.q0[:, 0:512], p_b[0:128, 1024:1536]))
            op("vector", "dve", lambda e, db=db: e.tensor_add(
                V.o1[:, db * 512:(db + 1) * 512], p_b[0:128, 0:512], V.q0[:, 0:512]))
        w("sync", "dve")
        dma_st(DR["out"][lt * 128:(lt + 1) * 128, :], V.o1[:, 0:D])
    w("sync", "st")
    return cnt


def build_nc():
    nc = bass.Bass("TRN2", target_bir_lowering=False)
    DR = {}
    DR["x_cm"] = nc.dram_tensor("x_cm", [D, LC], F32, kind="ExternalInput").ap()
    DR["win"] = nc.dram_tensor("Win", [D, 2 * ED], F32, kind="ExternalInput").ap()
    DR["rmsw"] = nc.dram_tensor("rmsw", [128, DT], F32, kind="ExternalInput").ap()
    DR["convw"] = nc.dram_tensor("convw", [ED, KC], F32, kind="ExternalInput").ap()
    DR["convb"] = nc.dram_tensor("convb", [128, ET], F32, kind="ExternalInput").ap()
    DR["wx"] = nc.dram_tensor("Wx", [ED, 96], F32, kind="ExternalInput").ap()
    DR["wdt"] = nc.dram_tensor("Wdt", [R, ED], F32, kind="ExternalInput").ap()
    DR["bdt"] = nc.dram_tensor("bdt", [128, ET], F32, kind="ExternalInput").ap()
    DR["alog"] = nc.dram_tensor("Alog", [ED, N], F32, kind="ExternalInput").ap()
    DR["dp"] = nc.dram_tensor("Dp", [128, ET], F32, kind="ExternalInput").ap()
    DR["wout"] = nc.dram_tensor("Wout", [ED, D], F32, kind="ExternalInput").ap()
    DR["lnw"] = nc.dram_tensor("lnw", [128, DT], F32, kind="ExternalInput").ap()
    DR["lnb"] = nc.dram_tensor("lnb", [128, DT], F32, kind="ExternalInput").ap()
    DR["wg"] = nc.dram_tensor("wg", [IF, D], F32, kind="ExternalInput").ap()
    DR["wu"] = nc.dram_tensor("wu", [IF, D], F32, kind="ExternalInput").ap()
    DR["wd"] = nc.dram_tensor("wd", [D, IF], F32, kind="ExternalInput").ap()
    DR["selmask"] = nc.dram_tensor("selmask", [128, N * 8], F32, kind="ExternalInput").ap()
    DR["halom"] = nc.dram_tensor("halom", [128, HW * 8], F32, kind="ExternalInput").ap()
    DR["selbc"] = nc.dram_tensor("selbc", [32, 96, 128], BF16, kind="ExternalInput").ap()
    DR["ident"] = nc.dram_tensor("ident", [128, 128], BF16, kind="ExternalInput").ap()
    DR["identf"] = nc.dram_tensor("identf", [128, 128], F32, kind="ExternalInput").ap()
    DR["out"] = nc.dram_tensor("out", [LC, D], F32, kind="ExternalOutput").ap()

    DR["u"] = [nc.dram_tensor(f"u_scr{i}", [128, LC], F32, kind="Internal").ap() for i in range(DT)]
    DR["x2cm"] = [nc.dram_tensor(f"x2_scr{i}", [128, LC], F32, kind="Internal").ap() for i in range(DT)]
    DR["xc"] = [nc.dram_tensor(f"xc_scr{i}", [128, LC], F32, kind="Internal").ap() for i in range(ET)]
    DR["g"] = [nc.dram_tensor(f"g_scr{i}", [128, LC], F32, kind="Internal").ap() for i in range(ET)]
    DR["wib"] = [nc.dram_tensor(f"wib{i}", [128, 2 * ED], BF16, kind="Internal").ap() for i in range(DT)]
    DR["wob"] = [nc.dram_tensor(f"wob{i}", [128, D], BF16, kind="Internal").ap() for i in range(ET)]
    DR["wobl"] = [nc.dram_tensor(f"wobl{i}", [128, D], BF16, kind="Internal").ap() for i in range(ET)]
    DR["ylo"] = [nc.dram_tensor(f"ylo{i}", [128, LC], BF16, kind="Internal").ap() for i in range(ET)]
    DR["wgq"] = [nc.dram_tensor(f"wgq{i}", [128, IF], BF16, kind="Internal").ap() for i in range(DT)]
    DR["wuq"] = [nc.dram_tensor(f"wuq{i}", [128, IF], BF16, kind="Internal").ap() for i in range(DT)]
    DR["wdq"] = [nc.dram_tensor(f"wdq{i}", [128, D], BF16, kind="Internal").ap() for i in range(ITF)]
    DR["xqt"] = [nc.dram_tensor(f"xqt{i}", [128, LC], BF16, kind="Internal").ap() for i in range(DT)]
    DR["aghs"] = [nc.dram_tensor(f"aghs{k}", [D, HW], F32, kind="Internal").ap() for k in range(DEPTH)]
    DR["aghd"] = [nc.dram_tensor(f"aghd{k}", [8, D, HW], F32, kind="Internal").ap() for k in range(DEPTH)]
    DR["agrs"] = [[nc.dram_tensor(f"agrs{k}_{s}", [128, N], F32, kind="Internal").ap()
                   for s in range(SR)] for k in range(DEPTH)]
    DR["agrd"] = [[nc.dram_tensor(f"agrd{k}_{s}", [8, 128, N], F32, kind="Internal").ap()
                   for s in range(SR)] for k in range(DEPTH)]

    specs = [
        ("t_af", [128, 24576], F32), ("t_ab", [128, 32768], BF16),
        ("t_rmsw", [128, DT], F32), ("t_lnw", [128, DT], F32), ("t_lnb", [128, DT], F32),
        ("t_cw", [128, ET * KC], F32), ("t_cb", [128, ET], F32), ("t_bdt", [128, ET], F32),
        ("t_dp", [128, ET], F32), ("t_alog", [128, ET * N], F32),
        ("t_wx", [128, ET * 96], F32), ("t_wdt", [64, ET * 128], F32),
        ("t_selbc", [96, 32 * 128], BF16), ("t_ident", [128, 128], BF16),
        ("t_identf", [128, 128], F32),
        ("t_selmask", [128, N * 8], F32), ("t_halom", [128, HW * 8], F32),
        ("t_ones", [128, 1], F32), ("t_ones1", [1, 128], F32),
        ("t_sc1", [128, 256], F32),
        ("t_dbc", [96, LC], F32),
                ("t_hal", [128, DT * HW], F32), ("t_halnb", [128, DT * HW], BF16),
        ("t_tail", [128, ET * 4], F32),
        ("t_hend", [128, N], F32), ("t_carry", [128, N], F32),
    ]
    ctxs = []
    T = {}
    for (nm, sh, dt_) in specs:
        c = nc.sbuf_tensor(nm, sh, dt_)
        ctxs.append((nm, c))
    for (nm, sh) in [("p_a", [128, 2048]), ("p_b", [128, 2048])]:
        c = nc.psum_tensor(nm, sh, F32)
        ctxs.append((nm, c))
    semnames = ["ld", "st", "gd", "pe", "act", "dve", "gps", "ag"]
    for s_ in semnames:
        ctxs.append(("sem_" + s_, nc.semaphore("sem_" + s_)))
    entered = {}
    for nm, c in ctxs:
        entered[nm] = c.__enter__()
    T = {nm: entered[nm] for nm in entered if not nm.startswith("sem_")}
    S = {s_: entered["sem_" + s_] for s_ in semnames}
    T = {nm: (h.ap() if hasattr(h, "ap") else h) for nm, h in T.items()}

    counts = {}
    with nc.Block() as block:
        @block.sync
        def _(e):
            counts["sync"] = emit_program("sync", e, T, S, DR)

        @block.tensor
        def _(e):
            counts["tensor"] = emit_program("tensor", e, T, S, DR)

        @block.scalar
        def _(e):
            counts["scalar"] = emit_program("scalar", e, T, S, DR)

        @block.vector
        def _(e):
            counts["vector"] = emit_program("vector", e, T, S, DR)

        @block.gpsimd
        def _(e):
            counts["gpsimd"] = emit_program("gpsimd", e, T, S, DR)

    base = counts["sync"]
    for r_, c_ in counts.items():
        assert c_ == base, f"counter divergence in {r_}: {c_} vs {base}"
    for nm, c in reversed(ctxs):
        c.__exit__(None, None, None)
    return nc


def kernel(**inputs):
    bf = ml_dtypes.bfloat16
    x = np.asarray(inputs["L0_attn_x"], dtype=np.float32)[0]
    rmsw = np.asarray(inputs["L0_attn_rms_w"], np.float32)
    Win = np.asarray(inputs["L0_attn_W_in"], np.float32)
    convw = np.asarray(inputs["L0_attn_conv_w"], np.float32)
    convb = np.asarray(inputs["L0_attn_conv_b"], np.float32)
    Wx = np.asarray(inputs["L0_attn_W_x"], np.float32)
    Wdt = np.asarray(inputs["L0_attn_W_dt"], np.float32)
    bdt = np.asarray(inputs["L0_attn_b_dt"], np.float32)
    Alog = np.asarray(inputs["L0_attn_A_log"], np.float32)
    Dp = np.asarray(inputs["L0_attn_Dp"], np.float32)
    Wout = np.asarray(inputs["L0_attn_W_out"], np.float32)
    lnw = np.asarray(inputs["L0_attn_ln_w"], np.float32)
    lnb = np.asarray(inputs["L0_attn_ln_b"], np.float32)
    wg = np.asarray(inputs["L0_ffn_w_gate"], np.float32)
    wu = np.asarray(inputs["L0_ffn_w_up"], np.float32)
    wd = np.asarray(inputs["L0_ffn_w_down"], np.float32)

    ident = np.eye(128, dtype=bf)
    identf = np.eye(128, dtype=np.float32)
    selbc = np.zeros((32, 96, 128), dtype=bf)
    for n2 in range(N):
        selbc[n2, 64 + n2, :] = 1.0
        selbc[N + n2, 80 + n2, :] = 1.0

    in_maps = []
    for c in range(NCORE):
        perm = np.concatenate([np.arange(((t - c) % ET) * 128, ((t - c) % ET) * 128 + 128)
                               for t in range(ET)])
        selmask = np.zeros((128, N, 8), np.float32)
        halom = np.zeros((128, HW, 8), np.float32)
        if c > 0:
            selmask[:, :, c - 1] = 1.0
            halom[:, :, c - 1] = 1.0
        xs_rows = x[c * LC:(c + 1) * LC]
        m = {
            "x_cm": np.ascontiguousarray(xs_rows.T),
            "Win": np.ascontiguousarray(
                np.concatenate([Win[:, :ED][:, perm], Win[:, ED:][:, perm]], axis=1)),
            "rmsw": np.ascontiguousarray(rmsw.reshape(DT, 128).T),
            "convw": np.ascontiguousarray(convw[perm]),
            "convb": np.ascontiguousarray(convb[perm].reshape(ET, 128).T),
            "Wx": np.ascontiguousarray(Wx[perm]),
            "Wdt": np.ascontiguousarray(Wdt[:, perm]),
            "bdt": np.ascontiguousarray(bdt[perm].reshape(ET, 128).T),
            "Alog": np.ascontiguousarray(Alog[perm]),
            "Dp": np.ascontiguousarray(Dp[perm].reshape(ET, 128).T),
            "Wout": np.ascontiguousarray(Wout[perm]),
            "lnw": np.ascontiguousarray(lnw.reshape(DT, 128).T),
            "lnb": np.ascontiguousarray(lnb.reshape(DT, 128).T),
            "wg": wg, "wu": wu, "wd": wd,
            "selmask": selmask.reshape(128, N * 8),
            "halom": halom.reshape(128, HW * 8),
            "selbc": selbc, "ident": ident, "identf": identf,
        }
        in_maps.append(m)

    if "nc" not in _NC_CACHE:
        _NC_CACHE["nc"] = build_nc()
    nc = _NC_CACHE["nc"]
    res = run_bass_kernel_spmd(nc, in_maps, core_ids=list(range(NCORE)))
    out = np.concatenate([res.results[c]["out"] for c in range(NCORE)], axis=0)
    return out.reshape(1, L, D).astype(np.float32)


_NC_CACHE = {}
